# version 01
# baseline (speedup 1.0000x reference)
import math
import numpy as np
from contextlib import ExitStack

import concourse.bass as bass
import concourse.mybir as mybir
from concourse.bass_utils import run_bass_kernel_spmd

F32 = mybir.dt.float32
BF16 = mybir.dt.bfloat16
AF = mybir.ActivationFunctionType
ALU = mybir.AluOpType
AX = mybir.AxisListType

P = 128
D = 1024
SEQ = 4096
NT = 16
NPAIR = 8
NEG = -1.0e9
EPS = 1e-5

S_QKV = (3.0 * 1024 * 64 * 16) ** (-0.25)
SC_Q = S_QKV / 8.0
C_POS = math.sqrt(2.0) / 16.0
SC_WO = (1024.0 * 1024.0) ** (-0.25)
GELU_C2 = 0.044715
GELU_2C = 2.0 * math.sqrt(2.0 / math.pi)

STAGE = "full"
WAC = None
SIM = None
B_MAXQT = None
B_PRE = "kvqau"
C_LEVEL = 5
C_SUB = 'all'


def kch_of(t):
    return t // 4 + 1


def make_sched(ntb=NT):
    sched = []
    sc = gt = qc = 0
    for hp in range(2):
        for qt in range(ntb):
            n1 = kch_of(qt)
            chunks = []
            for side_i, side in enumerate(("own", "oth")):
                for j in range(n1):
                    base = j * 512 + side_i * 2048
                    masked = (j == n1 - 1)
                    chunks.append(dict(sc=sc, side=side, j=j, base=base,
                                       masked=masked, bank=sc % 2))
                    sc += 1
            groups = []
            for gi in range(2 * n1):
                groups.append(dict(gt=gt, gi=gi, base=chunks[gi]["base"],
                                   bank=gt % 2))
                gt += 1
            sched.append(dict(hp=hp, qt=qt, qc=qc, chunks=chunks, groups=groups,
                              obank=2 + qc % 2, nch=2 * n1))
            qc += 1
    return sched


SCHED = make_sched()


class FakeInstr:
    def then_inc(self, *a, **k):
        return self


class FakeEng:
    def __getattr__(self, name):
        def f(*a, **k):
            return FakeInstr()
        return f


class SemName:
    def __init__(self, name):
        self.name = name


class Ctx:
    def __init__(self, sems, dry, labels, record=False):
        self.sems = sems
        self.dry = dry
        self.labels = labels
        self.record = record
        self.counts = {k: 0 for k in sems}

    def inc(self, instr, s, n=1, label=None):
        self.counts[s] += n
        if label is not None:
            key = (s,) + tuple(label)
            if self.dry:
                self.labels[key] = self.counts[s]
            else:
                assert self.labels.get(key) == self.counts[s], (
                    key, self.labels.get(key), self.counts[s])
        if not self.dry:
            instr.then_inc(self.sems[s], n)
        elif self.record:
            instr.then_inc(SemName(s), n)

    def alias(self, s, label):
        key = (s,) + tuple(label)
        if self.dry:
            self.labels[key] = self.counts[s]
        else:
            assert self.labels.get(key) == self.counts[s]

    def wait(self, eng, s, label):
        key = (s,) + tuple(label)
        if not self.dry or self.record:
            eng.wait_ge(SemName(s), self.labels[key]) if self.record else                 eng.wait_ge(self.sems[s], self.labels[key])

    def wait_if(self, eng, s, label):
        key = (s,) + tuple(label)
        if key in self.labels and (not self.dry or self.record):
            eng.wait_ge(SemName(s), self.labels[key]) if self.record else                 eng.wait_ge(self.sems[s], self.labels[key])


def build_nc(stage=STAGE):
    nc = bass.Bass("TRN2", target_bir_lowering=False)

    x_in = nc.dram_tensor("x_in", [32, P, D], F32, kind="ExternalInput").ap()
    qkv_in = nc.dram_tensor("qkv_in", [8, P, 3, 16, 64], F32, kind="ExternalInput").ap()
    qb_in = nc.dram_tensor("qb_in", [16, 64], F32, kind="ExternalInput").ap()
    pos_in = nc.dram_tensor("pos_in", [P, 2048], F32, kind="ExternalInput").ap()
    wo_in = nc.dram_tensor("wo_in", [8, P, D], F32, kind="ExternalInput").ap()
    ob_in = nc.dram_tensor("ob_in", [1, D], F32, kind="ExternalInput").ap()
    wr_in = nc.dram_tensor("wr_in", [8, P, 8], F32, kind="ExternalInput").ap()
    wfc_in = nc.dram_tensor("wfc_in", [9, 8, P, 4096], F32, kind="ExternalInput").ap()
    wpj_in = nc.dram_tensor("wpj_in", [9, 32, P, D], F32, kind="ExternalInput").ap()
    bfc_in = nc.dram_tensor("bfc_in", [9, 32, P], F32, kind="ExternalInput").ap()
    bpe_in = nc.dram_tensor("bpe_in", [8, D], F32, kind="ExternalInput").ap()
    bps_in = nc.dram_tensor("bps_in", [1, D], F32, kind="ExternalInput").ap()
    sq_in = nc.dram_tensor("sq_in", [P, SEQ], F32, kind="ExternalInput").ap()
    cq_in = nc.dram_tensor("cq_in", [P, SEQ], F32, kind="ExternalInput").ap()
    dm_in = nc.dram_tensor("dm_in", [4, P, 512], F32, kind="ExternalInput").ap()
    om_in = nc.dram_tensor("om_in", [4, P, 512], F32, kind="ExternalInput").ap()
    id_in = nc.dram_tensor("id_in", [P, P], F32, kind="ExternalInput").ap()

    y_out = nc.dram_tensor("y_out", [NT, P, D], F32, kind="ExternalOutput").ap()
    x1_sp = nc.dram_tensor("x1_sp", [NT, P, D], F32, kind="Internal").ap()
    ot_sp = nc.dram_tensor("ot_sp", [NPAIR, P, 2048], BF16, kind="Internal").ap()

    st_all = ExitStack()
    e = st_all.enter_context
    sems = {k: e(nc.semaphore(k)) for k in ("Sd", "Gd", "Pe", "Ac", "Ve", "St")}

    id_f = e(nc.sbuf_tensor("id_f", [P, P], F32))
    id_b = e(nc.sbuf_tensor("id_b", [P, P], BF16))
    ones_f = e(nc.sbuf_tensor("ones_f", [1, P], F32))
    ones_b = e(nc.sbuf_tensor("ones_b", [1, P], BF16))
    qb2 = e(nc.sbuf_tensor("qb2", [P, 8], F32))
    qbT = e(nc.sbuf_tensor("qbT", [64, 16], F32))
    ob32 = e(nc.sbuf_tensor("ob32", [1, D], F32))
    epst = e(nc.sbuf_tensor("epst", [P, 1], F32))
    PeT_e = e(nc.sbuf_tensor("PeT_e", [P, 8 * P], BF16))
    PeT_o = e(nc.sbuf_tensor("PeT_o", [P, 8 * P], BF16))

    st_ab = ExitStack()
    ea = st_ab.enter_context
    XT = ea(nc.sbuf_tensor("XT", [P, 8 * SEQ], BF16))
    sq_b = ea(nc.sbuf_tensor("sq_b", [P, SEQ], BF16))
    cq_b = ea(nc.sbuf_tensor("cq_b", [P, SEQ], BF16))
    dm_sb = ea(nc.sbuf_tensor("dm_sb", [P, 4 * 512], F32))
    om_sb = ea(nc.sbuf_tensor("om_sb", [P, 4 * 512], F32))

    st_a = ExitStack()
    x_st = [st_a.enter_context(nc.sbuf_tensor(f"x_st{i}", [P, D], F32)) for i in range(2)]
    xb_st = [st_a.enter_context(nc.sbuf_tensor(f"xb_st{i}", [P, D], BF16)) for i in range(2)]
    pos_st = st_a.enter_context(nc.sbuf_tensor("pos_st", [P, 2048], F32))
    qb_st = st_a.enter_context(nc.sbuf_tensor("qb_st", [16, 64], F32))
    ob_st = st_a.enter_context(nc.sbuf_tensor("ob_st", [1, D], F32))
    psA_T = [st_a.enter_context(nc.psum_tensor(f"psA_T{i}", [P, 1024], BF16)) for i in range(2)]
    psA_G = [st_a.enter_context(nc.psum_tensor(f"psA_G{i}", [P, 512], F32)) for i in range(2)]
    st_a.close()

    st_b = ExitStack()
    eb = st_b.enter_context
    KT = eb(nc.sbuf_tensor("KT", [P, SEQ], BF16))
    Vtm = eb(nc.sbuf_tensor("Vtm", [P, 2 * 32 * 65], BF16))
    Qp = eb(nc.sbuf_tensor("Qp", [P, 2048], BF16))
    A1s = eb(nc.sbuf_tensor("A1s", [P, 2048], BF16))
    A2s = eb(nc.sbuf_tensor("A2s", [P, 2048], BF16))
    U1 = [eb(nc.sbuf_tensor(f"U1_{i}", [P, 2048], BF16)) for i in range(2)]
    U2 = [eb(nc.sbuf_tensor(f"U2_{i}", [P, 2048], BF16)) for i in range(2)]
    Utmp = eb(nc.sbuf_tensor("Utmp", [P, 2048], BF16))
    Ssb = eb(nc.sbuf_tensor("Ssb", [P, SEQ], F32))
    Asb = [eb(nc.sbuf_tensor(f"Asb{i}", [P, SEQ], BF16)) for i in range(2)]
    ATr = [eb(nc.sbuf_tensor(f"ATr{i}", [P, 512], BF16)) for i in range(2)]
    mxt = eb(nc.sbuf_tensor("mxt", [P, 24], F32))
    lrow = eb(nc.sbuf_tensor("lrow", [1, P], F32))
    rrow = eb(nc.sbuf_tensor("rrow", [1, P], F32))
    rbb = eb(nc.sbuf_tensor("rbb", [64, P], F32))
    otp = eb(nc.sbuf_tensor("otp", [P, 2048], BF16))
    wq_t = [eb(nc.sbuf_tensor(f"wq_t{i}", [P, 1024], BF16)) for i in range(2)]
    wk_t = [eb(nc.sbuf_tensor(f"wk_t{i}", [P, 1024], BF16)) for i in range(2)]
    wv_t = [eb(nc.sbuf_tensor(f"wv_t{i}", [P, 1024], BF16)) for i in range(2)]
    psB = [eb(nc.psum_tensor(f"psB{i}", [P, 512], F32)) for i in range(6)]
    psB_T = [eb(nc.psum_tensor(f"psB_T{i}", [P, 512], BF16)) for i in range(2)]
    st_b.close()
    st_ab.close()

    st_cd = ExitStack()
    ec = st_cd.enter_context
    X1T = ec(nc.sbuf_tensor("X1T", [P, 8 * 2048], BF16))
    gate = ec(nc.sbuf_tensor("gate", [P, NT * 9], F32))
    gT = ec(nc.sbuf_tensor("gT", [8, 2048], BF16))

    st_c = ExitStack()
    cc = st_c.enter_context
    ot_ld = [cc(nc.sbuf_tensor(f"ot_ld{i}", [P, 2048], BF16)) for i in range(NPAIR)]
    wo_b = cc(nc.sbuf_tensor("wo_b", [P, 8 * D], BF16))
    wr_b = cc(nc.sbuf_tensor("wr_b", [P, 64], BF16))
    h1 = [cc(nc.sbuf_tensor(f"h1_{i}", [P, D], F32)) for i in range(4)]
    x1scr = [cc(nc.sbuf_tensor(f"x1scr{i}", [P, D], F32)) for i in range(2)]
    x1b = [cc(nc.sbuf_tensor(f"x1b{i}", [P, D], BF16)) for i in range(4)]
    sqscr = cc(nc.sbuf_tensor("sqscr", [P, D], F32))
    xre = [cc(nc.sbuf_tensor(f"xre{i}", [P, D], F32)) for i in range(2)]
    l_all = cc(nc.sbuf_tensor("l_all", [P, NT * 8], F32))
    stC = cc(nc.sbuf_tensor("stC", [P, NT * 8], F32))
    gch = cc(nc.sbuf_tensor("gch", [P, NT * 40], F32))
    psC = [cc(nc.psum_tensor(f"psC{i}", [P, 512], F32)) for i in range(3)]
    psC_T = [cc(nc.psum_tensor(f"psC_T{i}", [P, 1024], BF16)) for i in range(4)]
    st_c.close()

    st_d = ExitStack()
    dd = st_d.enter_context
    Hb = dd(nc.sbuf_tensor("Hb", [P, 32 * 1024], BF16))
    routed = dd(nc.sbuf_tensor("routed", [P, 8 * D], F32))
    xf = [dd(nc.sbuf_tensor(f"xf{i}", [P, D], F32)) for i in range(2)]
    sqscrD = dd(nc.sbuf_tensor("sqscrD", [P, D], F32))
    wscr = [dd(nc.sbuf_tensor(f"wscr{i}", [P, D], F32)) for i in range(2)]
    tscr = [dd(nc.sbuf_tensor(f"tscr{i}", [P, D], F32)) for i in range(2)]
    sgscr = [dd(nc.sbuf_tensor(f"sgscr{i}", [P, D], F32)) for i in range(2)]
    wfc_t = [dd(nc.sbuf_tensor(f"wfc_t{i}", [P, 1024], BF16)) for i in range(2)]
    wpj_t = [dd(nc.sbuf_tensor(f"wpj_t{i}", [P, 512], BF16)) for i in range(4)]
    bfc_t = [dd(nc.sbuf_tensor(f"bfc_t{i}", [P, 32], F32)) for i in range(2)]
    bfc_st = [dd(nc.sbuf_tensor(f"bfc_st{i}", [32, P], F32)) for i in range(2)]
    bpall = dd(nc.sbuf_tensor("bpall", [8, D], BF16))
    bpsh = dd(nc.sbuf_tensor("bpsh", [1, D], BF16))
    x1re = [dd(nc.sbuf_tensor(f"x1re{i}", [P, D], F32)) for i in range(2)]
    yst = [dd(nc.sbuf_tensor(f"yst{i}", [P, D], F32)) for i in range(2)]
    stE = dd(nc.sbuf_tensor("stE", [P, 8 * 8], F32))
    sqscrE = sqscrD
    psD = [dd(nc.psum_tensor(f"psD{i}", [P, 512], F32)) for i in range(8)]
    st_d.close()
    st_cd.close()

    labels = {}
    ntb = NT if B_MAXQT is None else B_MAXQT
    SCHEDL = make_sched(ntb)

    def phA_sync(g, c):
        c.inc(g.dma_start(id_f[:, :], id_in), "Sd", 16, ("idf",))
        c.inc(g.dma_start(dm_sb[:, :].rearrange("p (v f) -> p v f", v=4),
                          dm_in.rearrange("v p f -> p v f")), "Sd", 16, ("dm",))
        c.inc(g.dma_start(om_sb[:, :].rearrange("p (v f) -> p v f", v=4),
                          om_in.rearrange("v p f -> p v f")), "Sd", 16, ("om",))
        c.inc(g.dma_start(pos_st[:, :], pos_in), "Sd", 16, ("pos",))
        c.inc(g.dma_start(qb_st[:, :], qb_in), "Sd", 16, ("qb",))
        c.inc(g.dma_start(ob_st[:, :], ob_in), "Sd", 16, ("ob",))
        for i in range(32):
            if i >= 2:
                c.wait(g, "Ve", ("Acast", i - 2))
            c.inc(g.dma_start(x_st[i % 2][:, :], x_in[i]), "Sd", 16, ("Axd", i))

    def phA_gp(g, c):
        c.inc(g.dma_start(id_b[:, :], id_in), "Gd", 16, ("idb",))
        c.inc(g.dma_start(sq_b[:, :], sq_in), "Gd", 16, ("sqb",))
        c.inc(g.dma_start(cq_b[:, :], cq_in), "Gd", 16, ("cqb",))

    def phA_ve(v, c):
        v.memset(ones_f[:, :], 1.0)
        v.memset(epst[:, :], EPS)
        c.inc(v.memset(ones_b[:, :], 1.0), "Ve", 1, ("Aones",))
        for i in range(32):
            c.wait(v, "Sd", ("Axd", i))
            if i >= 2:
                c.wait(v, "Pe", ("Atr", i - 2))
            c.inc(v.tensor_copy(xb_st[i % 2][:, :], x_st[i % 2][:, :]), "Ve", 1, ("Acast", i))

    def phA_pe(t, c):
        c.wait(t, "Gd", ("idb",))
        for i in range(32):
            c.wait(t, "Ve", ("Acast", i))
            if i >= 2:
                c.wait(t, "Ac", ("Aev", i - 2))
            ins = None
            for dc in range(8):
                ins = t.transpose(psA_T[i % 2][:, dc * P:(dc + 1) * P],
                                  xb_st[i % 2][:, dc * P:(dc + 1) * P], id_b[:, :])
            c.inc(ins, "Pe", 1, ("Atr", i))
        c.wait(t, "Sd", ("pos",))
        c.wait(t, "Sd", ("idf",))
        for par in range(2):
            for n in range(16):
                j = par * 16 + n
                c.wait_if(t, "Ac", ("Apev", j - 2))
                ins = t.transpose(psA_G[j % 2][0:64, 0:P],
                                  pos_st[:, par * 1024 + n * 64: par * 1024 + (n + 1) * 64],
                                  id_f[:, :])
                c.inc(ins, "Pe", 1, ("Aptr", j))
        c.wait(t, "Sd", ("qb",))
        c.wait(t, "Ac", ("Apev", 30))
        c.inc(t.transpose(psA_G[0][0:64, 0:16], qb_st[0:16, 0:64], id_f[0:16, 0:16]),
              "Pe", 1, ("Aqbt",))

    def phA_ac(a, c):
        for i in range(32):
            c.wait(a, "Pe", ("Atr", i))
            ins = a.copy(
                XT[:, :].rearrange("p (dc f) -> p dc f", dc=8)[:, :, i * P:(i + 1) * P],
                psA_T[i % 2][:, :].rearrange("p (dc f) -> p dc f", dc=8))
            c.inc(ins, "Ac", 1, ("Aev", i))
        for par in range(2):
            for n in range(16):
                j = par * 16 + n
                c.wait(a, "Pe", ("Aptr", j))
                dst = PeT_e if par == 0 else PeT_o
                pp, hp = n // 2, n % 2
                ins = a.activation(dst[64 * hp:64 * hp + 64, pp * P:(pp + 1) * P],
                                   psA_G[j % 2][0:64, 0:P], AF.Identity, scale=C_POS)
                c.inc(ins, "Ac", 1, ("Apev", j))
        c.wait(a, "Pe", ("Aqbt",))
        a.activation(qbT[0:64, 0:16], psA_G[0][0:64, 0:16], AF.Identity, scale=0.125)
        for n in range(16):
            pp, hp = n // 2, n % 2
            a.copy(qb2[64 * hp:64 * hp + 64, pp:pp + 1], qbT[0:64, n:n + 1])
        c.wait(a, "Sd", ("ob",))
        c.inc(a.mul(ob32[0:1, :], ob_st[0:1, :], 32.0), "Ac", 1, ("Aconsts",))

    def phB_gp(g, c, p):
        buf = p % 2
        for Aidx, wt in ((0, wq_t), (1, wk_t), (2, wv_t)):
            if p >= 2:
                c.wait(g, "Pe", ("Bwdone", p - 2))
            src = qkv_in[:, :, Aidx, 2 * p:2 * p + 2, :].rearrange("dc p n h -> p dc (n h)")
            ins = g.dma_start(wt[buf][:, :].rearrange("p (dc f) -> p dc f", dc=8), src)
            c.inc(ins, "Gd", 16, ("Bw", p, Aidx))

    def phB_pe(t, c, p):
        buf = p % 2
        for Aidx in range(3):
            c.wait(t, "Gd", ("Bw", p, Aidx))
        if p == 0:
            c.wait(t, "Gd", ("sqb",))
            c.wait(t, "Gd", ("cqb",))
            c.wait(t, "Ve", ("Bvmemset",))
            c.wait(t, "Ac", ("Aconsts",))
            c.wait(t, "Ve", ("Aones",))
        for tch in range(8 if "k" in B_PRE else 0):
            c.wait_if(t, "Ac", ("Bkev", p, tch - 2))
            ins = None
            for dc in range(8):
                ins = t.matmul(psB[4 + (tch % 2)][:, :], wk_t[buf][:, dc * P:(dc + 1) * P],
                               XT[:, dc * SEQ + tch * 512: dc * SEQ + (tch + 1) * 512],
                               start=(dc == 0), stop=(dc == 7))
            c.inc(ins, "Pe", 1, ("Bkmm", p, tch))
        for grp in range(8 if "v" in B_PRE else 0):
            if grp < 2:
                c.wait_if(t, "Ac", ("Bkev", p, 6 + grp))
            c.wait_if(t, "Ve", ("Bvev", p, grp - 2))
            ins = None
            for ktl in range(4):
                kt = grp * 4 + ktl
                for dc in range(8):
                    ins = t.matmul(psB[4 + (grp % 2)][:, ktl * P:(ktl + 1) * P],
                                   XT[:, dc * SEQ + kt * P: dc * SEQ + (kt + 1) * P],
                                   wv_t[buf][:, dc * P:(dc + 1) * P],
                                   start=(dc == 0), stop=(dc == 7))
            c.inc(ins, "Pe", 1, ("Bvmm", p, grp))
        for tcg in range(4 if "q" in B_PRE else 0):
            if tcg < 2:
                c.wait_if(t, "Ve", ("Bvev", p, 6 + tcg))
            c.wait_if(t, "Ac", ("Bqev", p, tcg - 2))
            ins = None
            for dc in range(8):
                ins = t.matmul(psB[4 + (tcg % 2)][:, :], wq_t[buf][:, dc * P:(dc + 1) * P],
                               XT[:, dc * SEQ + tcg * 512: dc * SEQ + (tcg + 1) * 512],
                               start=(dc == 0), stop=(dc == 7))
            c.inc(ins, "Pe", 1, ("Bqmm", p, tcg))
        c.alias("Pe", ("Bwdone", p))

        def scores_emit(rec):
            hp, qt = rec["hp"], rec["qt"]
            for ch in rec["chunks"]:
                sc, bank = ch["sc"], ch["bank"]
                c.wait_if(t, "Ac", ("BsevA", p, sc - 2))
                c.wait_if(t, "Ve", ("BsevV", p, sc - 2))
                c.wait_if(t, "Ve", ("Bmax", p, sc - 2))
                t.matmul(psB[bank][:, :],
                         Qp[64 * hp:64 * hp + 64, qt * P:(qt + 1) * P],
                         KT[64 * hp:64 * hp + 64, ch["base"]:ch["base"] + 512],
                         start=True, stop=False)
                t.matmul(psB[bank][:, :], U1[hp][:, qt * P:(qt + 1) * P],
                         cq_b[:, ch["base"]:ch["base"] + 512], start=False, stop=False)
                ins = t.matmul(psB[bank][:, :], U2[hp][:, qt * P:(qt + 1) * P],
                               sq_b[:, ch["base"]:ch["base"] + 512], start=False, stop=True)
                c.inc(ins, "Pe", 1, ("Bsmm", p, sc))

        def pv_group(rec, k, first, last):
            gr = rec["groups"][k]
            hp, obank = rec["hp"], rec["obank"]
            c.wait(t, "Ac", ("Batev", p, gr["gt"]))
            if first:
                c.wait_if(t, "Ve", ("Botev", p, rec["qc"] - 2))
            ins = None
            for i4 in range(4):
                ktile = gr["base"] // P + i4
                ins = t.matmul(psB[obank][0:65, 0:P],
                               Vtm[:, hp * 2080 + ktile * 65: hp * 2080 + ktile * 65 + 65],
                               ATr[gr["bank"]][:, i4 * P:(i4 + 1) * P],
                               start=(first and i4 == 0), stop=(last and i4 == 3))
            if last:
                c.inc(ins, "Pe", 1, ("Bpv", p, rec["qc"]))

        def pv_emit(rec):
            qt, qc = rec["qt"], rec["qc"]
            c.wait(t, "Ac", ("Bexp", p, qc))
            groups = rec["groups"]
            for k, gr in enumerate(groups):
                if gr["gt"] >= 2:
                    c.wait_if(t, "Ac", ("Batev", p, gr["gt"] - 2))
                ins = None
                for i4 in range(4):
                    ins = t.transpose(psB_T[gr["bank"]][:, i4 * P:(i4 + 1) * P],
                                      Asb[qt % 2][:, gr["base"] + i4 * P: gr["base"] + (i4 + 1) * P],
                                      id_b[:, :])
                c.inc(ins, "Pe", 1, ("Btr", p, gr["gt"]))
                if k >= 1:
                    pv_group(rec, k - 1, first=(k - 1 == 0), last=False)
            pv_group(rec, len(groups) - 1, first=(len(groups) == 1), last=True)
            c.wait(t, "Ve", ("Brrow", p, qc))
            c.wait_if(t, "Ac", ("Brbg", p, qc - 1))
            ins = t.matmul(psB[4][0:64, 0:P], ones_f[0:1, 0:64], rrow[0:1, 0:P],
                           start=True, stop=True)
            c.inc(ins, "Pe", 1, ("Bbc", p, qc))

        for hp in range(2 if "a" in B_PRE else 0):
            c.wait_if(t, "Ac", ("Bqev", p, 3))
            if hp == 1:
                c.wait_if(t, "Ve", ("BUdone", p, 0))
                c.wait_if(t, "Ac", ("Brbg", p, ntb - 1))
            for par, PeTx in ((0, PeT_e), (1, PeT_o)):
                for tcg in range(4):
                    j = par * 4 + tcg
                    if hp == 1 and j < 2:
                        c.wait_if(t, "Ac", ("Baev", p, 0, 6 + j))
                    c.wait_if(t, "Ac", ("Baev", p, hp, j - 2))
                    ins = t.matmul(psB[4 + (j % 2)][:, :],
                                   PeTx[64 * hp:64 * hp + 64, p * P:(p + 1) * P],
                                   Qp[64 * hp:64 * hp + 64, tcg * 512:(tcg + 1) * 512],
                                   start=True, stop=True)
                    c.inc(ins, "Pe", 1, ("Bamm", p, hp, j))
            c.wait_if(t, "Ac", ("Bkev", p, 7))
            c.wait_if(t, "Ve", ("BUdone", p, hp))
            if hp == 0:
                c.wait_if(t, "Ve", ("Bvev", p, 7))

            recs = [r for r in SCHEDL if r["hp"] == hp]
            if recs:
                scores_emit(recs[0])
                for i, rec in enumerate(recs):
                    if i + 1 < len(recs):
                        scores_emit(recs[i + 1])
                    pv_emit(rec)

    def phB_ac(a, c, p):
        for tch in range(8 if "k" in B_PRE else 0):
            c.wait(a, "Pe", ("Bkmm", p, tch))
            ins = a.activation(KT[:, tch * 512:(tch + 1) * 512], psB[4 + (tch % 2)][:, :],
                               AF.Identity, scale=S_QKV)
            c.inc(ins, "Ac", 1, ("Bkev", p, tch))
        for tcg in range(4 if "q" in B_PRE else 0):
            c.wait(a, "Pe", ("Bqmm", p, tcg))
            ins = a.activation(Qp[:, tcg * 512:(tcg + 1) * 512], psB[4 + (tcg % 2)][:, :],
                               AF.Identity, bias=qb2[:, p:p + 1], scale=SC_Q)
            c.inc(ins, "Ac", 1, ("Bqev", p, tcg))
        def ac_tail(rec):
            qc = rec["qc"]
            for gr in rec["groups"]:
                c.wait(a, "Pe", ("Btr", p, gr["gt"]))
                c.inc(a.copy(ATr[gr["bank"]][:, :], psB_T[gr["bank"]][:, :]),
                      "Ac", 1, ("Batev", p, gr["gt"]))
            c.wait(a, "Pe", ("Bpv", p, qc))
            ins = a.copy(lrow[0:1, 0:P], psB[rec["obank"]][64:65, 0:P])
            c.inc(ins, "Ac", 1, ("Blrow", p, qc))
            c.wait(a, "Pe", ("Bbc", p, qc))
            ins = a.copy(rbb[0:64, 0:P], psB[4][0:64, 0:P])
            c.inc(ins, "Ac", 1, ("Brbg", p, qc))

        for hp in range(2 if "a" in B_PRE else 0):
            for j in range(8):
                c.wait(a, "Pe", ("Bamm", p, hp, j))
                if hp == 1 and j == 0:
                    c.wait_if(a, "Ve", ("BUdone", p, 0))
                dst = (A1s if j < 4 else A2s)[:, (j % 4) * 512:(j % 4 + 1) * 512]
                c.inc(a.copy(dst, psB[4 + (j % 2)][:, :]), "Ac", 1, ("Baev", p, hp, j))
            recs = [r for r in SCHEDL if r["hp"] == hp]
            for rec in recs:
                qt, qc = rec["qt"], rec["qc"]
                for ch in rec["chunks"]:
                    if ch["masked"]:
                        continue
                    c.wait(a, "Pe", ("Bsmm", p, ch["sc"]))
                    ins = a.copy(Ssb[:, ch["base"]:ch["base"] + 512], psB[ch["bank"]][:, :])
                    c.inc(ins, "Ac", 1, ("BsevA", p, ch["sc"]))
                c.wait(a, "Ve", ("Bnegmax", p, qc))
                if qc >= 2:
                    c.wait_if(a, "Pe", ("Btr", p, SCHEDL[qc - 2]["groups"][-1]["gt"]))
                w = kch_of(qt) * 512
                a.activation(Asb[qt % 2][:, 0:w], Ssb[:, 0:w], AF.Exp,
                             bias=mxt[:, qt:qt + 1], scale=1.0)
                ins = a.activation(Asb[qt % 2][:, 2048:2048 + w], Ssb[:, 2048:2048 + w],
                                   AF.Exp, bias=mxt[:, qt:qt + 1], scale=1.0)
                c.inc(ins, "Ac", 1, ("Bexp", p, qc))
                if qc > (0 if hp == 0 else ntb):
                    ac_tail(SCHEDL[qc - 1])
            if recs:
                ac_tail(recs[-1])

    def phB_ve(v, c, p):
        if p == 0:
            c.inc(v.memset(Vtm[:, :], 1.0), "Ve", 1, ("Bvmemset",))
        for grp in range(8 if "v" in B_PRE else 0):
            c.wait(v, "Pe", ("Bvmm", p, grp))
            ins = None
            for hp in range(2):
                src = psB[4 + (grp % 2)][:, :].rearrange("p (i f) -> p i f", i=4)[:, :, hp * 64:hp * 64 + 64]
                dstc = Vtm[:, hp * 2080 + grp * 4 * 65: hp * 2080 + (grp * 4 + 4) * 65]
                dst = dstc.rearrange("p (i f) -> p i f", i=4)[:, :, 0:64]
                ins = v.tensor_scalar(dst, src, S_QKV, None, ALU.mult)
            c.inc(ins, "Ve", 1, ("Bvev", p, grp))
        for hp in range(2 if "u" in B_PRE else 0):
            c.wait(v, "Ac", ("Baev", p, hp, 7))
            for tcg in range(4):
                v.tensor_tensor(U1[hp][:, tcg * 512:(tcg + 1) * 512],
                                A1s[:, tcg * 512:(tcg + 1) * 512],
                                sq_b[:, tcg * 512:(tcg + 1) * 512], ALU.mult)
            for tcg in range(4):
                v.tensor_tensor(Utmp[:, tcg * 512:(tcg + 1) * 512],
                                A2s[:, tcg * 512:(tcg + 1) * 512],
                                cq_b[:, tcg * 512:(tcg + 1) * 512], ALU.mult)
            for tcg in range(4):
                v.tensor_tensor(U1[hp][:, tcg * 512:(tcg + 1) * 512],
                                U1[hp][:, tcg * 512:(tcg + 1) * 512],
                                Utmp[:, tcg * 512:(tcg + 1) * 512], ALU.add)
            for tcg in range(4):
                v.tensor_tensor(U2[hp][:, tcg * 512:(tcg + 1) * 512],
                                A2s[:, tcg * 512:(tcg + 1) * 512],
                                sq_b[:, tcg * 512:(tcg + 1) * 512], ALU.mult)
            for tcg in range(4):
                v.tensor_tensor(Utmp[:, tcg * 512:(tcg + 1) * 512],
                                A1s[:, tcg * 512:(tcg + 1) * 512],
                                cq_b[:, tcg * 512:(tcg + 1) * 512], ALU.mult)
            ins = None
            for tcg in range(4):
                ins = v.tensor_tensor(U2[hp][:, tcg * 512:(tcg + 1) * 512],
                                      U2[hp][:, tcg * 512:(tcg + 1) * 512],
                                      Utmp[:, tcg * 512:(tcg + 1) * 512], ALU.subtract)
            c.inc(ins, "Ve", 1, ("BUdone", p, hp))

            def ve_tail(rec):
                qc2, qt2, hp2 = rec["qc"], rec["qt"], rec["hp"]
                c.wait(v, "Ac", ("Blrow", p, qc2))
                c.inc(v.reciprocal(rrow[0:1, 0:P], lrow[0:1, 0:P]), "Ve", 1, ("Brrow", p, qc2))
                c.wait(v, "Ac", ("Brbg", p, qc2))
                if p >= 1 and qc2 == 0:
                    c.wait(v, "St", ("Bot_st", p - 1))
                ins2 = v.tensor_tensor(otp[64 * hp2:64 * hp2 + 64, qt2 * P:(qt2 + 1) * P],
                                       psB[rec["obank"]][0:64, 0:P], rbb[0:64, 0:P], ALU.mult)
                c.inc(ins2, "Ve", 1, ("Botev", p, qc2))

            recs = [r for r in SCHEDL if r["hp"] == hp]
            for rec in recs:
                qt, qc = rec["qt"], rec["qc"]
                for ci, ch in enumerate(rec["chunks"]):
                    c.wait(v, "Pe", ("Bsmm", p, ch["sc"]))
                    if ch["masked"]:
                        if qc >= 1:
                            c.wait_if(v, "Ac", ("Bexp", p, qc - 1))
                        mvar = qt % 4
                        msrc = dm_sb if ch["side"] == "own" else om_sb
                        ins = v.tensor_tensor(Ssb[:, ch["base"]:ch["base"] + 512],
                                              psB[ch["bank"]][:, :],
                                              msrc[:, mvar * 512:(mvar + 1) * 512], ALU.add)
                        c.inc(ins, "Ve", 1, ("BsevV", p, ch["sc"]))
                        ins = v.reduce_max(mxt[:, 16 + ci:17 + ci], psB[ch["bank"]][:, :], axis=AX.X)
                    else:
                        c.wait(v, "Ac", ("BsevA", p, ch["sc"]))
                        ins = v.reduce_max(mxt[:, 16 + ci:17 + ci],
                                           Ssb[:, ch["base"]:ch["base"] + 512], axis=AX.X)
                    c.inc(ins, "Ve", 1, ("Bmax", p, ch["sc"]))
                v.drain()
                ins = v.reduce_max(mxt[:, qt:qt + 1], mxt[:, 16:16 + rec["nch"]],
                                   axis=AX.X, negate=True)
                c.inc(ins, "Ve", 1, ("Bnegmax", p, qc))
                if qc > (0 if hp == 0 else ntb):
                    ve_tail(SCHEDL[qc - 1])
            if recs:
                ve_tail(recs[-1])

    def phB_sync(g, c, p):
        if B_MAXQT == 0:
            return
        c.wait(g, "Ve", ("Botev", p, 2 * ntb - 1))
        c.inc(g.dma_start(ot_sp[p], otp[:, :]), "St", 16, ("Bot_st", p))

    def phC_sync(g, c):
        for p in range(NPAIR):
            c.inc(g.dma_start(ot_ld[p][:, :], ot_sp[p]), "Sd", 16, ("Cot", p))
        do_norm = C_LEVEL >= 2 and ('spill' in C_SUB or C_SUB == 'all')

        def spill(t_):
            if WAC is not None:
                if not c.dry:
                    g.wait_ge(sems["Ac"], WAC)
                elif c.record:
                    g.wait_ge(SemName("Ac"), WAC)
            elif C_SUB != 'all' and 'wAc1700' in C_SUB:
                c.wait(g, "Ac", ("Baev", 3, 1, 7))
            elif C_SUB != 'all' and 'wAc800' in C_SUB:
                c.wait(g, "Ac", ("Baev", 1, 1, 7))
            elif C_SUB == 'all' or 'nonormwait' not in C_SUB:
                c.wait(g, "Ac", ("Cnorm", t_))
            if C_SUB == 'all' or 'nox1sp' not in C_SUB:
                c.inc(g.dma_start(x1_sp[t_], x1scr[t_ % 2][:, :]), "St", 16, ("Cx1sp", t_))
            if stage == "attn" and (C_SUB == 'all' or 'noydbg' not in C_SUB):
                if C_SUB != 'all' and 'srch1' in C_SUB:
                    srcap = h1[t_ % 4][:, :]
                elif C_SUB != 'all' and 'srcxre' in C_SUB:
                    srcap = xre[t_ % 2][:, :]
                else:
                    srcap = x1scr[t_ % 2][:, :]
                c.inc(g.dma_start(y_out[t_], srcap), "St", 16, ("Cydbg", t_))

        for t_ in range(NT if C_LEVEL >= 2 else 0):
            if t_ >= 2:
                c.wait(g, "Ve", ("Cadd", t_ - 2))
            c.inc(g.dma_start(xre[t_ % 2][:, :], x_in[t_]), "Sd", 16, ("Cxre", t_))
            if do_norm and t_ >= 4:
                spill(t_ - 4)
        if do_norm:
            for t_ in range(NT - 4, NT):
                spill(t_)
        if stage == "attn":
            if not c.dry:
                g.wait_ge(sems["St"], c.counts["St"])
            elif c.record:
                g.wait_ge(SemName("St"), c.counts["St"])

    def phC_gp(g, c):
        c.inc(g.dma_start(wo_b[:, :].rearrange("p (dc f) -> p dc f", dc=8),
                          wo_in.rearrange("dc p f -> p dc f")), "Gd", 16, ("Cwo",))
        c.inc(g.dma_start(wr_b[:, :].rearrange("p (dc f) -> p dc f", dc=8),
                          wr_in.rearrange("dc p f -> p dc f")), "Gd", 16, ("Cwr",))
        c.inc(g.dma_start(bpall[:, :], bpe_in), "Gd", 16, ("Cbpe",))
        c.inc(g.dma_start(bpsh[:, :], bps_in), "Gd", 16, ("Cbps",))

    def emit_C_tr_logits(t, c, tl):
        for t_ in tl:
            c.wait(t, "Ac", ("Cx1b", t_))
            if t_ >= 4:
                c.wait(t, "Ac", ("Cx1tev", t_ - 4))
            ins = None
            for dc in range(8):
                ins = t.transpose(psC_T[t_ % 4][:, dc * P:(dc + 1) * P],
                                  x1b[t_ % 4][:, dc * P:(dc + 1) * P], id_b[:, :])
            c.inc(ins, "Pe", 1, ("Cx1t", t_))
        for t_ in (tl if C_LEVEL >= 4 else []):
            c.wait(t, "Ac", ("Cx1tev", t_))
            c.wait(t, "Gd", ("Cwr",))
            c.wait_if(t, "Ac", ("Clsb", t_ - 1))
            ins = None
            for dc in range(8):
                ins = t.matmul(psC[2][:, 0:8],
                               X1T[:, dc * 2048 + t_ * P: dc * 2048 + (t_ + 1) * P],
                               wr_b[:, dc * 8:(dc + 1) * 8], start=(dc == 0), stop=(dc == 7))
            c.inc(ins, "Pe", 1, ("Clog", t_))

    def phC_pe(t, c):
        for p in range(NPAIR):
            c.wait(t, "Sd", ("Cot", p))
        c.wait(t, "Gd", ("Cwo",))
        for w in range(4):
            tl = [4 * w + i for i in range(4)]
            for t_ in tl:
                for fo in range(2):
                    bank = (2 * t_ + fo) % 2
                    c.wait_if(t, "Ac", ("Copev", t_ - 1, fo))
                    ins = t.matmul(psC[bank][:, :], ones_f[0:1, 0:P],
                                   ob32[0:1, fo * 512:(fo + 1) * 512], start=True, stop=False)
                    for dc in range(8):
                        ins = t.matmul(psC[bank][:, :], ot_ld[dc][:, t_ * P:(t_ + 1) * P],
                                       wo_b[:, dc * D + fo * 512: dc * D + fo * 512 + 512],
                                       start=False, stop=(dc == 7))
                    c.inc(ins, "Pe", 1, ("Copmm", t_, fo))
            if w >= 1 and C_LEVEL >= 3:
                emit_C_tr_logits(t, c, [4 * (w - 1) + i for i in range(4)])
        if C_LEVEL >= 3:
            emit_C_tr_logits(t, c, [12, 13, 14, 15])
        for t_ in range(NT if C_LEVEL >= 5 else 0):
            c.wait(t, "Ve", ("Cgate", t_))
            c.wait(t, "Ac", ("Clsb", NT - 1))
            c.wait_if(t, "Ac", ("CgTev", t_ - 1))
            ins = t.transpose(psC[2][0:8, (t_ % 2) * P:(t_ % 2) * P + P],
                              gate[:, t_ * 9:t_ * 9 + 8], id_f[:, :])
            c.inc(ins, "Pe", 1, ("CgT", t_))

    def phC_ac(a, c):
        def col(t_, r):
            return stC[:, t_ * 8 + r: t_ * 8 + r + 1]
        for w in range(4):
            tl = [4 * w + i for i in range(4)]
            for t_ in tl:
                for fo in range(2):
                    bank = (2 * t_ + fo) % 2
                    c.wait(a, "Pe", ("Copmm", t_, fo))
                    ins = a.activation(h1[t_ % 4][:, fo * 512:(fo + 1) * 512],
                                       psC[bank][:, :], AF.Identity, scale=SC_WO)
                    c.inc(ins, "Ac", 1, ("Copev", t_, fo))
            if 'sq' in C_SUB or C_SUB == 'all':
                for t_ in (tl if C_LEVEL >= 2 else []):
                    c.wait(a, "Ve", ("Cadd", t_))
                    ins = a.activation(sqscr[:, :], h1[t_ % 4][:, :], AF.Square, accum_out=col(t_, 5))
                    c.inc(ins, "Ac", 1, ("Csq", t_))
            if 'mm2' in C_SUB or C_SUB == 'all':
                for t_ in (tl if C_LEVEL >= 2 else []):
                    c.wait(a, "Ve", ("Cm", t_))
                    c.inc(a.activation(col(t_, 6), col(t_, 1), AF.Square), "Ac", 1, ("Cmm2", t_))
            if 'rstd' in C_SUB or C_SUB == 'all':
                for t_ in (tl if C_LEVEL >= 2 else []):
                    c.wait(a, "Ve", ("Cvar", t_))
                    a.activation(col(t_, 3), col(t_, 2), AF.Ln, bias=epst[:, 0:1], scale=1.0)
                    c.inc(a.activation(col(t_, 3), col(t_, 3), AF.Exp, scale=-0.5),
                          "Ac", 1, ("Crstd", t_))
            if 'norm' in C_SUB or C_SUB == 'all':
                for t_ in (tl if C_LEVEL >= 2 else []):
                    if C_SUB == 'all' or 'nowait' not in C_SUB:
                        c.wait(a, "Ve", ("Cbb", t_))
                    if t_ >= 2:
                        c.wait_if(a, "St", ("Cx1sp", t_ - 2))
                        c.wait_if(a, "St", ("Cydbg", t_ - 2))
                    if C_SUB != 'all' and 'sqdst' in C_SUB:
                        ins = a.copy(sqscr[:, :], h1[t_ % 4][:, :])
                    elif C_SUB != 'all' and 'plain' in C_SUB:
                        ins = a.copy(x1scr[t_ % 2][:, :], h1[t_ % 4][:, :])
                    else:
                        ins = a.activation(x1scr[t_ % 2][:, :], h1[t_ % 4][:, :], AF.Identity,
                                           bias=col(t_, 4), scale=col(t_, 3))
                    c.inc(ins, "Ac", 1, ("Cnorm", t_))
                    if C_LEVEL >= 3:
                        if t_ >= 4:
                            c.wait(a, "Pe", ("Cx1t", t_ - 4))
                        c.inc(a.copy(x1b[t_ % 4][:, :], x1scr[t_ % 2][:, :]), "Ac", 1, ("Cx1b", t_))
            tlp = [t_ - 4 for t_ in tl if t_ - 4 >= 0]
            for t_ in (tlp if C_LEVEL >= 3 else []):
                c.wait(a, "Pe", ("Cx1t", t_))
                ins = a.copy(
                    X1T[:, :].rearrange("p (dc f) -> p dc f", dc=8)[:, :, t_ * P:(t_ + 1) * P],
                    psC_T[t_ % 4][:, :].rearrange("p (dc f) -> p dc f", dc=8))
                c.inc(ins, "Ac", 1, ("Cx1tev", t_))
            for t_ in (tlp if C_LEVEL >= 4 else []):
                c.wait(a, "Pe", ("Clog", t_))
                c.inc(a.copy(l_all[:, t_ * 8:(t_ + 1) * 8], psC[2][:, 0:8]),
                      "Ac", 1, ("Clsb", t_))
        for t_ in ([12, 13, 14, 15] if C_LEVEL >= 3 else []):
            c.wait(a, "Pe", ("Cx1t", t_))
            ins = a.copy(
                X1T[:, :].rearrange("p (dc f) -> p dc f", dc=8)[:, :, t_ * P:(t_ + 1) * P],
                psC_T[t_ % 4][:, :].rearrange("p (dc f) -> p dc f", dc=8))
            c.inc(ins, "Ac", 1, ("Cx1tev", t_))
        for t_ in ([12, 13, 14, 15] if C_LEVEL >= 4 else []):
            c.wait(a, "Pe", ("Clog", t_))
            c.inc(a.copy(l_all[:, t_ * 8:(t_ + 1) * 8], psC[2][:, 0:8]),
                  "Ac", 1, ("Clsb", t_))
        for t_ in range(NT if C_LEVEL >= 5 else 0):
            c.wait(a, "Ve", ("Cnegm1", t_))
            ins = a.activation(gch[:, t_ * 40 + 8: t_ * 40 + 16],
                               l_all[:, t_ * 8:(t_ + 1) * 8], AF.Exp,
                               bias=gch[:, t_ * 40: t_ * 40 + 1], scale=1.0)
            c.inc(ins, "Ac", 1, ("Cge", t_))
        for t_ in range(NT if C_LEVEL >= 5 else 0):
            c.wait(a, "Pe", ("CgT", t_))
            ins = a.copy(gT[0:8, t_ * P:(t_ + 1) * P],
                         psC[2][0:8, (t_ % 2) * P:(t_ % 2) * P + P])
            c.inc(ins, "Ac", 1, ("CgTev", t_))

    def phC_ve(v, c):
        def col(t_, r):
            return stC[:, t_ * 8 + r: t_ * 8 + r + 1]
        v.memset(gate[:, :], 1.0)
        for w in range(4):
            tl = [4 * w + i for i in range(4)]
            if 'add' in C_SUB or C_SUB == 'all':
                for t_ in (tl if C_LEVEL >= 2 else []):
                    c.wait(v, "Ac", ("Copev", t_, 1))
                    c.wait(v, "Sd", ("Cxre", t_))
                    ins = v.tensor_tensor(h1[t_ % 4][:, :], h1[t_ % 4][:, :], xre[t_ % 2][:, :], ALU.add)
                    c.inc(ins, "Ve", 1, ("Cadd", t_))
            if 's1' in C_SUB or C_SUB == 'all':
                for t_ in (tl if C_LEVEL >= 2 else []):
                    c.inc(v.reduce_sum(col(t_, 0), h1[t_ % 4][:, :], axis=AX.X), "Ve", 1, ("Cs1", t_))
            if 'm' in C_SUB or C_SUB == 'all':
                for t_ in (tl if C_LEVEL >= 2 else []):
                    c.inc(v.tensor_scalar(col(t_, 1), col(t_, 0), 1.0 / D, None, ALU.mult),
                          "Ve", 1, ("Cm", t_))
            if 'var' in C_SUB or C_SUB == 'all':
                for t_ in (tl if C_LEVEL >= 2 else []):
                    c.wait(v, "Ac", ("Csq", t_))
                    c.wait(v, "Ac", ("Cmm2", t_))
                    ins = v.scalar_tensor_tensor(col(t_, 2), col(t_, 5), 1.0 / D, col(t_, 6),
                                                 ALU.mult, ALU.subtract)
                    c.inc(ins, "Ve", 1, ("Cvar", t_))
            if 'bb' in C_SUB or C_SUB == 'all':
                for t_ in (tl if C_LEVEL >= 2 else []):
                    c.wait(v, "Ac", ("Crstd", t_))
                    ins = v.scalar_tensor_tensor(col(t_, 4), col(t_, 1), -1.0, col(t_, 3),
                                                 ALU.mult, ALU.mult)
                    c.inc(ins, "Ve", 1, ("Cbb", t_))
        def g8(t_, r):
            return gch[:, t_ * 40 + r: t_ * 40 + r + 8]
        def g1(t_, r):
            return gch[:, t_ * 40 + r: t_ * 40 + r + 1]
        for t_ in range(NT if C_LEVEL >= 5 else 0):
            c.wait(v, "Ac", ("Clsb", t_))
            v.reduce_max(g1(t_, 1), l_all[:, t_ * 8:(t_ + 1) * 8], axis=AX.X)
        for t_ in range(NT if C_LEVEL >= 5 else 0):
            c.inc(v.tensor_scalar(g1(t_, 0), g1(t_, 1), -1.0, None, ALU.mult),
                  "Ve", 1, ("Cnegm1", t_))
        for t_ in range(NT if C_LEVEL >= 5 else 0):
            v.tensor_scalar(g8(t_, 16), l_all[:, t_ * 8:(t_ + 1) * 8], g1(t_, 1), None, ALU.is_equal)
        for t_ in range(NT if C_LEVEL >= 5 else 0):
            v.tensor_scalar(g8(t_, 24), g8(t_, 16), NEG, None, ALU.mult)
        for t_ in range(NT if C_LEVEL >= 5 else 0):
            v.tensor_tensor(g8(t_, 24), g8(t_, 24), l_all[:, t_ * 8:(t_ + 1) * 8], ALU.add)
        for t_ in range(NT if C_LEVEL >= 5 else 0):
            v.reduce_max(g1(t_, 2), g8(t_, 24), axis=AX.X)
        for t_ in range(NT if C_LEVEL >= 5 else 0):
            v.tensor_scalar(g8(t_, 32), l_all[:, t_ * 8:(t_ + 1) * 8], g1(t_, 2), None, ALU.is_ge)
        for t_ in range(NT if C_LEVEL >= 5 else 0):
            c.wait(v, "Ac", ("Cge", t_))
            v.tensor_tensor(g8(t_, 32), g8(t_, 32), g8(t_, 8), ALU.mult)
        for t_ in range(NT if C_LEVEL >= 5 else 0):
            v.reduce_sum(g1(t_, 3), g8(t_, 32), axis=AX.X)
        for t_ in range(NT if C_LEVEL >= 5 else 0):
            v.reciprocal(g1(t_, 4), g1(t_, 3))
        for t_ in range(NT if C_LEVEL >= 5 else 0):
            ins = v.tensor_scalar(gate[:, t_ * 9: t_ * 9 + 8], g8(t_, 32), g1(t_, 4), None, ALU.mult)
            c.inc(ins, "Ve", 1, ("Cgate", t_))

    NE = 9

    def gcol(ei):
        return 8 if ei == 0 else ei - 1

    def phD_gp(g, c):
        for tg in range(2):
            for ei in range(NE):
                k = tg * NE + ei
                for hb in range(32):
                    kk = k * 32 + hb
                    if kk >= 2:
                        pk, phb = (kk - 2) // 32, (kk - 2) % 32
                        c.wait(g, "Ac", ("Dfcev", pk, phb, 1))
                    src = wfc_in[ei, :, :, hb * P:(hb + 1) * P].rearrange("dc p f -> p dc f")
                    ins = g.dma_start(wfc_t[kk % 2][:, :].rearrange("p (dc f) -> p dc f", dc=8), src)
                    c.inc(ins, "Gd", 16, ("Dwfc", k, hb))
                for fo in range(2):
                    for hb in range(32):
                        kk = (k * 2 + fo) * 32 + hb
                        if kk >= 4:
                            pkk = kk - 4
                            pk, pfo, phb = pkk // 64, (pkk // 32) % 2, pkk % 32
                            if phb < 31:
                                c.wait(g, "Pe", ("Dpjhb", pk, pfo, phb))
                            else:
                                c.wait(g, "Pe", ("Dpj", pk, pfo, 7))
                        ins = g.dma_start(wpj_t[kk % 4][:, :],
                                          wpj_in[ei, hb, :, fo * 512:(fo + 1) * 512])
                        c.inc(ins, "Gd", 16, ("Dwpj", k, fo, hb))

    def phD_sync(g, c):
        for tg in range(2):
            for ei in range(NE):
                k = tg * NE + ei
                if k >= 2:
                    c.wait(g, "Pe", ("Dbfct", k - 2))
                c.inc(g.dma_start(bfc_st[k % 2][:, :], bfc_in[ei]), "Sd", 16, ("Dbfc", k))
            for tt in range(8):
                t_ = tg * 8 + tt
                if t_ >= 2:
                    c.wait(g, "Ve", ("Eadd", t_ - 2))
                c.inc(g.dma_start(x1re[t_ % 2][:, :], x1_sp[t_]), "Sd", 16, ("Ex1", t_))
            for tt in range(8):
                t_ = tg * 8 + tt
                c.wait(g, "Ac", ("Enorm", t_))
                c.inc(g.dma_start(y_out[t_], yst[t_ % 2][:, :]), "St", 16, ("Ey", t_))
        if not c.dry:
            g.wait_ge(sems["St"], c.counts["St"])
        elif c.record:
            g.wait_ge(SemName("St"), c.counts["St"])

    def phD_pe(t, c):
        for tg in range(2):
            for ei in range(NE):
                k = tg * NE + ei
                c.wait(t, "Sd", ("Dbfc", k))
                c.wait_if(t, "Ac", ("Dbfcev", k - 1))
                c.wait_if(t, "Ve", ("Dpjev", k - 1, 1, 4))
                c.inc(t.transpose(psD[4][:, 0:32], bfc_st[k % 2][0:32, 0:P], id_f[0:32, 0:32]),
                      "Pe", 1, ("Dbfct", k))
                for hb in range(32):
                    c.wait(t, "Gd", ("Dwfc", k, hb))
                    for tc in range(2):
                        bank = (hb % 2) * 2 + tc
                        c.wait_if(t, "Ac", ("Dfcev", k, hb - 2, tc))
                        if hb < 2:
                            c.wait_if(t, "Ve", ("Dpjev", k - 1, 1, bank))
                        ins = None
                        for dc in range(8):
                            ins = t.matmul(
                                psD[bank][:, :], wfc_t[(k * 32 + hb) % 2][:, dc * P:(dc + 1) * P],
                                X1T[:, dc * 2048 + tg * 1024 + tc * 512:
                                    dc * 2048 + tg * 1024 + (tc + 1) * 512],
                                start=(dc == 0), stop=(dc == 7))
                        c.inc(ins, "Pe", 1, ("Dfcmm", k, hb, tc))
                c.wait(t, "Ve", ("DHw", k, 31))
                for fo in range(2):
                    biased = (ei == 0)
                    for tt in range(8):
                        bank = tt
                        if fo == 0:
                            if tt < 4:
                                c.wait(t, "Ac", ("Dfcev", k, 30 + tt // 2, tt % 2))
                            elif tt == 4:
                                c.wait(t, "Ac", ("Dbfcev", k))
                            else:
                                c.wait_if(t, "Ve", ("Dpjev", k - 1, 1, tt))
                        else:
                            c.wait(t, "Ve", ("Dpjev", k, 0, tt))
                        if biased:
                            if k == 0 and tt == 0 and fo == 0:
                                c.wait(t, "Ac", ("CgTev", NT - 1))
                                c.wait(t, "Gd", ("Cbpe",))
                                c.wait(t, "Gd", ("Cbps",))
                            t.matmul(psD[bank][:, :],
                                     gT[0:8, (tg * 8 + tt) * P:(tg * 8 + tt + 1) * P],
                                     bpall[0:8, fo * 512:(fo + 1) * 512], start=True, stop=False)
                            t.matmul(psD[bank][:, :], ones_b[0:1, 0:P],
                                     bpsh[0:1, fo * 512:(fo + 1) * 512], start=False, stop=False)
                    for hb in range(32):
                        c.wait(t, "Gd", ("Dwpj", k, fo, hb))
                        ins = None
                        for tt in range(8):
                            ins = t.matmul(psD[tt][:, :],
                                           Hb[:, hb * 1024 + tt * P: hb * 1024 + (tt + 1) * P],
                                           wpj_t[((k * 2 + fo) * 32 + hb) % 4][:, :],
                                           start=((not biased) and hb == 0), stop=(hb == 31))
                            if hb == 31:
                                c.inc(ins, "Pe", 1, ("Dpj", k, fo, tt))
                        if hb < 31:
                            c.inc(ins, "Pe", 1, ("Dpjhb", k, fo, hb))

    def phD_ac(a, c):
        for tg in range(2):
            for ei in range(NE):
                k = tg * NE + ei
                c.wait(a, "Pe", ("Dbfct", k))
                c.inc(a.copy(bfc_t[k % 2][:, 0:32], psD[4][:, 0:32]), "Ac", 1, ("Dbfcev", k))
                for hb in range(32):
                    if hb >= 1:
                        c.wait(a, "Ve", ("DHw", k, hb - 1))
                    elif k >= 1:
                        c.wait(a, "Ve", ("DHw", k - 1, 31))
                    ins = None
                    for tc in range(2):
                        bank = (hb % 2) * 2 + tc
                        c.wait(a, "Pe", ("Dfcmm", k, hb, tc))
                        ins = a.activation(xf[hb % 2][:, tc * 512:(tc + 1) * 512], psD[bank][:, :],
                                           AF.Identity, bias=bfc_t[k % 2][:, hb:hb + 1])
                        c.inc(ins, "Ac", 1, ("Dfcev", k, hb, tc))
                    a.activation(sqscrD[:, :], xf[hb % 2][:, :], AF.Square)
                    ins = a.activation(wscr[hb % 2][:, :], sqscrD[:, :], AF.Identity,
                                       bias=1.0, scale=GELU_C2)
                    c.inc(ins, "Ac", 1, ("Dw", k, hb))
                    c.wait(a, "Ve", ("Dt", k, hb))
                    ins = a.activation(sgscr[hb % 2][:, :], tscr[hb % 2][:, :], AF.Sigmoid,
                                       scale=GELU_2C)
                    c.inc(ins, "Ac", 1, ("Dsg", k, hb))
            def colE(tt, r):
                return stE[:, tt * 8 + r: tt * 8 + r + 1]
            for tt in range(8):
                t_ = tg * 8 + tt
                c.wait(a, "Ve", ("Eadd", t_))
                ins = a.activation(sqscrE[:, :], routed[:, tt * D:(tt + 1) * D], AF.Square,
                                   accum_out=colE(tt, 5))
                c.inc(ins, "Ac", 1, ("Esq", t_))
            for tt in range(8):
                t_ = tg * 8 + tt
                c.wait(a, "Ve", ("Em", t_))
                c.inc(a.activation(colE(tt, 6), colE(tt, 1), AF.Square), "Ac", 1, ("Emm2", t_))
            for tt in range(8):
                t_ = tg * 8 + tt
                c.wait(a, "Ve", ("Evar", t_))
                a.activation(colE(tt, 3), colE(tt, 2), AF.Ln, bias=epst[:, 0:1], scale=1.0)
                c.inc(a.activation(colE(tt, 3), colE(tt, 3), AF.Exp, scale=-0.5),
                      "Ac", 1, ("Erstd", t_))
            for tt in range(8):
                t_ = tg * 8 + tt
                c.wait(a, "Ve", ("Ebb", t_))
                if t_ >= 2:
                    c.wait(a, "St", ("Ey", t_ - 2))
                ins = a.activation(yst[t_ % 2][:, :], routed[:, tt * D:(tt + 1) * D], AF.Identity,
                                   bias=colE(tt, 4), scale=colE(tt, 3))
                c.inc(ins, "Ac", 1, ("Enorm", t_))

    def phD_ve(v, c):
        for tg in range(2):
            for ei in range(NE):
                k = tg * NE + ei
                for hb in range(32):
                    c.wait(v, "Ac", ("Dw", k, hb))
                    ins = v.tensor_tensor(tscr[hb % 2][:, :], xf[hb % 2][:, :],
                                          wscr[hb % 2][:, :], ALU.mult)
                    c.inc(ins, "Ve", 1, ("Dt", k, hb))
                    c.wait(v, "Ac", ("Dsg", k, hb))
                    if k >= 1:
                        if hb < 31:
                            c.wait(v, "Pe", ("Dpjhb", k - 1, 1, hb))
                        else:
                            c.wait(v, "Pe", ("Dpj", k - 1, 1, 7))
                    ins = v.tensor_tensor(Hb[:, hb * 1024:(hb + 1) * 1024], xf[hb % 2][:, :],
                                          sgscr[hb % 2][:, :], ALU.mult)
                    c.inc(ins, "Ve", 1, ("DHw", k, hb))
                for fo in range(2):
                    for tt in range(8):
                        c.wait(v, "Pe", ("Dpj", k, fo, tt))
                        dst = routed[:, tt * D + fo * 512: tt * D + (fo + 1) * 512]
                        if ei == 0:
                            if tg == 1:
                                c.wait(v, "Ac", ("Enorm", tt))
                            ins = v.tensor_copy(dst, psD[tt][:, :])
                        else:
                            gc = gate[:, (tg * 8 + tt) * 9 + gcol(ei):
                                      (tg * 8 + tt) * 9 + gcol(ei) + 1]
                            ins = v.scalar_tensor_tensor(dst, psD[tt][:, :], gc, dst,
                                                         ALU.mult, ALU.add)
                        c.inc(ins, "Ve", 1, ("Dpjev", k, fo, tt))
            def colE(tt, r):
                return stE[:, tt * 8 + r: tt * 8 + r + 1]
            for tt in range(8):
                t_ = tg * 8 + tt
                c.wait(v, "Sd", ("Ex1", t_))
                ins = v.tensor_tensor(routed[:, tt * D:(tt + 1) * D],
                                      routed[:, tt * D:(tt + 1) * D], x1re[t_ % 2][:, :], ALU.add)
                c.inc(ins, "Ve", 1, ("Eadd", t_))
            for tt in range(8):
                t_ = tg * 8 + tt
                c.inc(v.reduce_sum(colE(tt, 0), routed[:, tt * D:(tt + 1) * D], axis=AX.X),
                      "Ve", 1, ("Es1", t_))
            for tt in range(8):
                t_ = tg * 8 + tt
                c.inc(v.tensor_scalar(colE(tt, 1), colE(tt, 0), 1.0 / D, None, ALU.mult),
                      "Ve", 1, ("Em", t_))
            for tt in range(8):
                t_ = tg * 8 + tt
                c.wait(v, "Ac", ("Esq", t_))
                c.wait(v, "Ac", ("Emm2", t_))
                ins = v.scalar_tensor_tensor(colE(tt, 2), colE(tt, 5), 1.0 / D, colE(tt, 6),
                                             ALU.mult, ALU.subtract)
                c.inc(ins, "Ve", 1, ("Evar", t_))
            for tt in range(8):
                t_ = tg * 8 + tt
                c.wait(v, "Ac", ("Erstd", t_))
                ins = v.scalar_tensor_tensor(colE(tt, 4), colE(tt, 1), -1.0, colE(tt, 3),
                                             ALU.mult, ALU.mult)
                c.inc(ins, "Ve", 1, ("Ebb", t_))

    passes = [("dry", True, False)]
    if SIM is not None:
        passes.append(("sim", True, True))
    else:
        passes.append(("real", False, False))
    for pname, dry, record in passes:
        c = Ctx(sems, dry, labels, record=record)
        if dry:
            FE = FakeEng()

            def blk(*fns):
                if record:
                    for eng_name, fn in fns:
                        fn(SIM.engine(eng_name), c)
                    SIM.barrier()
                else:
                    for eng_name, fn in fns:
                        fn(FE, c)
        else:
            def blk(*fns):
                with nc.Block() as block:
                    for eng_name, fn in fns:
                        getattr(block, eng_name)(lambda engine, f=fn: f(engine, c))

        blk(("sync", phA_sync), ("gpsimd", phA_gp), ("vector", phA_ve),
            ("tensor", phA_pe), ("scalar", phA_ac))
        if stage.startswith("b"):
            npair_run = int(stage[1:])
        else:
            npair_run = NPAIR
        if stage == "a":
            def dumpA_ac(a, c2):
                for i in range(NT):
                    if i >= 2:
                        c2.wait(a, "St", ("dbgAs", i - 2))
                    ins = a.copy(x_st[i % 2][:, :], XT[:, i * 1024:(i + 1) * 1024])
                    c2.inc(ins, "Ac", 1, ("dbgA", i))
            def dumpA_sync(g, c2):
                for i in range(NT):
                    c2.wait(g, "Ac", ("dbgA", i))
                    c2.inc(g.dma_start(y_out[i], x_st[i % 2][:, :]), "St", 16, ("dbgAs", i))
                if not c2.dry:
                    g.wait_ge(sems["St"], c2.counts["St"])
            blk(("scalar", dumpA_ac), ("sync", dumpA_sync))
            continue_pairs = False
        else:
            continue_pairs = True
        for p in range(npair_run if continue_pairs else 0):
            fns = []
            if p == 0:
                fns.append(("gpsimd", lambda eng, c2: phB_gp(eng, c2, 0)))
            fns += [("tensor", lambda eng, c2, p=p: phB_pe(eng, c2, p)),
                    ("scalar", lambda eng, c2, p=p: phB_ac(eng, c2, p)),
                    ("vector", lambda eng, c2, p=p: phB_ve(eng, c2, p)),
                    ("sync", lambda eng, c2, p=p: phB_sync(eng, c2, p))]
            if p + 1 < NPAIR:
                fns.append(("gpsimd", lambda eng, c2, pn=p + 1: phB_gp(eng, c2, pn)))
            blk(*fns)
        if stage.startswith("b"):
            def dumpB_gp(g, c2):
                if B_MAXQT == 0:
                    outs = [KT[:, 0:1024], KT[:, 1024:2048], KT[:, 2048:3072], KT[:, 3072:4096],
                            Qp[:, 0:1024], Qp[:, 1024:2048],
                            U1[0][:, 0:1024], U1[0][:, 1024:2048],
                            U2[0][:, 0:1024], U2[0][:, 1024:2048],
                            A1s[:, 0:1024], A1s[:, 1024:2048],
                            Vtm[:, 0:1024], Vtm[:, 1024:2048]]
                    for i, src_ap in enumerate(outs):
                        ins = g.dma_start(y_out[i], src_ap)
                        c2.inc(ins, "St", 16, ("dbgB", i, 0))
                else:
                    for p in range(npair_run):
                        c2.wait(g, "St", ("Bot_st", p))
                        for half in range(2):
                            ins = g.dma_start(y_out[2 * p + half],
                                              ot_sp[p][:, half * 1024:(half + 1) * 1024])
                            c2.inc(ins, "St", 16, ("dbgB", p, half))
                if not c2.dry:
                    g.wait_ge(sems["St"], c2.counts["St"])
            blk(("gpsimd", dumpB_gp))
        if stage in ("attn", "full"):
            blk(("sync", phC_sync), ("gpsimd", phC_gp), ("tensor", phC_pe),
                ("scalar", phC_ac), ("vector", phC_ve))
        if stage == "full":
            blk(("gpsimd", phD_gp), ("sync", phD_sync), ("tensor", phD_pe),
                ("scalar", phD_ac), ("vector", phD_ve))

    st_all.close()
    return nc



def _tables():
    j = np.arange(128, dtype=np.float64)
    w = 10000.0 ** (-(2.0 * j) / 256.0)
    pp = np.arange(SEQ, dtype=np.float64)
    ang = pp[None, :] * w[:, None]
    return np.sin(ang).astype(np.float32), np.cos(ang).astype(np.float32)


def _masks(h):
    r = np.arange(P)[:, None]
    cidx = np.arange(512)[None, :]
    dm = np.zeros((4, P, 512), np.float32)
    om = np.zeros((4, P, 512), np.float32)
    for v in range(4):
        dm[v] = np.where(cidx <= 128 * v + r, 0.0, NEG)
        if h == 0:
            om[v] = np.where(cidx < 128 * v, 0.0, NEG)
        else:
            om[v] = np.where(cidx < 128 * (v + 1), 0.0, NEG)
    return dm, om


_NC_CACHE = {}


def kernel(L0_attn_input, L0_attn_qkv, L0_attn_q_bias, L0_attn_positional,
           L0_attn_out_kernel, L0_attn_out_bias, L0_ffn_Wfc_s, L0_ffn_bfc_s,
           L0_ffn_Wproj_s, L0_ffn_bproj_s, L0_ffn_Wr, L0_ffn_Wfc, L0_ffn_bfc,
           L0_ffn_Wproj, L0_ffn_bproj):
    x = np.asarray(L0_attn_input, np.float32)
    qkv = np.ascontiguousarray(np.asarray(L0_attn_qkv, np.float32).reshape(8, P, 3, 16, 64))
    qb = np.ascontiguousarray(np.asarray(L0_attn_q_bias, np.float32))
    pos = np.ascontiguousarray(np.asarray(L0_attn_positional, np.float32).reshape(P, 2048))
    wo = np.ascontiguousarray(np.asarray(L0_attn_out_kernel, np.float32).reshape(8, P, D))
    ob = np.ascontiguousarray(np.asarray(L0_attn_out_bias, np.float32).reshape(1, D))
    wr = np.ascontiguousarray(np.asarray(L0_ffn_Wr, np.float32).reshape(8, P, 8))
    wfc = np.ascontiguousarray(np.concatenate(
        [np.asarray(L0_ffn_Wfc_s, np.float32).reshape(1, 1024, 4096),
         np.asarray(L0_ffn_Wfc, np.float32)], 0).reshape(9, 8, P, 4096))
    wpj = np.ascontiguousarray(np.concatenate(
        [np.asarray(L0_ffn_Wproj_s, np.float32).reshape(1, 4096, 1024),
         np.asarray(L0_ffn_Wproj, np.float32)], 0).reshape(9, 32, P, D))
    bfc = np.ascontiguousarray(np.concatenate(
        [np.asarray(L0_ffn_bfc_s, np.float32).reshape(1, 4096),
         np.asarray(L0_ffn_bfc, np.float32)], 0).reshape(9, 32, P))
    bpe = np.ascontiguousarray(np.asarray(L0_ffn_bproj, np.float32))
    bps = np.ascontiguousarray(np.asarray(L0_ffn_bproj_s, np.float32).reshape(1, D))
    sqt, cqt = _tables()
    ident = np.eye(P, dtype=np.float32)

    if STAGE not in _NC_CACHE:
        _NC_CACHE[STAGE] = build_nc(STAGE)
    nc = _NC_CACHE[STAGE]

    in_maps = []
    metas = []
    for core in range(8):
        b, h = core // 2, core % 2
        perm_tiles = [2 * t + h for t in range(NT)] + [2 * t + (1 - h) for t in range(NT)]
        tok_perm = np.concatenate([np.arange(128 * g, 128 * g + 128) for g in perm_tiles])
        xb_ = np.ascontiguousarray(x[b][tok_perm].reshape(32, P, D))
        dmm, omm = _masks(h)
        in_maps.append(dict(
            x_in=xb_, qkv_in=qkv, qb_in=qb, pos_in=pos, wo_in=wo, ob_in=ob,
            wr_in=wr, wfc_in=wfc, wpj_in=wpj, bfc_in=bfc, bpe_in=bpe, bps_in=bps,
            sq_in=np.ascontiguousarray(sqt[:, tok_perm]),
            cq_in=np.ascontiguousarray(cqt[:, tok_perm]),
            dm_in=dmm, om_in=omm, id_in=ident,
        ))
        metas.append((b, h))

    res = run_bass_kernel_spmd(nc, in_maps, core_ids=list(range(8)))

    out = np.empty((4, SEQ, D), np.float32)
    for core in range(8):
        b, h = metas[core]
        y = np.asarray(res.results[core]["y_out"], np.float32)
        for t in range(NT):
            g = 2 * t + h
            out[b, 128 * g:128 * g + 128, :] = y[t]
    return out


if __name__ == "__main__":
    nc = build_nc(STAGE)
    print("built instructions:", nc.n_instructions())


# version 03
# speedup vs baseline: 1.0020x; 1.0020x over previous
import math
import numpy as np
from contextlib import ExitStack

import concourse.bass as bass
import concourse.mybir as mybir
from concourse.bass_utils import run_bass_kernel_spmd

F32 = mybir.dt.float32
BF16 = mybir.dt.bfloat16
AF = mybir.ActivationFunctionType
ALU = mybir.AluOpType
AX = mybir.AxisListType

P = 128
D = 1024
SEQ = 4096
NT = 16
NPAIR = 8
NEG = -1.0e9
EPS = 1e-5

S_QKV = (3.0 * 1024 * 64 * 16) ** (-0.25)
SC_Q = S_QKV / 8.0
C_POS = math.sqrt(2.0) / 16.0
SC_WO = (1024.0 * 1024.0) ** (-0.25)
GELU_C2 = 0.044715
GELU_2C = 2.0 * math.sqrt(2.0 / math.pi)

STAGE = "full"
WAC = None
SIM = None
B_MAXQT = None
B_PRE = "kvqau"
C_LEVEL = 5
C_SUB = 'all'


def kch_of(t):
    return t // 4 + 1


def make_sched(ntb=NT):
    sched = []
    sc = gt = qc = 0
    for hp in range(2):
        for qt in range(ntb):
            n1 = kch_of(qt)
            chunks = []
            for side_i, side in enumerate(("own", "oth")):
                for j in range(n1):
                    base = j * 512 + side_i * 2048
                    masked = (j == n1 - 1)
                    chunks.append(dict(sc=sc, side=side, j=j, base=base,
                                       masked=masked, bank=sc % 2))
                    sc += 1
            groups = []
            for gi in range(2 * n1):
                groups.append(dict(gt=gt, gi=gi, base=chunks[gi]["base"],
                                   bank=gt % 2))
                gt += 1
            sched.append(dict(hp=hp, qt=qt, qc=qc, chunks=chunks, groups=groups,
                              obank=2 + qc % 2, nch=2 * n1))
            qc += 1
    return sched


SCHED = make_sched()


class FakeInstr:
    def then_inc(self, *a, **k):
        return self


class FakeEng:
    def __getattr__(self, name):
        def f(*a, **k):
            return FakeInstr()
        return f


class SemName:
    def __init__(self, name):
        self.name = name


class Ctx:
    def __init__(self, sems, dry, labels, record=False):
        self.sems = sems
        self.dry = dry
        self.labels = labels
        self.record = record
        self.counts = {k: 0 for k in sems}

    def inc(self, instr, s, n=1, label=None):
        self.counts[s] += n
        if label is not None:
            key = (s,) + tuple(label)
            if self.dry:
                self.labels[key] = self.counts[s]
            else:
                assert self.labels.get(key) == self.counts[s], (
                    key, self.labels.get(key), self.counts[s])
        if not self.dry:
            instr.then_inc(self.sems[s], n)
        elif self.record:
            instr.then_inc(SemName(s), n)

    def alias(self, s, label):
        key = (s,) + tuple(label)
        if self.dry:
            self.labels[key] = self.counts[s]
        else:
            assert self.labels.get(key) == self.counts[s]

    def wait(self, eng, s, label):
        key = (s,) + tuple(label)
        if not self.dry or self.record:
            eng.wait_ge(SemName(s), self.labels[key]) if self.record else                 eng.wait_ge(self.sems[s], self.labels[key])

    def wait_if(self, eng, s, label):
        key = (s,) + tuple(label)
        if key in self.labels and (not self.dry or self.record):
            eng.wait_ge(SemName(s), self.labels[key]) if self.record else                 eng.wait_ge(self.sems[s], self.labels[key])


def build_nc(stage=STAGE):
    nc = bass.Bass("TRN2", target_bir_lowering=False)

    x_in = nc.dram_tensor("x_in", [32, P, D], F32, kind="ExternalInput").ap()
    qkv_in = nc.dram_tensor("qkv_in", [8, P, 3, 16, 64], F32, kind="ExternalInput").ap()
    qb_in = nc.dram_tensor("qb_in", [16, 64], F32, kind="ExternalInput").ap()
    pos_in = nc.dram_tensor("pos_in", [P, 2048], F32, kind="ExternalInput").ap()
    wo_in = nc.dram_tensor("wo_in", [8, P, D], F32, kind="ExternalInput").ap()
    ob_in = nc.dram_tensor("ob_in", [1, D], F32, kind="ExternalInput").ap()
    wr_in = nc.dram_tensor("wr_in", [8, P, 8], F32, kind="ExternalInput").ap()
    wfc_in = nc.dram_tensor("wfc_in", [9, 8, P, 4096], F32, kind="ExternalInput").ap()
    wpj_in = nc.dram_tensor("wpj_in", [9, 32, P, D], F32, kind="ExternalInput").ap()
    bfc_in = nc.dram_tensor("bfc_in", [9, 32, P], F32, kind="ExternalInput").ap()
    bpe_in = nc.dram_tensor("bpe_in", [8, D], F32, kind="ExternalInput").ap()
    bps_in = nc.dram_tensor("bps_in", [1, D], F32, kind="ExternalInput").ap()
    sq_in = nc.dram_tensor("sq_in", [P, SEQ], F32, kind="ExternalInput").ap()
    cq_in = nc.dram_tensor("cq_in", [P, SEQ], F32, kind="ExternalInput").ap()
    dm_in = nc.dram_tensor("dm_in", [4, P, 512], F32, kind="ExternalInput").ap()
    om_in = nc.dram_tensor("om_in", [4, P, 512], F32, kind="ExternalInput").ap()
    id_in = nc.dram_tensor("id_in", [P, P], F32, kind="ExternalInput").ap()

    y_out = nc.dram_tensor("y_out", [NT, P, D], F32, kind="ExternalOutput").ap()
    x1_sp = nc.dram_tensor("x1_sp", [NT, P, D], F32, kind="Internal").ap()
    ot_sp = nc.dram_tensor("ot_sp", [NPAIR, P, 2048], BF16, kind="Internal").ap()

    st_all = ExitStack()
    e = st_all.enter_context
    sems = {k: e(nc.semaphore(k)) for k in ("Sd", "Gd", "Pe", "Ac", "Ve", "St")}

    id_f = e(nc.sbuf_tensor("id_f", [P, P], F32))
    id_b = e(nc.sbuf_tensor("id_b", [P, P], BF16))
    ones_f = e(nc.sbuf_tensor("ones_f", [1, P], F32))
    ones_b = e(nc.sbuf_tensor("ones_b", [1, P], BF16))
    qb2 = e(nc.sbuf_tensor("qb2", [P, 8], F32))
    qbT = e(nc.sbuf_tensor("qbT", [64, 16], F32))
    ob32 = e(nc.sbuf_tensor("ob32", [1, D], F32))
    epst = e(nc.sbuf_tensor("epst", [P, 1], F32))
    PeT_e = e(nc.sbuf_tensor("PeT_e", [P, 8 * P], BF16))
    PeT_o = e(nc.sbuf_tensor("PeT_o", [P, 8 * P], BF16))

    st_ab = ExitStack()
    ea = st_ab.enter_context
    XT = ea(nc.sbuf_tensor("XT", [P, 8 * SEQ], BF16))
    sq_b = ea(nc.sbuf_tensor("sq_b", [P, SEQ], BF16))
    cq_b = ea(nc.sbuf_tensor("cq_b", [P, SEQ], BF16))
    dm_sb = ea(nc.sbuf_tensor("dm_sb", [P, 4 * 512], F32))
    om_sb = ea(nc.sbuf_tensor("om_sb", [P, 4 * 512], F32))

    st_a = ExitStack()
    x_st = [st_a.enter_context(nc.sbuf_tensor(f"x_st{i}", [P, D], F32)) for i in range(2)]
    xb_st = [st_a.enter_context(nc.sbuf_tensor(f"xb_st{i}", [P, D], BF16)) for i in range(2)]
    pos_st = st_a.enter_context(nc.sbuf_tensor("pos_st", [P, 2048], F32))
    qb_st = st_a.enter_context(nc.sbuf_tensor("qb_st", [16, 64], F32))
    ob_st = st_a.enter_context(nc.sbuf_tensor("ob_st", [1, D], F32))
    psA_T = [st_a.enter_context(nc.psum_tensor(f"psA_T{i}", [P, 1024], BF16)) for i in range(2)]
    psA_G = [st_a.enter_context(nc.psum_tensor(f"psA_G{i}", [P, 512], F32)) for i in range(2)]
    st_a.close()

    st_b = ExitStack()
    eb = st_b.enter_context
    KT = eb(nc.sbuf_tensor("KT", [P, SEQ], BF16))
    Vtm = eb(nc.sbuf_tensor("Vtm", [P, 2 * 32 * 65], BF16))
    Qp = eb(nc.sbuf_tensor("Qp", [P, 2048], BF16))
    A1s = eb(nc.sbuf_tensor("A1s", [P, 2048], BF16))
    A2s = eb(nc.sbuf_tensor("A2s", [P, 2048], BF16))
    U1 = [eb(nc.sbuf_tensor(f"U1_{i}", [P, 2048], BF16)) for i in range(2)]
    U2 = [eb(nc.sbuf_tensor(f"U2_{i}", [P, 2048], BF16)) for i in range(2)]
    Utmp = eb(nc.sbuf_tensor("Utmp", [P, 2048], BF16))
    Ssb = eb(nc.sbuf_tensor("Ssb", [P, SEQ], F32))
    Asb = [eb(nc.sbuf_tensor(f"Asb{i}", [P, SEQ], BF16)) for i in range(2)]
    ATr = [eb(nc.sbuf_tensor(f"ATr{i}", [P, 512], BF16)) for i in range(2)]
    mxt = eb(nc.sbuf_tensor("mxt", [P, 24], F32))
    lrow = eb(nc.sbuf_tensor("lrow", [1, P], F32))
    rrow = eb(nc.sbuf_tensor("rrow", [1, P], F32))
    rbb = eb(nc.sbuf_tensor("rbb", [64, P], F32))
    otp = eb(nc.sbuf_tensor("otp", [P, 2048], BF16))
    wq_t = [eb(nc.sbuf_tensor(f"wq_t{i}", [P, 1024], BF16)) for i in range(2)]
    wk_t = [eb(nc.sbuf_tensor(f"wk_t{i}", [P, 1024], BF16)) for i in range(2)]
    wv_t = [eb(nc.sbuf_tensor(f"wv_t{i}", [P, 1024], BF16)) for i in range(2)]
    psB = [eb(nc.psum_tensor(f"psB{i}", [P, 512], F32)) for i in range(6)]
    psB_T = [eb(nc.psum_tensor(f"psB_T{i}", [P, 512], BF16)) for i in range(2)]
    st_b.close()
    st_ab.close()

    st_cd = ExitStack()
    ec = st_cd.enter_context
    X1T = ec(nc.sbuf_tensor("X1T", [P, 8 * 2048], BF16))
    gate = ec(nc.sbuf_tensor("gate", [P, NT * 9], F32))
    gT = ec(nc.sbuf_tensor("gT", [8, 2048], BF16))

    st_c = ExitStack()
    cc = st_c.enter_context
    ot_ld = [cc(nc.sbuf_tensor(f"ot_ld{i}", [P, 2048], BF16)) for i in range(NPAIR)]
    wo_b = cc(nc.sbuf_tensor("wo_b", [P, 8 * D], BF16))
    wr_b = cc(nc.sbuf_tensor("wr_b", [P, 64], BF16))
    h1 = [cc(nc.sbuf_tensor(f"h1_{i}", [P, D], F32)) for i in range(4)]
    x1scr = [cc(nc.sbuf_tensor(f"x1scr{i}", [P, D], F32)) for i in range(2)]
    x1b = [cc(nc.sbuf_tensor(f"x1b{i}", [P, D], BF16)) for i in range(4)]
    sqscr = cc(nc.sbuf_tensor("sqscr", [P, D], F32))
    xre = [cc(nc.sbuf_tensor(f"xre{i}", [P, D], F32)) for i in range(2)]
    l_all = cc(nc.sbuf_tensor("l_all", [P, NT * 8], F32))
    stC = cc(nc.sbuf_tensor("stC", [P, NT * 8], F32))
    gch = cc(nc.sbuf_tensor("gch", [P, NT * 40], F32))
    psC = [cc(nc.psum_tensor(f"psC{i}", [P, 512], F32)) for i in range(3)]
    psC_T = [cc(nc.psum_tensor(f"psC_T{i}", [P, 1024], BF16)) for i in range(4)]
    st_c.close()

    st_d = ExitStack()
    dd = st_d.enter_context
    Hb = dd(nc.sbuf_tensor("Hb", [P, 32 * 1024], BF16))
    routed = dd(nc.sbuf_tensor("routed", [P, 8 * D], F32))
    xf = [dd(nc.sbuf_tensor(f"xf{i}", [P, D], F32)) for i in range(2)]
    wscr = [dd(nc.sbuf_tensor(f"wscr{i}", [P, D], F32)) for i in range(2)]
    tscr = [dd(nc.sbuf_tensor(f"tscr{i}", [P, D], F32)) for i in range(2)]
    sgscr = [dd(nc.sbuf_tensor(f"sgscr{i}", [P, D], F32)) for i in range(2)]
    wfc_t = [dd(nc.sbuf_tensor(f"wfc_t{i}", [P, 1024], BF16)) for i in range(4)]
    wpj_t = [dd(nc.sbuf_tensor(f"wpj_t{i}", [P, 512], BF16)) for i in range(4)]
    bfc_t = [dd(nc.sbuf_tensor(f"bfc_t{i}", [P, 32], F32)) for i in range(2)]
    bfc_st = [dd(nc.sbuf_tensor(f"bfc_st{i}", [32, P], F32)) for i in range(2)]
    bpall = dd(nc.sbuf_tensor("bpall", [8, D], BF16))
    bpsh = dd(nc.sbuf_tensor("bpsh", [1, D], BF16))
    x1re = [dd(nc.sbuf_tensor(f"x1re{i}", [P, D], F32)) for i in range(2)]
    yst = [dd(nc.sbuf_tensor(f"yst{i}", [P, D], F32)) for i in range(2)]
    stE = dd(nc.sbuf_tensor("stE", [P, 8 * 8], F32))
    psD = [dd(nc.psum_tensor(f"psD{i}", [P, 512], F32)) for i in range(8)]
    st_d.close()
    st_cd.close()

    labels = {}
    ntb = NT if B_MAXQT is None else B_MAXQT
    SCHEDL = make_sched(ntb)

    def phA_sync(g, c):
        c.inc(g.dma_start(id_f[:, :], id_in), "Sd", 16, ("idf",))
        c.inc(g.dma_start(dm_sb[:, :].rearrange("p (v f) -> p v f", v=4),
                          dm_in.rearrange("v p f -> p v f")), "Sd", 16, ("dm",))
        c.inc(g.dma_start(om_sb[:, :].rearrange("p (v f) -> p v f", v=4),
                          om_in.rearrange("v p f -> p v f")), "Sd", 16, ("om",))
        c.inc(g.dma_start(pos_st[:, :], pos_in), "Sd", 16, ("pos",))
        c.inc(g.dma_start(qb_st[:, :], qb_in), "Sd", 16, ("qb",))
        c.inc(g.dma_start(ob_st[:, :], ob_in), "Sd", 16, ("ob",))
        for i in range(32):
            if i >= 2:
                c.wait(g, "Ve", ("Acast", i - 2))
            c.inc(g.dma_start(x_st[i % 2][:, :], x_in[i]), "Sd", 16, ("Axd", i))

    def phA_gp(g, c):
        c.inc(g.dma_start(id_b[:, :], id_in), "Gd", 16, ("idb",))
        c.inc(g.dma_start(sq_b[:, :], sq_in), "Gd", 16, ("sqb",))
        c.inc(g.dma_start(cq_b[:, :], cq_in), "Gd", 16, ("cqb",))

    def phA_ve(v, c):
        v.memset(ones_f[:, :], 1.0)
        v.memset(epst[:, :], EPS)
        c.inc(v.memset(ones_b[:, :], 1.0), "Ve", 1, ("Aones",))
        for i in range(32):
            c.wait(v, "Sd", ("Axd", i))
            if i >= 2:
                c.wait(v, "Pe", ("Atr", i - 2))
            c.inc(v.tensor_copy(xb_st[i % 2][:, :], x_st[i % 2][:, :]), "Ve", 1, ("Acast", i))

    def phA_pe(t, c):
        c.wait(t, "Gd", ("idb",))
        for i in range(32):
            c.wait(t, "Ve", ("Acast", i))
            if i >= 2:
                c.wait(t, "Ac", ("Aev", i - 2))
            ins = None
            for dc in range(8):
                ins = t.transpose(psA_T[i % 2][:, dc * P:(dc + 1) * P],
                                  xb_st[i % 2][:, dc * P:(dc + 1) * P], id_b[:, :])
            c.inc(ins, "Pe", 1, ("Atr", i))
        c.wait(t, "Sd", ("pos",))
        c.wait(t, "Sd", ("idf",))
        for par in range(2):
            for n in range(16):
                j = par * 16 + n
                c.wait_if(t, "Ac", ("Apev", j - 2))
                ins = t.transpose(psA_G[j % 2][0:64, 0:P],
                                  pos_st[:, par * 1024 + n * 64: par * 1024 + (n + 1) * 64],
                                  id_f[:, :])
                c.inc(ins, "Pe", 1, ("Aptr", j))
        c.wait(t, "Sd", ("qb",))
        c.wait(t, "Ac", ("Apev", 30))
        c.inc(t.transpose(psA_G[0][0:64, 0:16], qb_st[0:16, 0:64], id_f[0:16, 0:16]),
              "Pe", 1, ("Aqbt",))

    def phA_ac(a, c):
        for i in range(32):
            c.wait(a, "Pe", ("Atr", i))
            ins = a.copy(
                XT[:, :].rearrange("p (dc f) -> p dc f", dc=8)[:, :, i * P:(i + 1) * P],
                psA_T[i % 2][:, :].rearrange("p (dc f) -> p dc f", dc=8))
            c.inc(ins, "Ac", 1, ("Aev", i))
        for par in range(2):
            for n in range(16):
                j = par * 16 + n
                c.wait(a, "Pe", ("Aptr", j))
                dst = PeT_e if par == 0 else PeT_o
                pp, hp = n // 2, n % 2
                ins = a.activation(dst[64 * hp:64 * hp + 64, pp * P:(pp + 1) * P],
                                   psA_G[j % 2][0:64, 0:P], AF.Identity, scale=C_POS)
                c.inc(ins, "Ac", 1, ("Apev", j))
        c.wait(a, "Pe", ("Aqbt",))
        a.activation(qbT[0:64, 0:16], psA_G[0][0:64, 0:16], AF.Identity, scale=0.125)
        for n in range(16):
            pp, hp = n // 2, n % 2
            a.copy(qb2[64 * hp:64 * hp + 64, pp:pp + 1], qbT[0:64, n:n + 1])
        c.wait(a, "Sd", ("ob",))
        c.inc(a.mul(ob32[0:1, :], ob_st[0:1, :], 32.0), "Ac", 1, ("Aconsts",))

    def phB_gp(g, c, p):
        buf = p % 2
        for Aidx, wt in ((0, wq_t), (1, wk_t), (2, wv_t)):
            if p >= 2:
                c.wait(g, "Pe", ("Bwdone", p - 2))
            src = qkv_in[:, :, Aidx, 2 * p:2 * p + 2, :].rearrange("dc p n h -> p dc (n h)")
            ins = g.dma_start(wt[buf][:, :].rearrange("p (dc f) -> p dc f", dc=8), src)
            c.inc(ins, "Gd", 16, ("Bw", p, Aidx))

    def phB_pe(t, c, p):
        buf = p % 2
        for Aidx in range(3):
            c.wait(t, "Gd", ("Bw", p, Aidx))
        if p == 0:
            c.wait(t, "Gd", ("sqb",))
            c.wait(t, "Gd", ("cqb",))
            c.wait(t, "Ve", ("Bvmemset",))
            c.wait(t, "Ac", ("Aconsts",))
            c.wait(t, "Ve", ("Aones",))
        for tch in range(8 if "k" in B_PRE else 0):
            c.wait_if(t, "Ac", ("Bkev", p, tch - 2))
            ins = None
            for dc in range(8):
                ins = t.matmul(psB[4 + (tch % 2)][:, :], wk_t[buf][:, dc * P:(dc + 1) * P],
                               XT[:, dc * SEQ + tch * 512: dc * SEQ + (tch + 1) * 512],
                               start=(dc == 0), stop=(dc == 7))
            c.inc(ins, "Pe", 1, ("Bkmm", p, tch))
        for grp in range(8 if "v" in B_PRE else 0):
            if grp < 2:
                c.wait_if(t, "Ac", ("Bkev", p, 6 + grp))
            c.wait_if(t, "Ve", ("Bvev", p, grp - 2))
            ins = None
            for ktl in range(4):
                kt = grp * 4 + ktl
                for dc in range(8):
                    ins = t.matmul(psB[4 + (grp % 2)][:, ktl * P:(ktl + 1) * P],
                                   XT[:, dc * SEQ + kt * P: dc * SEQ + (kt + 1) * P],
                                   wv_t[buf][:, dc * P:(dc + 1) * P],
                                   start=(dc == 0), stop=(dc == 7))
            c.inc(ins, "Pe", 1, ("Bvmm", p, grp))
        for tcg in range(4 if "q" in B_PRE else 0):
            if tcg < 2:
                c.wait_if(t, "Ve", ("Bvev", p, 6 + tcg))
            c.wait_if(t, "Ac", ("Bqev", p, tcg - 2))
            ins = None
            for dc in range(8):
                ins = t.matmul(psB[4 + (tcg % 2)][:, :], wq_t[buf][:, dc * P:(dc + 1) * P],
                               XT[:, dc * SEQ + tcg * 512: dc * SEQ + (tcg + 1) * 512],
                               start=(dc == 0), stop=(dc == 7))
            c.inc(ins, "Pe", 1, ("Bqmm", p, tcg))
        c.alias("Pe", ("Bwdone", p))

        def scores_emit(rec):
            hp, qt = rec["hp"], rec["qt"]
            for ch in rec["chunks"]:
                sc, bank = ch["sc"], ch["bank"]
                c.wait_if(t, "Ac", ("BsevA", p, sc - 2))
                c.wait_if(t, "Ve", ("BsevV", p, sc - 2))
                c.wait_if(t, "Ve", ("Bmax", p, sc - 2))
                t.matmul(psB[bank][:, :],
                         Qp[64 * hp:64 * hp + 64, qt * P:(qt + 1) * P],
                         KT[64 * hp:64 * hp + 64, ch["base"]:ch["base"] + 512],
                         start=True, stop=False)
                t.matmul(psB[bank][:, :], U1[hp][:, qt * P:(qt + 1) * P],
                         cq_b[:, ch["base"]:ch["base"] + 512], start=False, stop=False)
                ins = t.matmul(psB[bank][:, :], U2[hp][:, qt * P:(qt + 1) * P],
                               sq_b[:, ch["base"]:ch["base"] + 512], start=False, stop=True)
                c.inc(ins, "Pe", 1, ("Bsmm", p, sc))

        def pv_group(rec, k, first, last):
            gr = rec["groups"][k]
            hp, obank = rec["hp"], rec["obank"]
            c.wait(t, "Ac", ("Batev", p, gr["gt"]))
            if first:
                c.wait_if(t, "Ve", ("Botev", p, rec["qc"] - 2))
            ins = None
            for i4 in range(4):
                ktile = gr["base"] // P + i4
                ins = t.matmul(psB[obank][0:65, 0:P],
                               Vtm[:, hp * 2080 + ktile * 65: hp * 2080 + ktile * 65 + 65],
                               ATr[gr["bank"]][:, i4 * P:(i4 + 1) * P],
                               start=(first and i4 == 0), stop=(last and i4 == 3))
            if last:
                c.inc(ins, "Pe", 1, ("Bpv", p, rec["qc"]))

        def pv_emit(rec):
            qt, qc = rec["qt"], rec["qc"]
            c.wait(t, "Ac", ("Bexp", p, qc))
            groups = rec["groups"]
            for k, gr in enumerate(groups):
                if gr["gt"] >= 2:
                    c.wait_if(t, "Ac", ("Batev", p, gr["gt"] - 2))
                ins = None
                for i4 in range(4):
                    ins = t.transpose(psB_T[gr["bank"]][:, i4 * P:(i4 + 1) * P],
                                      Asb[qt % 2][:, gr["base"] + i4 * P: gr["base"] + (i4 + 1) * P],
                                      id_b[:, :])
                c.inc(ins, "Pe", 1, ("Btr", p, gr["gt"]))
                if k >= 1:
                    pv_group(rec, k - 1, first=(k - 1 == 0), last=False)
            pv_group(rec, len(groups) - 1, first=(len(groups) == 1), last=True)
            c.wait(t, "Ve", ("Brrow", p, qc))
            c.wait_if(t, "Ac", ("Brbg", p, qc - 1))
            ins = t.matmul(psB[4][0:64, 0:P], ones_f[0:1, 0:64], rrow[0:1, 0:P],
                           start=True, stop=True)
            c.inc(ins, "Pe", 1, ("Bbc", p, qc))

        for hp in range(2 if "a" in B_PRE else 0):
            c.wait_if(t, "Ac", ("Bqev", p, 3))
            if hp == 1:
                c.wait_if(t, "Ve", ("BUdone", p, 0))
                c.wait_if(t, "Ac", ("Brbg", p, ntb - 1))
            for par, PeTx in ((0, PeT_e), (1, PeT_o)):
                for tcg in range(4):
                    j = par * 4 + tcg
                    if hp == 1 and j < 2:
                        c.wait_if(t, "Ac", ("Baev", p, 0, 6 + j))
                    c.wait_if(t, "Ac", ("Baev", p, hp, j - 2))
                    ins = t.matmul(psB[4 + (j % 2)][:, :],
                                   PeTx[64 * hp:64 * hp + 64, p * P:(p + 1) * P],
                                   Qp[64 * hp:64 * hp + 64, tcg * 512:(tcg + 1) * 512],
                                   start=True, stop=True)
                    c.inc(ins, "Pe", 1, ("Bamm", p, hp, j))
            c.wait_if(t, "Ac", ("Bkev", p, 7))
            c.wait_if(t, "Ve", ("BUdone", p, hp))
            if hp == 0:
                c.wait_if(t, "Ve", ("Bvev", p, 7))

            recs = [r for r in SCHEDL if r["hp"] == hp]
            if recs:
                scores_emit(recs[0])
                for i, rec in enumerate(recs):
                    if i + 1 < len(recs):
                        scores_emit(recs[i + 1])
                    pv_emit(rec)

    def phB_ac(a, c, p):
        for tch in range(8 if "k" in B_PRE else 0):
            c.wait(a, "Pe", ("Bkmm", p, tch))
            ins = a.activation(KT[:, tch * 512:(tch + 1) * 512], psB[4 + (tch % 2)][:, :],
                               AF.Identity, scale=S_QKV)
            c.inc(ins, "Ac", 1, ("Bkev", p, tch))
        for tcg in range(4 if "q" in B_PRE else 0):
            c.wait(a, "Pe", ("Bqmm", p, tcg))
            ins = a.activation(Qp[:, tcg * 512:(tcg + 1) * 512], psB[4 + (tcg % 2)][:, :],
                               AF.Identity, bias=qb2[:, p:p + 1], scale=SC_Q)
            c.inc(ins, "Ac", 1, ("Bqev", p, tcg))
        def ac_tail(rec):
            qc = rec["qc"]
            for gr in rec["groups"]:
                c.wait(a, "Pe", ("Btr", p, gr["gt"]))
                c.inc(a.copy(ATr[gr["bank"]][:, :], psB_T[gr["bank"]][:, :]),
                      "Ac", 1, ("Batev", p, gr["gt"]))
            c.wait(a, "Pe", ("Bpv", p, qc))
            ins = a.copy(lrow[0:1, 0:P], psB[rec["obank"]][64:65, 0:P])
            c.inc(ins, "Ac", 1, ("Blrow", p, qc))
            c.wait(a, "Pe", ("Bbc", p, qc))
            ins = a.copy(rbb[0:64, 0:P], psB[4][0:64, 0:P])
            c.inc(ins, "Ac", 1, ("Brbg", p, qc))

        for hp in range(2 if "a" in B_PRE else 0):
            for j in range(8):
                c.wait(a, "Pe", ("Bamm", p, hp, j))
                if hp == 1 and j == 0:
                    c.wait_if(a, "Ve", ("BUdone", p, 0))
                dst = (A1s if j < 4 else A2s)[:, (j % 4) * 512:(j % 4 + 1) * 512]
                c.inc(a.copy(dst, psB[4 + (j % 2)][:, :]), "Ac", 1, ("Baev", p, hp, j))
            recs = [r for r in SCHEDL if r["hp"] == hp]
            for rec in recs:
                qt, qc = rec["qt"], rec["qc"]
                for ch in rec["chunks"]:
                    if ch["masked"]:
                        continue
                    c.wait(a, "Pe", ("Bsmm", p, ch["sc"]))
                    ins = a.copy(Ssb[:, ch["base"]:ch["base"] + 512], psB[ch["bank"]][:, :])
                    c.inc(ins, "Ac", 1, ("BsevA", p, ch["sc"]))
                c.wait(a, "Ve", ("Bnegmax", p, qc))
                if qc >= 2:
                    c.wait_if(a, "Pe", ("Btr", p, SCHEDL[qc - 2]["groups"][-1]["gt"]))
                w = kch_of(qt) * 512
                a.activation(Asb[qt % 2][:, 0:w], Ssb[:, 0:w], AF.Exp,
                             bias=mxt[:, qt:qt + 1], scale=1.0)
                ins = a.activation(Asb[qt % 2][:, 2048:2048 + w], Ssb[:, 2048:2048 + w],
                                   AF.Exp, bias=mxt[:, qt:qt + 1], scale=1.0)
                c.inc(ins, "Ac", 1, ("Bexp", p, qc))
                if qc > (0 if hp == 0 else ntb):
                    ac_tail(SCHEDL[qc - 1])
            if recs:
                ac_tail(recs[-1])

    def phB_ve(v, c, p):
        if p == 0:
            c.inc(v.memset(Vtm[:, :], 1.0), "Ve", 1, ("Bvmemset",))
        for grp in range(8 if "v" in B_PRE else 0):
            c.wait(v, "Pe", ("Bvmm", p, grp))
            ins = None
            for hp in range(2):
                src = psB[4 + (grp % 2)][:, :].rearrange("p (i f) -> p i f", i=4)[:, :, hp * 64:hp * 64 + 64]
                dstc = Vtm[:, hp * 2080 + grp * 4 * 65: hp * 2080 + (grp * 4 + 4) * 65]
                dst = dstc.rearrange("p (i f) -> p i f", i=4)[:, :, 0:64]
                ins = v.tensor_scalar(dst, src, S_QKV, None, ALU.mult)
            c.inc(ins, "Ve", 1, ("Bvev", p, grp))
        for hp in range(2 if "u" in B_PRE else 0):
            c.wait(v, "Ac", ("Baev", p, hp, 7))
            for tcg in range(4):
                v.tensor_tensor(U1[hp][:, tcg * 512:(tcg + 1) * 512],
                                A1s[:, tcg * 512:(tcg + 1) * 512],
                                sq_b[:, tcg * 512:(tcg + 1) * 512], ALU.mult)
            for tcg in range(4):
                v.tensor_tensor(Utmp[:, tcg * 512:(tcg + 1) * 512],
                                A2s[:, tcg * 512:(tcg + 1) * 512],
                                cq_b[:, tcg * 512:(tcg + 1) * 512], ALU.mult)
            for tcg in range(4):
                v.tensor_tensor(U1[hp][:, tcg * 512:(tcg + 1) * 512],
                                U1[hp][:, tcg * 512:(tcg + 1) * 512],
                                Utmp[:, tcg * 512:(tcg + 1) * 512], ALU.add)
            for tcg in range(4):
                v.tensor_tensor(U2[hp][:, tcg * 512:(tcg + 1) * 512],
                                A2s[:, tcg * 512:(tcg + 1) * 512],
                                sq_b[:, tcg * 512:(tcg + 1) * 512], ALU.mult)
            for tcg in range(4):
                v.tensor_tensor(Utmp[:, tcg * 512:(tcg + 1) * 512],
                                A1s[:, tcg * 512:(tcg + 1) * 512],
                                cq_b[:, tcg * 512:(tcg + 1) * 512], ALU.mult)
            ins = None
            for tcg in range(4):
                ins = v.tensor_tensor(U2[hp][:, tcg * 512:(tcg + 1) * 512],
                                      U2[hp][:, tcg * 512:(tcg + 1) * 512],
                                      Utmp[:, tcg * 512:(tcg + 1) * 512], ALU.subtract)
            c.inc(ins, "Ve", 1, ("BUdone", p, hp))

            def ve_tail(rec):
                qc2, qt2, hp2 = rec["qc"], rec["qt"], rec["hp"]
                c.wait(v, "Ac", ("Blrow", p, qc2))
                c.inc(v.reciprocal(rrow[0:1, 0:P], lrow[0:1, 0:P]), "Ve", 1, ("Brrow", p, qc2))
                c.wait(v, "Ac", ("Brbg", p, qc2))
                if p >= 1 and qc2 == 0:
                    c.wait(v, "St", ("Bot_st", p - 1))
                ins2 = v.tensor_tensor(otp[64 * hp2:64 * hp2 + 64, qt2 * P:(qt2 + 1) * P],
                                       psB[rec["obank"]][0:64, 0:P], rbb[0:64, 0:P], ALU.mult)
                c.inc(ins2, "Ve", 1, ("Botev", p, qc2))

            recs = [r for r in SCHEDL if r["hp"] == hp]
            for rec in recs:
                qt, qc = rec["qt"], rec["qc"]
                for ci, ch in enumerate(rec["chunks"]):
                    c.wait(v, "Pe", ("Bsmm", p, ch["sc"]))
                    if ch["masked"]:
                        if qc >= 1:
                            c.wait_if(v, "Ac", ("Bexp", p, qc - 1))
                        mvar = qt % 4
                        msrc = dm_sb if ch["side"] == "own" else om_sb
                        ins = v.tensor_tensor(Ssb[:, ch["base"]:ch["base"] + 512],
                                              psB[ch["bank"]][:, :],
                                              msrc[:, mvar * 512:(mvar + 1) * 512], ALU.add)
                        c.inc(ins, "Ve", 1, ("BsevV", p, ch["sc"]))
                        ins = v.reduce_max(mxt[:, 16 + ci:17 + ci], psB[ch["bank"]][:, :], axis=AX.X)
                    else:
                        c.wait(v, "Ac", ("BsevA", p, ch["sc"]))
                        ins = v.reduce_max(mxt[:, 16 + ci:17 + ci],
                                           Ssb[:, ch["base"]:ch["base"] + 512], axis=AX.X)
                    c.inc(ins, "Ve", 1, ("Bmax", p, ch["sc"]))
                v.drain()
                ins = v.reduce_max(mxt[:, qt:qt + 1], mxt[:, 16:16 + rec["nch"]],
                                   axis=AX.X, negate=True)
                c.inc(ins, "Ve", 1, ("Bnegmax", p, qc))
                if qc > (0 if hp == 0 else ntb):
                    ve_tail(SCHEDL[qc - 1])
            if recs:
                ve_tail(recs[-1])

    def phB_sync(g, c, p):
        if B_MAXQT == 0:
            return
        c.wait(g, "Ve", ("Botev", p, 2 * ntb - 1))
        c.inc(g.dma_start(ot_sp[p], otp[:, :]), "St", 16, ("Bot_st", p))

    def phC_sync(g, c):
        for p in range(NPAIR):
            c.inc(g.dma_start(ot_ld[p][:, :], ot_sp[p]), "Sd", 16, ("Cot", p))
        do_norm = C_LEVEL >= 2 and ('spill' in C_SUB or C_SUB == 'all')

        def spill(t_):
            if WAC is not None:
                if not c.dry:
                    g.wait_ge(sems["Ac"], WAC)
                elif c.record:
                    g.wait_ge(SemName("Ac"), WAC)
            elif C_SUB != 'all' and 'wAc1700' in C_SUB:
                c.wait(g, "Ac", ("Baev", 3, 1, 7))
            elif C_SUB != 'all' and 'wAc800' in C_SUB:
                c.wait(g, "Ac", ("Baev", 1, 1, 7))
            elif C_SUB == 'all' or 'nonormwait' not in C_SUB:
                c.wait(g, "Ac", ("Cnorm", t_))
            if C_SUB == 'all' or 'nox1sp' not in C_SUB:
                c.inc(g.dma_start(x1_sp[t_], x1scr[t_ % 2][:, :]), "St", 16, ("Cx1sp", t_))
            if stage == "attn" and (C_SUB == 'all' or 'noydbg' not in C_SUB):
                if C_SUB != 'all' and 'srch1' in C_SUB:
                    srcap = h1[t_ % 4][:, :]
                elif C_SUB != 'all' and 'srcxre' in C_SUB:
                    srcap = xre[t_ % 2][:, :]
                else:
                    srcap = x1scr[t_ % 2][:, :]
                c.inc(g.dma_start(y_out[t_], srcap), "St", 16, ("Cydbg", t_))

        for t_ in range(NT if C_LEVEL >= 2 else 0):
            if t_ >= 2:
                c.wait(g, "Ve", ("Cadd", t_ - 2))
            c.inc(g.dma_start(xre[t_ % 2][:, :], x_in[t_]), "Sd", 16, ("Cxre", t_))
            if do_norm and t_ >= 4:
                spill(t_ - 4)
        if do_norm:
            for t_ in range(NT - 4, NT):
                spill(t_)
        if stage == "attn":
            if not c.dry:
                g.wait_ge(sems["St"], c.counts["St"])
            elif c.record:
                g.wait_ge(SemName("St"), c.counts["St"])

    def phC_gp(g, c):
        c.inc(g.dma_start(wo_b[:, :].rearrange("p (dc f) -> p dc f", dc=8),
                          wo_in.rearrange("dc p f -> p dc f")), "Gd", 16, ("Cwo",))
        c.inc(g.dma_start(wr_b[:, :].rearrange("p (dc f) -> p dc f", dc=8),
                          wr_in.rearrange("dc p f -> p dc f")), "Gd", 16, ("Cwr",))
        c.inc(g.dma_start(bpall[:, :], bpe_in), "Gd", 16, ("Cbpe",))
        c.inc(g.dma_start(bpsh[:, :], bps_in), "Gd", 16, ("Cbps",))

    def emit_C_tr_logits(t, c, tl):
        for t_ in tl:
            c.wait(t, "Ac", ("Cx1b", t_))
            if t_ >= 4:
                c.wait(t, "Ac", ("Cx1tev", t_ - 4))
            ins = None
            for dc in range(8):
                ins = t.transpose(psC_T[t_ % 4][:, dc * P:(dc + 1) * P],
                                  x1b[t_ % 4][:, dc * P:(dc + 1) * P], id_b[:, :])
            c.inc(ins, "Pe", 1, ("Cx1t", t_))
        for t_ in (tl if C_LEVEL >= 4 else []):
            c.wait(t, "Ac", ("Cx1tev", t_))
            c.wait(t, "Gd", ("Cwr",))
            c.wait_if(t, "Ac", ("Clsb", t_ - 1))
            ins = None
            for dc in range(8):
                ins = t.matmul(psC[2][:, 0:8],
                               X1T[:, dc * 2048 + t_ * P: dc * 2048 + (t_ + 1) * P],
                               wr_b[:, dc * 8:(dc + 1) * 8], start=(dc == 0), stop=(dc == 7))
            c.inc(ins, "Pe", 1, ("Clog", t_))

    def phC_pe(t, c):
        for p in range(NPAIR):
            c.wait(t, "Sd", ("Cot", p))
        c.wait(t, "Gd", ("Cwo",))
        for w in range(4):
            tl = [4 * w + i for i in range(4)]
            for t_ in tl:
                for fo in range(2):
                    bank = (2 * t_ + fo) % 2
                    c.wait_if(t, "Ac", ("Copev", t_ - 1, fo))
                    ins = t.matmul(psC[bank][:, :], ones_f[0:1, 0:P],
                                   ob32[0:1, fo * 512:(fo + 1) * 512], start=True, stop=False)
                    for dc in range(8):
                        ins = t.matmul(psC[bank][:, :], ot_ld[dc][:, t_ * P:(t_ + 1) * P],
                                       wo_b[:, dc * D + fo * 512: dc * D + fo * 512 + 512],
                                       start=False, stop=(dc == 7))
                    c.inc(ins, "Pe", 1, ("Copmm", t_, fo))
            if w >= 1 and C_LEVEL >= 3:
                emit_C_tr_logits(t, c, [4 * (w - 1) + i for i in range(4)])
        if C_LEVEL >= 3:
            emit_C_tr_logits(t, c, [12, 13, 14, 15])
        for t_ in range(NT if C_LEVEL >= 5 else 0):
            c.wait(t, "Ve", ("Cgate", t_))
            c.wait(t, "Ac", ("Clsb", NT - 1))
            c.wait_if(t, "Ac", ("CgTev", t_ - 1))
            ins = t.transpose(psC[2][0:8, (t_ % 2) * P:(t_ % 2) * P + P],
                              gate[:, t_ * 9:t_ * 9 + 8], id_f[:, :])
            c.inc(ins, "Pe", 1, ("CgT", t_))

    def phC_ac(a, c):
        def col(t_, r):
            return stC[:, t_ * 8 + r: t_ * 8 + r + 1]
        for w in range(4):
            tl = [4 * w + i for i in range(4)]
            for t_ in tl:
                for fo in range(2):
                    bank = (2 * t_ + fo) % 2
                    c.wait(a, "Pe", ("Copmm", t_, fo))
                    ins = a.activation(h1[t_ % 4][:, fo * 512:(fo + 1) * 512],
                                       psC[bank][:, :], AF.Identity, scale=SC_WO)
                    c.inc(ins, "Ac", 1, ("Copev", t_, fo))
            if 'sq' in C_SUB or C_SUB == 'all':
                for t_ in (tl if C_LEVEL >= 2 else []):
                    c.wait(a, "Ve", ("Cadd", t_))
                    ins = a.activation(sqscr[:, :], h1[t_ % 4][:, :], AF.Square, accum_out=col(t_, 5))
                    c.inc(ins, "Ac", 1, ("Csq", t_))
            if 'mm2' in C_SUB or C_SUB == 'all':
                for t_ in (tl if C_LEVEL >= 2 else []):
                    c.wait(a, "Ve", ("Cm", t_))
                    c.inc(a.activation(col(t_, 6), col(t_, 1), AF.Square), "Ac", 1, ("Cmm2", t_))
            if 'rstd' in C_SUB or C_SUB == 'all':
                for t_ in (tl if C_LEVEL >= 2 else []):
                    c.wait(a, "Ve", ("Cvar", t_))
                    a.activation(col(t_, 3), col(t_, 2), AF.Ln, bias=epst[:, 0:1], scale=1.0)
                    c.inc(a.activation(col(t_, 3), col(t_, 3), AF.Exp, scale=-0.5),
                          "Ac", 1, ("Crstd", t_))
            if 'norm' in C_SUB or C_SUB == 'all':
                for t_ in (tl if C_LEVEL >= 2 else []):
                    if C_SUB == 'all' or 'nowait' not in C_SUB:
                        c.wait(a, "Ve", ("Cbb", t_))
                    if t_ >= 2:
                        c.wait_if(a, "St", ("Cx1sp", t_ - 2))
                        c.wait_if(a, "St", ("Cydbg", t_ - 2))
                    if C_SUB != 'all' and 'sqdst' in C_SUB:
                        ins = a.copy(sqscr[:, :], h1[t_ % 4][:, :])
                    elif C_SUB != 'all' and 'plain' in C_SUB:
                        ins = a.copy(x1scr[t_ % 2][:, :], h1[t_ % 4][:, :])
                    else:
                        ins = a.activation(x1scr[t_ % 2][:, :], h1[t_ % 4][:, :], AF.Identity,
                                           bias=col(t_, 4), scale=col(t_, 3))
                    c.inc(ins, "Ac", 1, ("Cnorm", t_))
                    if C_LEVEL >= 3:
                        if t_ >= 4:
                            c.wait(a, "Pe", ("Cx1t", t_ - 4))
                        c.inc(a.copy(x1b[t_ % 4][:, :], x1scr[t_ % 2][:, :]), "Ac", 1, ("Cx1b", t_))
            tlp = [t_ - 4 for t_ in tl if t_ - 4 >= 0]
            for t_ in (tlp if C_LEVEL >= 3 else []):
                c.wait(a, "Pe", ("Cx1t", t_))
                ins = a.copy(
                    X1T[:, :].rearrange("p (dc f) -> p dc f", dc=8)[:, :, t_ * P:(t_ + 1) * P],
                    psC_T[t_ % 4][:, :].rearrange("p (dc f) -> p dc f", dc=8))
                c.inc(ins, "Ac", 1, ("Cx1tev", t_))
            for t_ in (tlp if C_LEVEL >= 4 else []):
                c.wait(a, "Pe", ("Clog", t_))
                c.inc(a.copy(l_all[:, t_ * 8:(t_ + 1) * 8], psC[2][:, 0:8]),
                      "Ac", 1, ("Clsb", t_))
        for t_ in ([12, 13, 14, 15] if C_LEVEL >= 3 else []):
            c.wait(a, "Pe", ("Cx1t", t_))
            ins = a.copy(
                X1T[:, :].rearrange("p (dc f) -> p dc f", dc=8)[:, :, t_ * P:(t_ + 1) * P],
                psC_T[t_ % 4][:, :].rearrange("p (dc f) -> p dc f", dc=8))
            c.inc(ins, "Ac", 1, ("Cx1tev", t_))
        for t_ in ([12, 13, 14, 15] if C_LEVEL >= 4 else []):
            c.wait(a, "Pe", ("Clog", t_))
            c.inc(a.copy(l_all[:, t_ * 8:(t_ + 1) * 8], psC[2][:, 0:8]),
                  "Ac", 1, ("Clsb", t_))
        for t_ in range(NT if C_LEVEL >= 5 else 0):
            c.wait(a, "Ve", ("Cnegm1", t_))
            ins = a.activation(gch[:, t_ * 40 + 8: t_ * 40 + 16],
                               l_all[:, t_ * 8:(t_ + 1) * 8], AF.Exp,
                               bias=gch[:, t_ * 40: t_ * 40 + 1], scale=1.0)
            c.inc(ins, "Ac", 1, ("Cge", t_))
        for t_ in range(NT if C_LEVEL >= 5 else 0):
            c.wait(a, "Pe", ("CgT", t_))
            ins = a.copy(gT[0:8, t_ * P:(t_ + 1) * P],
                         psC[2][0:8, (t_ % 2) * P:(t_ % 2) * P + P])
            c.inc(ins, "Ac", 1, ("CgTev", t_))

    def phC_ve(v, c):
        def col(t_, r):
            return stC[:, t_ * 8 + r: t_ * 8 + r + 1]
        v.memset(gate[:, :], 1.0)
        for w in range(4):
            tl = [4 * w + i for i in range(4)]
            if 'add' in C_SUB or C_SUB == 'all':
                for t_ in (tl if C_LEVEL >= 2 else []):
                    c.wait(v, "Ac", ("Copev", t_, 1))
                    c.wait(v, "Sd", ("Cxre", t_))
                    ins = v.tensor_tensor(h1[t_ % 4][:, :], h1[t_ % 4][:, :], xre[t_ % 2][:, :], ALU.add)
                    c.inc(ins, "Ve", 1, ("Cadd", t_))
            if 's1' in C_SUB or C_SUB == 'all':
                for t_ in (tl if C_LEVEL >= 2 else []):
                    c.inc(v.reduce_sum(col(t_, 0), h1[t_ % 4][:, :], axis=AX.X), "Ve", 1, ("Cs1", t_))
            if 'm' in C_SUB or C_SUB == 'all':
                for t_ in (tl if C_LEVEL >= 2 else []):
                    c.inc(v.tensor_scalar(col(t_, 1), col(t_, 0), 1.0 / D, None, ALU.mult),
                          "Ve", 1, ("Cm", t_))
            if 'var' in C_SUB or C_SUB == 'all':
                for t_ in (tl if C_LEVEL >= 2 else []):
                    c.wait(v, "Ac", ("Csq", t_))
                    c.wait(v, "Ac", ("Cmm2", t_))
                    ins = v.scalar_tensor_tensor(col(t_, 2), col(t_, 5), 1.0 / D, col(t_, 6),
                                                 ALU.mult, ALU.subtract)
                    c.inc(ins, "Ve", 1, ("Cvar", t_))
            if 'bb' in C_SUB or C_SUB == 'all':
                for t_ in (tl if C_LEVEL >= 2 else []):
                    c.wait(v, "Ac", ("Crstd", t_))
                    ins = v.scalar_tensor_tensor(col(t_, 4), col(t_, 1), -1.0, col(t_, 3),
                                                 ALU.mult, ALU.mult)
                    c.inc(ins, "Ve", 1, ("Cbb", t_))
        def g8(t_, r):
            return gch[:, t_ * 40 + r: t_ * 40 + r + 8]
        def g1(t_, r):
            return gch[:, t_ * 40 + r: t_ * 40 + r + 1]
        for t_ in range(NT if C_LEVEL >= 5 else 0):
            c.wait(v, "Ac", ("Clsb", t_))
            v.reduce_max(g1(t_, 1), l_all[:, t_ * 8:(t_ + 1) * 8], axis=AX.X)
        for t_ in range(NT if C_LEVEL >= 5 else 0):
            c.inc(v.tensor_scalar(g1(t_, 0), g1(t_, 1), -1.0, None, ALU.mult),
                  "Ve", 1, ("Cnegm1", t_))
        for t_ in range(NT if C_LEVEL >= 5 else 0):
            v.tensor_scalar(g8(t_, 16), l_all[:, t_ * 8:(t_ + 1) * 8], g1(t_, 1), None, ALU.is_equal)
        for t_ in range(NT if C_LEVEL >= 5 else 0):
            v.tensor_scalar(g8(t_, 24), g8(t_, 16), NEG, None, ALU.mult)
        for t_ in range(NT if C_LEVEL >= 5 else 0):
            v.tensor_tensor(g8(t_, 24), g8(t_, 24), l_all[:, t_ * 8:(t_ + 1) * 8], ALU.add)
        for t_ in range(NT if C_LEVEL >= 5 else 0):
            v.reduce_max(g1(t_, 2), g8(t_, 24), axis=AX.X)
        for t_ in range(NT if C_LEVEL >= 5 else 0):
            v.tensor_scalar(g8(t_, 32), l_all[:, t_ * 8:(t_ + 1) * 8], g1(t_, 2), None, ALU.is_ge)
        for t_ in range(NT if C_LEVEL >= 5 else 0):
            c.wait(v, "Ac", ("Cge", t_))
            v.tensor_tensor(g8(t_, 32), g8(t_, 32), g8(t_, 8), ALU.mult)
        for t_ in range(NT if C_LEVEL >= 5 else 0):
            v.reduce_sum(g1(t_, 3), g8(t_, 32), axis=AX.X)
        for t_ in range(NT if C_LEVEL >= 5 else 0):
            v.reciprocal(g1(t_, 4), g1(t_, 3))
        for t_ in range(NT if C_LEVEL >= 5 else 0):
            ins = v.tensor_scalar(gate[:, t_ * 9: t_ * 9 + 8], g8(t_, 32), g1(t_, 4), None, ALU.mult)
            c.inc(ins, "Ve", 1, ("Cgate", t_))

    NE = 9

    def gcol(ei):
        return 8 if ei == 0 else ei - 1

    def phD_gp(g, c):
        def wfc_load(k, hb):
            ei2 = k % NE
            kk = k * 32 + hb
            if kk >= 4:
                pk, phb = (kk - 4) // 32, (kk - 4) % 32
                c.wait(g, "Ac", ("Dfcev", pk, phb, 1))
            s2 = wfc_in[ei2, :, :, hb * P:(hb + 1) * P].rearrange("dc p f -> p dc f")
            ins = g.dma_start(wfc_t[kk % 4][:, :].rearrange("p (dc f) -> p dc f", dc=8), s2)
            c.inc(ins, "Gd", 16, ("Dwfc", k, hb))

        def wpj_load(k, fo, hb):
            ei2 = k % NE
            kk = (k * 2 + fo) * 32 + hb
            if kk >= 4:
                pkk = kk - 4
                pk, pfo, phb = pkk // 64, (pkk // 32) % 2, pkk % 32
                if phb < 31:
                    c.wait(g, "Pe", ("Dpjhb", pk, pfo, phb))
                else:
                    c.wait(g, "Pe", ("Dpj", pk, pfo, 7))
            ins = g.dma_start(wpj_t[kk % 4][:, :],
                              wpj_in[ei2, hb, :, fo * 512:(fo + 1) * 512])
            c.inc(ins, "Gd", 16, ("Dwpj", k, fo, hb))

        for hb in range(4):
            wfc_load(0, hb)
        for k in range(2 * NE):
            for hb in range(4, 32):
                wfc_load(k, hb)
            for hb in range(32):
                wpj_load(k, 0, hb)
            if k + 1 < 2 * NE:
                for hb in range(4):
                    wfc_load(k + 1, hb)
            for hb in range(32):
                wpj_load(k, 1, hb)

    def phD_sync(g, c):
        for tg in range(2):
            for ei in range(NE):
                k = tg * NE + ei
                if k >= 2:
                    c.wait(g, "Pe", ("Dbfct", k - 2))
                c.inc(g.dma_start(bfc_st[k % 2][:, :], bfc_in[ei]), "Sd", 16, ("Dbfc", k))
            for tt in range(8):
                t_ = tg * 8 + tt
                if t_ >= 2:
                    c.wait(g, "Ve", ("Eadd", t_ - 2))
                c.inc(g.dma_start(x1re[t_ % 2][:, :], x1_sp[t_]), "Sd", 16, ("Ex1", t_))
            for tt in range(8):
                t_ = tg * 8 + tt
                c.wait(g, "Ac", ("Enorm", t_))
                c.inc(g.dma_start(y_out[t_], yst[t_ % 2][:, :]), "St", 16, ("Ey", t_))
        if not c.dry:
            g.wait_ge(sems["St"], c.counts["St"])
        elif c.record:
            g.wait_ge(SemName("St"), c.counts["St"])

    def phD_pe(t, c):
        for tg in range(2):
            for ei in range(NE):
                k = tg * NE + ei
                c.wait(t, "Sd", ("Dbfc", k))
                c.wait_if(t, "Ac", ("Dbfcev", k - 1))
                c.wait_if(t, "Ve", ("Dpjev", k - 1, 1, 4))
                c.inc(t.transpose(psD[4][:, 0:32], bfc_st[k % 2][0:32, 0:P], id_f[0:32, 0:32]),
                      "Pe", 1, ("Dbfct", k))
                for hb in range(32):
                    c.wait(t, "Gd", ("Dwfc", k, hb))
                    for tc in range(2):
                        bank = (hb % 2) * 2 + tc
                        c.wait_if(t, "Ac", ("Dfcev", k, hb - 2, tc))
                        if hb < 2:
                            c.wait_if(t, "Ve", ("Dpjev", k - 1, 1, bank))
                        ins = None
                        for dc in range(8):
                            ins = t.matmul(
                                psD[bank][:, :], wfc_t[(k * 32 + hb) % 4][:, dc * P:(dc + 1) * P],
                                X1T[:, dc * 2048 + tg * 1024 + tc * 512:
                                    dc * 2048 + tg * 1024 + (tc + 1) * 512],
                                start=(dc == 0), stop=(dc == 7))
                        c.inc(ins, "Pe", 1, ("Dfcmm", k, hb, tc))
                for fo in range(2):
                    biased = (ei == 0)
                    for tt in range(8):
                        bank = tt
                        if fo == 0:
                            if tt < 4:
                                c.wait(t, "Ac", ("Dfcev", k, 30 + tt // 2, tt % 2))
                            elif tt == 4:
                                c.wait(t, "Ac", ("Dbfcev", k))
                            else:
                                c.wait_if(t, "Ve", ("Dpjev", k - 1, 1, tt))
                        else:
                            c.wait(t, "Ve", ("Dpjev", k, 0, tt))
                        if biased:
                            if k == 0 and tt == 0 and fo == 0:
                                c.wait(t, "Ac", ("CgTev", NT - 1))
                                c.wait(t, "Gd", ("Cbpe",))
                                c.wait(t, "Gd", ("Cbps",))
                            t.matmul(psD[bank][:, :],
                                     gT[0:8, (tg * 8 + tt) * P:(tg * 8 + tt + 1) * P],
                                     bpall[0:8, fo * 512:(fo + 1) * 512], start=True, stop=False)
                            t.matmul(psD[bank][:, :], ones_b[0:1, 0:P],
                                     bpsh[0:1, fo * 512:(fo + 1) * 512], start=False, stop=False)
                    for hb in range(32):
                        c.wait(t, "Gd", ("Dwpj", k, fo, hb))
                        if fo == 0:
                            c.wait(t, "Ve", ("DHw", k, hb))
                        ins = None
                        for tt in range(8):
                            ins = t.matmul(psD[tt][:, :],
                                           Hb[:, hb * 1024 + tt * P: hb * 1024 + (tt + 1) * P],
                                           wpj_t[((k * 2 + fo) * 32 + hb) % 4][:, :],
                                           start=((not biased) and hb == 0), stop=(hb == 31))
                            if hb == 31:
                                c.inc(ins, "Pe", 1, ("Dpj", k, fo, tt))
                        if hb < 31:
                            c.inc(ins, "Pe", 1, ("Dpjhb", k, fo, hb))

    def phD_ac(a, c):
        for tg in range(2):
            for ei in range(NE):
                k = tg * NE + ei
                c.wait(a, "Pe", ("Dbfct", k))
                c.inc(a.copy(bfc_t[k % 2][:, 0:32], psD[4][:, 0:32]), "Ac", 1, ("Dbfcev", k))
                for hb in range(32):
                    if hb >= 1:
                        c.wait(a, "Ve", ("DHw", k, hb - 1))
                    elif k >= 1:
                        c.wait(a, "Ve", ("DHw", k - 1, 31))
                    ins = None
                    for tc in range(2):
                        bank = (hb % 2) * 2 + tc
                        c.wait(a, "Pe", ("Dfcmm", k, hb, tc))
                        ins = a.activation(xf[hb % 2][:, tc * 512:(tc + 1) * 512], psD[bank][:, :],
                                           AF.Identity, bias=bfc_t[k % 2][:, hb:hb + 1])
                        c.inc(ins, "Ac", 1, ("Dfcev", k, hb, tc))
                    a.activation(wscr[hb % 2][:, :], xf[hb % 2][:, :], AF.Square)
                    ins = a.activation(wscr[hb % 2][:, :], wscr[hb % 2][:, :], AF.Identity,
                                       bias=1.0, scale=GELU_C2)
                    c.inc(ins, "Ac", 1, ("Dw", k, hb))
                    c.wait(a, "Ve", ("Dt", k, hb))
                    ins = a.activation(sgscr[hb % 2][:, :], tscr[hb % 2][:, :], AF.Sigmoid,
                                       scale=GELU_2C)
                    c.inc(ins, "Ac", 1, ("Dsg", k, hb))
            def colE(tt, r):
                return stE[:, tt * 8 + r: tt * 8 + r + 1]
            for tt in range(8):
                t_ = tg * 8 + tt
                c.wait(a, "Ve", ("Eadd", t_))
                ins = a.activation(wscr[0][:, :], routed[:, tt * D:(tt + 1) * D], AF.Square,
                                   accum_out=colE(tt, 5))
                c.inc(ins, "Ac", 1, ("Esq", t_))
            for tt in range(8):
                t_ = tg * 8 + tt
                c.wait(a, "Ve", ("Em", t_))
                c.inc(a.activation(colE(tt, 6), colE(tt, 1), AF.Square), "Ac", 1, ("Emm2", t_))
            for tt in range(8):
                t_ = tg * 8 + tt
                c.wait(a, "Ve", ("Evar", t_))
                a.activation(colE(tt, 3), colE(tt, 2), AF.Ln, bias=epst[:, 0:1], scale=1.0)
                c.inc(a.activation(colE(tt, 3), colE(tt, 3), AF.Exp, scale=-0.5),
                      "Ac", 1, ("Erstd", t_))
            for tt in range(8):
                t_ = tg * 8 + tt
                c.wait(a, "Ve", ("Ebb", t_))
                if t_ >= 2:
                    c.wait(a, "St", ("Ey", t_ - 2))
                ins = a.activation(yst[t_ % 2][:, :], routed[:, tt * D:(tt + 1) * D], AF.Identity,
                                   bias=colE(tt, 4), scale=colE(tt, 3))
                c.inc(ins, "Ac", 1, ("Enorm", t_))

    def phD_ve(v, c):
        for tg in range(2):
            for ei in range(NE):
                k = tg * NE + ei
                for hb in range(32):
                    c.wait(v, "Ac", ("Dw", k, hb))
                    ins = v.tensor_tensor(tscr[hb % 2][:, :], xf[hb % 2][:, :],
                                          wscr[hb % 2][:, :], ALU.mult)
                    c.inc(ins, "Ve", 1, ("Dt", k, hb))
                    c.wait(v, "Ac", ("Dsg", k, hb))
                    if k >= 1:
                        if hb < 31:
                            c.wait(v, "Pe", ("Dpjhb", k - 1, 1, hb))
                        else:
                            c.wait(v, "Pe", ("Dpj", k - 1, 1, 7))
                    ins = v.tensor_tensor(Hb[:, hb * 1024:(hb + 1) * 1024], xf[hb % 2][:, :],
                                          sgscr[hb % 2][:, :], ALU.mult)
                    c.inc(ins, "Ve", 1, ("DHw", k, hb))
                for fo in range(2):
                    for tt in range(8):
                        c.wait(v, "Pe", ("Dpj", k, fo, tt))
                        dst = routed[:, tt * D + fo * 512: tt * D + (fo + 1) * 512]
                        if ei == 0:
                            if tg == 1:
                                c.wait(v, "Ac", ("Enorm", tt))
                            ins = v.tensor_copy(dst, psD[tt][:, :])
                        else:
                            gc = gate[:, (tg * 8 + tt) * 9 + gcol(ei):
                                      (tg * 8 + tt) * 9 + gcol(ei) + 1]
                            ins = v.scalar_tensor_tensor(dst, psD[tt][:, :], gc, dst,
                                                         ALU.mult, ALU.add)
                        c.inc(ins, "Ve", 1, ("Dpjev", k, fo, tt))
            def colE(tt, r):
                return stE[:, tt * 8 + r: tt * 8 + r + 1]
            for tt in range(8):
                t_ = tg * 8 + tt
                c.wait(v, "Sd", ("Ex1", t_))
                ins = v.tensor_tensor(routed[:, tt * D:(tt + 1) * D],
                                      routed[:, tt * D:(tt + 1) * D], x1re[t_ % 2][:, :], ALU.add)
                c.inc(ins, "Ve", 1, ("Eadd", t_))
            for tt in range(8):
                t_ = tg * 8 + tt
                c.inc(v.reduce_sum(colE(tt, 0), routed[:, tt * D:(tt + 1) * D], axis=AX.X),
                      "Ve", 1, ("Es1", t_))
            for tt in range(8):
                t_ = tg * 8 + tt
                c.inc(v.tensor_scalar(colE(tt, 1), colE(tt, 0), 1.0 / D, None, ALU.mult),
                      "Ve", 1, ("Em", t_))
            for tt in range(8):
                t_ = tg * 8 + tt
                c.wait(v, "Ac", ("Esq", t_))
                c.wait(v, "Ac", ("Emm2", t_))
                ins = v.scalar_tensor_tensor(colE(tt, 2), colE(tt, 5), 1.0 / D, colE(tt, 6),
                                             ALU.mult, ALU.subtract)
                c.inc(ins, "Ve", 1, ("Evar", t_))
            for tt in range(8):
                t_ = tg * 8 + tt
                c.wait(v, "Ac", ("Erstd", t_))
                ins = v.scalar_tensor_tensor(colE(tt, 4), colE(tt, 1), -1.0, colE(tt, 3),
                                             ALU.mult, ALU.mult)
                c.inc(ins, "Ve", 1, ("Ebb", t_))

    passes = [("dry", True, False)]
    if SIM is not None:
        passes.append(("sim", True, True))
    else:
        passes.append(("real", False, False))
    for pname, dry, record in passes:
        c = Ctx(sems, dry, labels, record=record)
        if dry:
            FE = FakeEng()

            def blk(*fns):
                if record:
                    for eng_name, fn in fns:
                        fn(SIM.engine(eng_name), c)
                    SIM.barrier()
                else:
                    for eng_name, fn in fns:
                        fn(FE, c)
        else:
            def blk(*fns):
                with nc.Block() as block:
                    for eng_name, fn in fns:
                        getattr(block, eng_name)(lambda engine, f=fn: f(engine, c))

        blk(("sync", phA_sync), ("gpsimd", phA_gp), ("vector", phA_ve),
            ("tensor", phA_pe), ("scalar", phA_ac))
        if stage.startswith("b"):
            npair_run = int(stage[1:])
        else:
            npair_run = NPAIR
        if stage == "a":
            def dumpA_ac(a, c2):
                for i in range(NT):
                    if i >= 2:
                        c2.wait(a, "St", ("dbgAs", i - 2))
                    ins = a.copy(x_st[i % 2][:, :], XT[:, i * 1024:(i + 1) * 1024])
                    c2.inc(ins, "Ac", 1, ("dbgA", i))
            def dumpA_sync(g, c2):
                for i in range(NT):
                    c2.wait(g, "Ac", ("dbgA", i))
                    c2.inc(g.dma_start(y_out[i], x_st[i % 2][:, :]), "St", 16, ("dbgAs", i))
                if not c2.dry:
                    g.wait_ge(sems["St"], c2.counts["St"])
            blk(("scalar", dumpA_ac), ("sync", dumpA_sync))
            continue_pairs = False
        else:
            continue_pairs = True
        for p in range(npair_run if continue_pairs else 0):
            fns = []
            if p == 0:
                fns.append(("gpsimd", lambda eng, c2: phB_gp(eng, c2, 0)))
            fns += [("tensor", lambda eng, c2, p=p: phB_pe(eng, c2, p)),
                    ("scalar", lambda eng, c2, p=p: phB_ac(eng, c2, p)),
                    ("vector", lambda eng, c2, p=p: phB_ve(eng, c2, p)),
                    ("sync", lambda eng, c2, p=p: phB_sync(eng, c2, p))]
            if p + 1 < NPAIR:
                fns.append(("gpsimd", lambda eng, c2, pn=p + 1: phB_gp(eng, c2, pn)))
            blk(*fns)
        if stage.startswith("b"):
            def dumpB_gp(g, c2):
                if B_MAXQT == 0:
                    outs = [KT[:, 0:1024], KT[:, 1024:2048], KT[:, 2048:3072], KT[:, 3072:4096],
                            Qp[:, 0:1024], Qp[:, 1024:2048],
                            U1[0][:, 0:1024], U1[0][:, 1024:2048],
                            U2[0][:, 0:1024], U2[0][:, 1024:2048],
                            A1s[:, 0:1024], A1s[:, 1024:2048],
                            Vtm[:, 0:1024], Vtm[:, 1024:2048]]
                    for i, src_ap in enumerate(outs):
                        ins = g.dma_start(y_out[i], src_ap)
                        c2.inc(ins, "St", 16, ("dbgB", i, 0))
                else:
                    for p in range(npair_run):
                        c2.wait(g, "St", ("Bot_st", p))
                        for half in range(2):
                            ins = g.dma_start(y_out[2 * p + half],
                                              ot_sp[p][:, half * 1024:(half + 1) * 1024])
                            c2.inc(ins, "St", 16, ("dbgB", p, half))
                if not c2.dry:
                    g.wait_ge(sems["St"], c2.counts["St"])
            blk(("gpsimd", dumpB_gp))
        if stage in ("attn", "full"):
            blk(("sync", phC_sync), ("gpsimd", phC_gp), ("tensor", phC_pe),
                ("scalar", phC_ac), ("vector", phC_ve))
        if stage == "full":
            blk(("gpsimd", phD_gp), ("sync", phD_sync), ("tensor", phD_pe),
                ("scalar", phD_ac), ("vector", phD_ve))

    st_all.close()
    return nc



def _tables():
    j = np.arange(128, dtype=np.float64)
    w = 10000.0 ** (-(2.0 * j) / 256.0)
    pp = np.arange(SEQ, dtype=np.float64)
    ang = pp[None, :] * w[:, None]
    return np.sin(ang).astype(np.float32), np.cos(ang).astype(np.float32)


def _masks(h):
    r = np.arange(P)[:, None]
    cidx = np.arange(512)[None, :]
    dm = np.zeros((4, P, 512), np.float32)
    om = np.zeros((4, P, 512), np.float32)
    for v in range(4):
        dm[v] = np.where(cidx <= 128 * v + r, 0.0, NEG)
        if h == 0:
            om[v] = np.where(cidx < 128 * v, 0.0, NEG)
        else:
            om[v] = np.where(cidx < 128 * (v + 1), 0.0, NEG)
    return dm, om


_NC_CACHE = {}


def kernel(L0_attn_input, L0_attn_qkv, L0_attn_q_bias, L0_attn_positional,
           L0_attn_out_kernel, L0_attn_out_bias, L0_ffn_Wfc_s, L0_ffn_bfc_s,
           L0_ffn_Wproj_s, L0_ffn_bproj_s, L0_ffn_Wr, L0_ffn_Wfc, L0_ffn_bfc,
           L0_ffn_Wproj, L0_ffn_bproj):
    x = np.asarray(L0_attn_input, np.float32)
    qkv = np.ascontiguousarray(np.asarray(L0_attn_qkv, np.float32).reshape(8, P, 3, 16, 64))
    qb = np.ascontiguousarray(np.asarray(L0_attn_q_bias, np.float32))
    pos = np.ascontiguousarray(np.asarray(L0_attn_positional, np.float32).reshape(P, 2048))
    wo = np.ascontiguousarray(np.asarray(L0_attn_out_kernel, np.float32).reshape(8, P, D))
    ob = np.ascontiguousarray(np.asarray(L0_attn_out_bias, np.float32).reshape(1, D))
    wr = np.ascontiguousarray(np.asarray(L0_ffn_Wr, np.float32).reshape(8, P, 8))
    wfc = np.ascontiguousarray(np.concatenate(
        [np.asarray(L0_ffn_Wfc_s, np.float32).reshape(1, 1024, 4096),
         np.asarray(L0_ffn_Wfc, np.float32)], 0).reshape(9, 8, P, 4096))
    wpj = np.ascontiguousarray(np.concatenate(
        [np.asarray(L0_ffn_Wproj_s, np.float32).reshape(1, 4096, 1024),
         np.asarray(L0_ffn_Wproj, np.float32)], 0).reshape(9, 32, P, D))
    bfc = np.ascontiguousarray(np.concatenate(
        [np.asarray(L0_ffn_bfc_s, np.float32).reshape(1, 4096),
         np.asarray(L0_ffn_bfc, np.float32)], 0).reshape(9, 32, P))
    bpe = np.ascontiguousarray(np.asarray(L0_ffn_bproj, np.float32))
    bps = np.ascontiguousarray(np.asarray(L0_ffn_bproj_s, np.float32).reshape(1, D))
    sqt, cqt = _tables()
    ident = np.eye(P, dtype=np.float32)

    if STAGE not in _NC_CACHE:
        _NC_CACHE[STAGE] = build_nc(STAGE)
    nc = _NC_CACHE[STAGE]

    in_maps = []
    metas = []
    for core in range(8):
        b, h = core // 2, core % 2
        perm_tiles = [2 * t + h for t in range(NT)] + [2 * t + (1 - h) for t in range(NT)]
        tok_perm = np.concatenate([np.arange(128 * g, 128 * g + 128) for g in perm_tiles])
        xb_ = np.ascontiguousarray(x[b][tok_perm].reshape(32, P, D))
        dmm, omm = _masks(h)
        in_maps.append(dict(
            x_in=xb_, qkv_in=qkv, qb_in=qb, pos_in=pos, wo_in=wo, ob_in=ob,
            wr_in=wr, wfc_in=wfc, wpj_in=wpj, bfc_in=bfc, bpe_in=bpe, bps_in=bps,
            sq_in=np.ascontiguousarray(sqt[:, tok_perm]),
            cq_in=np.ascontiguousarray(cqt[:, tok_perm]),
            dm_in=dmm, om_in=omm, id_in=ident,
        ))
        metas.append((b, h))

    res = run_bass_kernel_spmd(nc, in_maps, core_ids=list(range(8)))

    out = np.empty((4, SEQ, D), np.float32)
    for core in range(8):
        b, h = metas[core]
        y = np.asarray(res.results[core]["y_out"], np.float32)
        for t in range(NT):
            g = 2 * t + h
            out[b, 128 * g:128 * g + 128, :] = y[t]
    return out


if __name__ == "__main__":
    nc = build_nc(STAGE)
    print("built instructions:", nc.n_instructions())


# version 05
# speedup vs baseline: 1.0451x; 1.0430x over previous
import math
import numpy as np
from contextlib import ExitStack

import concourse.bass as bass
import concourse.mybir as mybir
from concourse.bass_utils import run_bass_kernel_spmd

F32 = mybir.dt.float32
BF16 = mybir.dt.bfloat16
AF = mybir.ActivationFunctionType
ALU = mybir.AluOpType
AX = mybir.AxisListType

P = 128
D = 1024
SEQ = 4096
NT = 16
NPAIR = 8
NEG = -1.0e9
EPS = 1e-5

S_QKV = (3.0 * 1024 * 64 * 16) ** (-0.25)
SC_Q = S_QKV / 8.0
C_POS = math.sqrt(2.0) / 16.0
SC_WO = (1024.0 * 1024.0) ** (-0.25)
GELU_C2 = 0.044715
GELU_2C = 2.0 * math.sqrt(2.0 / math.pi)

STAGE = "full"
WAC = None
SIM = None
B_MAXQT = None
B_PRE = "kvqau"
C_LEVEL = 5
C_SUB = 'all'


def kch_of(t):
    return t // 4 + 1


def make_sched(ntb=NT):
    sched = []
    sc = gt = qc = 0
    for hp in range(2):
        for qt in range(ntb):
            n1 = kch_of(qt)
            chunks = []
            for side_i, side in enumerate(("own", "oth")):
                for j in range(n1):
                    base = j * 512 + side_i * 2048
                    masked = (j == n1 - 1)
                    chunks.append(dict(sc=sc, side=side, j=j, base=base,
                                       masked=masked, bank=sc % 2))
                    sc += 1
            groups = []
            for gi in range(2 * n1):
                groups.append(dict(gt=gt, gi=gi, base=chunks[gi]["base"],
                                   bank=gt % 2))
                gt += 1
            sched.append(dict(hp=hp, qt=qt, qc=qc, chunks=chunks, groups=groups,
                              obank=2 + qc % 2, nch=2 * n1))
            qc += 1
    return sched


SCHED = make_sched()


class FakeInstr:
    def then_inc(self, *a, **k):
        return self


class FakeEng:
    def __getattr__(self, name):
        def f(*a, **k):
            return FakeInstr()
        return f


class SemName:
    def __init__(self, name):
        self.name = name


class Ctx:
    def __init__(self, sems, dry, labels, record=False):
        self.sems = sems
        self.dry = dry
        self.labels = labels
        self.record = record
        self.counts = {k: 0 for k in sems}

    def inc(self, instr, s, n=1, label=None):
        self.counts[s] += n
        if label is not None:
            key = (s,) + tuple(label)
            if self.dry:
                self.labels[key] = self.counts[s]
            else:
                assert self.labels.get(key) == self.counts[s], (
                    key, self.labels.get(key), self.counts[s])
        if not self.dry:
            instr.then_inc(self.sems[s], n)
        elif self.record:
            instr.then_inc(SemName(s), n)

    def alias(self, s, label):
        key = (s,) + tuple(label)
        if self.dry:
            self.labels[key] = self.counts[s]
        else:
            assert self.labels.get(key) == self.counts[s]

    def wait(self, eng, s, label):
        key = (s,) + tuple(label)
        if not self.dry or self.record:
            eng.wait_ge(SemName(s), self.labels[key]) if self.record else                 eng.wait_ge(self.sems[s], self.labels[key])

    def wait_if(self, eng, s, label):
        key = (s,) + tuple(label)
        if key in self.labels and (not self.dry or self.record):
            eng.wait_ge(SemName(s), self.labels[key]) if self.record else                 eng.wait_ge(self.sems[s], self.labels[key])


def build_nc(stage=STAGE):
    nc = bass.Bass("TRN2", target_bir_lowering=False)

    x_in = nc.dram_tensor("x_in", [32, P, D], F32, kind="ExternalInput").ap()
    qkv_in = nc.dram_tensor("qkv_in", [8, P, 3, 16, 64], F32, kind="ExternalInput").ap()
    qb_in = nc.dram_tensor("qb_in", [16, 64], F32, kind="ExternalInput").ap()
    pos_in = nc.dram_tensor("pos_in", [P, 2048], F32, kind="ExternalInput").ap()
    wo_in = nc.dram_tensor("wo_in", [8, P, D], F32, kind="ExternalInput").ap()
    ob_in = nc.dram_tensor("ob_in", [1, D], F32, kind="ExternalInput").ap()
    wr_in = nc.dram_tensor("wr_in", [8, P, 8], F32, kind="ExternalInput").ap()
    wfc_in = nc.dram_tensor("wfc_in", [9, 8, P, 4096], F32, kind="ExternalInput").ap()
    wpj_in = nc.dram_tensor("wpj_in", [9, 32, P, D], F32, kind="ExternalInput").ap()
    bfc_in = nc.dram_tensor("bfc_in", [9, 32, P], F32, kind="ExternalInput").ap()
    bpe_in = nc.dram_tensor("bpe_in", [8, D], F32, kind="ExternalInput").ap()
    bps_in = nc.dram_tensor("bps_in", [1, D], F32, kind="ExternalInput").ap()
    sq_in = nc.dram_tensor("sq_in", [P, SEQ], F32, kind="ExternalInput").ap()
    cq_in = nc.dram_tensor("cq_in", [P, SEQ], F32, kind="ExternalInput").ap()
    dm_in = nc.dram_tensor("dm_in", [4, P, 512], F32, kind="ExternalInput").ap()
    om_in = nc.dram_tensor("om_in", [4, P, 512], F32, kind="ExternalInput").ap()
    id_in = nc.dram_tensor("id_in", [P, P], F32, kind="ExternalInput").ap()

    y_out = nc.dram_tensor("y_out", [NT, P, D], F32, kind="ExternalOutput").ap()
    x1_sp = nc.dram_tensor("x1_sp", [NT, P, D], F32, kind="Internal").ap()
    ot_sp = nc.dram_tensor("ot_sp", [NPAIR, P, 2048], BF16, kind="Internal").ap()
    l_sp = nc.dram_tensor("l_sp", [NPAIR, 2, 1, 2048], F32, kind="Internal").ap()

    st_all = ExitStack()
    e = st_all.enter_context
    sems = {k: e(nc.semaphore(k)) for k in ("Sd", "Gd", "Pe", "Ac", "Ve", "St")}

    id_f = e(nc.sbuf_tensor("id_f", [P, P], F32))
    id_b = e(nc.sbuf_tensor("id_b", [P, P], BF16))
    ones_f = e(nc.sbuf_tensor("ones_f", [1, P], F32))
    ones_b = e(nc.sbuf_tensor("ones_b", [1, P], BF16))
    qb2 = e(nc.sbuf_tensor("qb2", [P, 8], F32))
    qbT = e(nc.sbuf_tensor("qbT", [64, 16], F32))
    ob32 = e(nc.sbuf_tensor("ob32", [1, D], F32))
    epst = e(nc.sbuf_tensor("epst", [P, 1], F32))
    PeT_e = e(nc.sbuf_tensor("PeT_e", [P, 8 * P], BF16))
    PeT_o = e(nc.sbuf_tensor("PeT_o", [P, 8 * P], BF16))

    st_ab = ExitStack()
    ea = st_ab.enter_context
    XT = ea(nc.sbuf_tensor("XT", [P, 8 * SEQ], BF16))
    sq_b = ea(nc.sbuf_tensor("sq_b", [P, SEQ], BF16))
    cq_b = ea(nc.sbuf_tensor("cq_b", [P, SEQ], BF16))
    dm_sb = ea(nc.sbuf_tensor("dm_sb", [P, 4 * 512], BF16))
    om_sb = ea(nc.sbuf_tensor("om_sb", [P, 4 * 512], BF16))

    st_a = ExitStack()
    x_st = [st_a.enter_context(nc.sbuf_tensor(f"x_st{i}", [P, D], F32)) for i in range(2)]
    xb_st = [st_a.enter_context(nc.sbuf_tensor(f"xb_st{i}", [P, D], BF16)) for i in range(2)]
    pos_st = st_a.enter_context(nc.sbuf_tensor("pos_st", [P, 2048], F32))
    qb_st = st_a.enter_context(nc.sbuf_tensor("qb_st", [16, 64], F32))
    ob_st = st_a.enter_context(nc.sbuf_tensor("ob_st", [1, D], F32))
    psA_T = [st_a.enter_context(nc.psum_tensor(f"psA_T{i}", [P, 1024], BF16)) for i in range(2)]
    psA_G = [st_a.enter_context(nc.psum_tensor(f"psA_G{i}", [P, 512], F32)) for i in range(2)]
    st_a.close()

    st_b = ExitStack()
    eb = st_b.enter_context
    KT = eb(nc.sbuf_tensor("KT", [P, SEQ], BF16))
    Vtm = eb(nc.sbuf_tensor("Vtm", [P, 2 * 32 * 65], BF16))
    Qp = eb(nc.sbuf_tensor("Qp", [P, 2048], BF16))
    A1s = eb(nc.sbuf_tensor("A1s", [P, 2048], BF16))
    A2s = eb(nc.sbuf_tensor("A2s", [P, 2048], BF16))
    U1 = [eb(nc.sbuf_tensor(f"U1_{i}", [P, 2048], BF16)) for i in range(2)]
    U2 = [eb(nc.sbuf_tensor(f"U2_{i}", [P, 2048], BF16)) for i in range(2)]
    Utmp = eb(nc.sbuf_tensor("Utmp", [P, 2048], BF16))
    Ssb = eb(nc.sbuf_tensor("Ssb", [P, SEQ], F32))
    Asb = [eb(nc.sbuf_tensor(f"Asb{i}", [P, SEQ], BF16)) for i in range(2)]
    ATr = [eb(nc.sbuf_tensor(f"ATr{i}", [P, 512], BF16)) for i in range(2)]
    mxt = eb(nc.sbuf_tensor("mxt", [P, 24], F32))
    lbuf = eb(nc.sbuf_tensor("lbuf", [1, 2048], F32))
    otp = eb(nc.sbuf_tensor("otp", [P, 2048], BF16))
    wq_t = [eb(nc.sbuf_tensor(f"wq_t{i}", [P, 1024], BF16)) for i in range(2)]
    wk_t = [eb(nc.sbuf_tensor(f"wk_t{i}", [P, 1024], BF16)) for i in range(2)]
    wv_t = [eb(nc.sbuf_tensor(f"wv_t{i}", [P, 1024], BF16)) for i in range(2)]
    psB = [eb(nc.psum_tensor(f"psB{i}", [P, 512], F32)) for i in range(6)]
    psB_T = [eb(nc.psum_tensor(f"psB_T{i}", [P, 512], BF16)) for i in range(2)]
    st_b.close()
    st_ab.close()

    st_cd = ExitStack()
    ec = st_cd.enter_context
    X1T = ec(nc.sbuf_tensor("X1T", [P, 8 * 2048], BF16))
    gate = ec(nc.sbuf_tensor("gate", [P, NT * 9], F32))
    gT = ec(nc.sbuf_tensor("gT", [8, 2048], BF16))

    st_c = ExitStack()
    cc = st_c.enter_context
    ot_ld = [cc(nc.sbuf_tensor(f"ot_ld{i}", [P, 2048], BF16)) for i in range(NPAIR)]
    wo_b = cc(nc.sbuf_tensor("wo_b", [P, 8 * D], BF16))
    wr_b = cc(nc.sbuf_tensor("wr_b", [P, 64], BF16))
    h1 = [cc(nc.sbuf_tensor(f"h1_{i}", [P, D], F32)) for i in range(4)]
    x1scr = [cc(nc.sbuf_tensor(f"x1scr{i}", [P, D], F32)) for i in range(2)]
    x1b = [cc(nc.sbuf_tensor(f"x1b{i}", [P, D], BF16)) for i in range(4)]
    sqscr = cc(nc.sbuf_tensor("sqscr", [P, D], F32))
    xre = [cc(nc.sbuf_tensor(f"xre{i}", [P, D], F32)) for i in range(2)]
    lld = cc(nc.sbuf_tensor("lld", [1, 2048], F32))
    lrc = cc(nc.sbuf_tensor("lrc", [1, 2048], F32))
    rbc = [cc(nc.sbuf_tensor(f"rbc{i}", [P, 512], BF16)) for i in range(2)]
    l_all = cc(nc.sbuf_tensor("l_all", [P, NT * 8], F32))
    stC = cc(nc.sbuf_tensor("stC", [P, NT * 8], F32))
    gch = cc(nc.sbuf_tensor("gch", [P, NT * 40], F32))
    psC = [cc(nc.psum_tensor(f"psC{i}", [P, 512], F32)) for i in range(3)]
    psC_T = [cc(nc.psum_tensor(f"psC_T{i}", [P, 1024], BF16)) for i in range(4)]
    st_c.close()

    st_d = ExitStack()
    dd = st_d.enter_context
    Hb = dd(nc.sbuf_tensor("Hb", [P, 32 * 1024], BF16))
    routed = dd(nc.sbuf_tensor("routed", [P, 8 * D], F32))
    xf = [dd(nc.sbuf_tensor(f"xf{i}", [P, D], F32)) for i in range(2)]
    wscr = [dd(nc.sbuf_tensor(f"wscr{i}", [P, D], F32)) for i in range(2)]
    tscr = [dd(nc.sbuf_tensor(f"tscr{i}", [P, D], F32)) for i in range(2)]
    sgscr = [dd(nc.sbuf_tensor(f"sgscr{i}", [P, D], F32)) for i in range(2)]
    wfc_t = [dd(nc.sbuf_tensor(f"wfc_t{i}", [P, 1024], BF16)) for i in range(4)]
    wpj_t = [dd(nc.sbuf_tensor(f"wpj_t{i}", [P, 512], BF16)) for i in range(4)]
    bfc_t = [dd(nc.sbuf_tensor(f"bfc_t{i}", [P, 32], F32)) for i in range(2)]
    bfc_st = [dd(nc.sbuf_tensor(f"bfc_st{i}", [32, P], F32)) for i in range(2)]
    bpall = dd(nc.sbuf_tensor("bpall", [8, D], BF16))
    bpsh = dd(nc.sbuf_tensor("bpsh", [1, D], BF16))
    x1re = [dd(nc.sbuf_tensor(f"x1re{i}", [P, D], F32)) for i in range(2)]
    yst = [dd(nc.sbuf_tensor(f"yst{i}", [P, D], F32)) for i in range(2)]
    stE = dd(nc.sbuf_tensor("stE", [P, 8 * 8], F32))
    psD = [dd(nc.psum_tensor(f"psD{i}", [P, 512], F32)) for i in range(8)]
    st_d.close()
    st_cd.close()

    labels = {}
    ntb = NT if B_MAXQT is None else B_MAXQT
    SCHEDL = make_sched(ntb)

    def phA_sync(g, c):
        c.inc(g.dma_start(id_f[:, :], id_in), "Sd", 16, ("idf",))

        c.inc(g.dma_start(pos_st[:, :], pos_in), "Sd", 16, ("pos",))
        c.inc(g.dma_start(qb_st[:, :], qb_in), "Sd", 16, ("qb",))
        c.inc(g.dma_start(ob_st[:, :], ob_in), "Sd", 16, ("ob",))
        for i in range(32):
            if i >= 2:
                c.wait(g, "Ve", ("Acast", i - 2))
            c.inc(g.dma_start(x_st[i % 2][:, :], x_in[i]), "Sd", 16, ("Axd", i))

    def phA_gp(g, c):
        c.inc(g.dma_start(dm_sb[:, :].rearrange("p (v f) -> p v f", v=4),
                          dm_in.rearrange("v p f -> p v f")), "Gd", 16, ("dm",))
        c.inc(g.dma_start(om_sb[:, :].rearrange("p (v f) -> p v f", v=4),
                          om_in.rearrange("v p f -> p v f")), "Gd", 16, ("om",))
        c.inc(g.dma_start(id_b[:, :], id_in), "Gd", 16, ("idb",))
        c.inc(g.dma_start(sq_b[:, :], sq_in), "Gd", 16, ("sqb",))
        c.inc(g.dma_start(cq_b[:, :], cq_in), "Gd", 16, ("cqb",))

    def phA_ve(v, c):
        v.memset(ones_f[:, :], 1.0)
        v.memset(epst[:, :], EPS)
        c.inc(v.memset(ones_b[:, :], 1.0), "Ve", 1, ("Aones",))
        for i in range(32):
            c.wait(v, "Sd", ("Axd", i))
            if i >= 2:
                c.wait(v, "Pe", ("Atr", i - 2))
            c.inc(v.tensor_copy(xb_st[i % 2][:, :], x_st[i % 2][:, :]), "Ve", 1, ("Acast", i))

    def phA_pe(t, c):
        c.wait(t, "Gd", ("idb",))
        for i in range(32):
            c.wait(t, "Ve", ("Acast", i))
            if i >= 2:
                c.wait(t, "Ac", ("Aev", i - 2))
            ins = None
            for dc in range(8):
                ins = t.transpose(psA_T[i % 2][:, dc * P:(dc + 1) * P],
                                  xb_st[i % 2][:, dc * P:(dc + 1) * P], id_b[:, :])
            c.inc(ins, "Pe", 1, ("Atr", i))
        c.wait(t, "Sd", ("pos",))
        c.wait(t, "Sd", ("idf",))
        for par in range(2):
            for n in range(16):
                j = par * 16 + n
                c.wait_if(t, "Ac", ("Apev", j - 2))
                ins = t.transpose(psA_G[j % 2][0:64, 0:P],
                                  pos_st[:, par * 1024 + n * 64: par * 1024 + (n + 1) * 64],
                                  id_f[:, :])
                c.inc(ins, "Pe", 1, ("Aptr", j))
        c.wait(t, "Sd", ("qb",))
        c.wait(t, "Ac", ("Apev", 30))
        c.inc(t.transpose(psA_G[0][0:64, 0:16], qb_st[0:16, 0:64], id_f[0:16, 0:16]),
              "Pe", 1, ("Aqbt",))

    def phA_ac(a, c):
        for i in range(32):
            c.wait(a, "Pe", ("Atr", i))
            ins = a.copy(
                XT[:, :].rearrange("p (dc f) -> p dc f", dc=8)[:, :, i * P:(i + 1) * P],
                psA_T[i % 2][:, :].rearrange("p (dc f) -> p dc f", dc=8))
            c.inc(ins, "Ac", 1, ("Aev", i))
        for par in range(2):
            for n in range(16):
                j = par * 16 + n
                c.wait(a, "Pe", ("Aptr", j))
                dst = PeT_e if par == 0 else PeT_o
                pp, hp = n // 2, n % 2
                ins = a.activation(dst[64 * hp:64 * hp + 64, pp * P:(pp + 1) * P],
                                   psA_G[j % 2][0:64, 0:P], AF.Identity, scale=C_POS)
                c.inc(ins, "Ac", 1, ("Apev", j))
        c.wait(a, "Pe", ("Aqbt",))
        a.activation(qbT[0:64, 0:16], psA_G[0][0:64, 0:16], AF.Identity, scale=0.125)
        for n in range(16):
            pp, hp = n // 2, n % 2
            a.copy(qb2[64 * hp:64 * hp + 64, pp:pp + 1], qbT[0:64, n:n + 1])
        c.wait(a, "Sd", ("ob",))
        c.inc(a.mul(ob32[0:1, :], ob_st[0:1, :], 32.0), "Ac", 1, ("Aconsts",))

    def phB_gp(g, c, p):
        buf = p % 2
        for Aidx, wt in ((0, wq_t), (1, wk_t), (2, wv_t)):
            if p >= 2:
                c.wait(g, "Pe", ("Bwdone", p - 2))
            src = qkv_in[:, :, Aidx, 2 * p:2 * p + 2, :].rearrange("dc p n h -> p dc (n h)")
            ins = g.dma_start(wt[buf][:, :].rearrange("p (dc f) -> p dc f", dc=8), src)
            c.inc(ins, "Gd", 16, ("Bw", p, Aidx))

    def phB_pe(t, c, p):
        buf = p % 2
        for Aidx in range(3):
            c.wait(t, "Gd", ("Bw", p, Aidx))
        if p == 0:
            c.wait(t, "Gd", ("sqb",))
            c.wait(t, "Gd", ("cqb",))
            c.wait(t, "Ve", ("Bvmemset",))
            c.wait(t, "Ac", ("Aconsts",))
            c.wait(t, "Ve", ("Aones",))
        for tch in range(8 if "k" in B_PRE else 0):
            c.wait_if(t, "Ac", ("Bkev", p, tch - 2))
            ins = None
            for dc in range(8):
                ins = t.matmul(psB[4 + (tch % 2)][:, :], wk_t[buf][:, dc * P:(dc + 1) * P],
                               XT[:, dc * SEQ + tch * 512: dc * SEQ + (tch + 1) * 512],
                               start=(dc == 0), stop=(dc == 7))
            c.inc(ins, "Pe", 1, ("Bkmm", p, tch))
        for grp in range(8 if "v" in B_PRE else 0):
            if grp < 2:
                c.wait_if(t, "Ac", ("Bkev", p, 6 + grp))
            c.wait_if(t, "Ve", ("Bvev", p, grp - 2))
            ins = None
            for ktl in range(4):
                kt = grp * 4 + ktl
                for dc in range(8):
                    ins = t.matmul(psB[4 + (grp % 2)][:, ktl * P:(ktl + 1) * P],
                                   XT[:, dc * SEQ + kt * P: dc * SEQ + (kt + 1) * P],
                                   wv_t[buf][:, dc * P:(dc + 1) * P],
                                   start=(dc == 0), stop=(dc == 7))
            c.inc(ins, "Pe", 1, ("Bvmm", p, grp))
        for tcg in range(4 if "q" in B_PRE else 0):
            if tcg < 2:
                c.wait_if(t, "Ve", ("Bvev", p, 6 + tcg))
            c.wait_if(t, "Ac", ("Bqev", p, tcg - 2))
            ins = None
            for dc in range(8):
                ins = t.matmul(psB[4 + (tcg % 2)][:, :], wq_t[buf][:, dc * P:(dc + 1) * P],
                               XT[:, dc * SEQ + tcg * 512: dc * SEQ + (tcg + 1) * 512],
                               start=(dc == 0), stop=(dc == 7))
            c.inc(ins, "Pe", 1, ("Bqmm", p, tcg))
        c.alias("Pe", ("Bwdone", p))

        def scores_emit(rec):
            hp, qt = rec["hp"], rec["qt"]
            for ch in rec["chunks"]:
                sc, bank = ch["sc"], ch["bank"]
                c.wait_if(t, "Ac", ("BsevA", p, sc - 2))
                c.wait_if(t, "Ve", ("BsevV", p, sc - 2))
                c.wait_if(t, "Ve", ("Bmax", p, sc - 2))
                t.matmul(psB[bank][:, :],
                         Qp[64 * hp:64 * hp + 64, qt * P:(qt + 1) * P],
                         KT[64 * hp:64 * hp + 64, ch["base"]:ch["base"] + 512],
                         start=True, stop=False)
                t.matmul(psB[bank][:, :], U1[hp][:, qt * P:(qt + 1) * P],
                         cq_b[:, ch["base"]:ch["base"] + 512], start=False, stop=False)
                ins = t.matmul(psB[bank][:, :], U2[hp][:, qt * P:(qt + 1) * P],
                               sq_b[:, ch["base"]:ch["base"] + 512], start=False, stop=True)
                c.inc(ins, "Pe", 1, ("Bsmm", p, sc))

        def pv_group(rec, k, first, last):
            gr = rec["groups"][k]
            hp, obank = rec["hp"], rec["obank"]
            c.wait(t, "Ac", ("Batev", p, gr["gt"]))
            if first:
                c.wait_if(t, "Ve", ("Botev", p, rec["qc"] - 2))
            ins = None
            for i4 in range(4):
                ktile = gr["base"] // P + i4
                ins = t.matmul(psB[obank][0:65, 0:P],
                               Vtm[:, hp * 2080 + ktile * 65: hp * 2080 + ktile * 65 + 65],
                               ATr[gr["bank"]][:, i4 * P:(i4 + 1) * P],
                               start=(first and i4 == 0), stop=(last and i4 == 3))
            if last:
                c.inc(ins, "Pe", 1, ("Bpv", p, rec["qc"]))

        def pv_emit(rec):
            qt, qc = rec["qt"], rec["qc"]
            c.wait(t, "Ac", ("Bexp", p, qc))
            groups = rec["groups"]
            for k, gr in enumerate(groups):
                if gr["gt"] >= 2:
                    c.wait_if(t, "Ac", ("Batev", p, gr["gt"] - 2))
                ins = None
                for i4 in range(4):
                    ins = t.transpose(psB_T[gr["bank"]][:, i4 * P:(i4 + 1) * P],
                                      Asb[qt % 2][:, gr["base"] + i4 * P: gr["base"] + (i4 + 1) * P],
                                      id_b[:, :])
                c.inc(ins, "Pe", 1, ("Btr", p, gr["gt"]))
                if k >= 1:
                    pv_group(rec, k - 1, first=(k - 1 == 0), last=False)
            pv_group(rec, len(groups) - 1, first=(len(groups) == 1), last=True)

        for hp in range(2 if "a" in B_PRE else 0):
            c.wait_if(t, "Ac", ("Bqev", p, 3))
            if hp == 1:
                c.wait_if(t, "Ve", ("BUdone", p, 0))
            for par, PeTx in ((0, PeT_e), (1, PeT_o)):
                for tcg in range(4):
                    j = par * 4 + tcg
                    if hp == 1 and j < 2:
                        c.wait_if(t, "Ac", ("Baev", p, 0, 6 + j))
                    c.wait_if(t, "Ac", ("Baev", p, hp, j - 2))
                    ins = t.matmul(psB[4 + (j % 2)][:, :],
                                   PeTx[64 * hp:64 * hp + 64, p * P:(p + 1) * P],
                                   Qp[64 * hp:64 * hp + 64, tcg * 512:(tcg + 1) * 512],
                                   start=True, stop=True)
                    c.inc(ins, "Pe", 1, ("Bamm", p, hp, j))
            c.wait_if(t, "Ac", ("Bkev", p, 7))
            c.wait_if(t, "Ve", ("BUdone", p, hp))
            if hp == 0:
                c.wait_if(t, "Ve", ("Bvev", p, 7))

            recs = [r for r in SCHEDL if r["hp"] == hp]
            if recs:
                scores_emit(recs[0])
                for i, rec in enumerate(recs):
                    if i + 1 < len(recs):
                        scores_emit(recs[i + 1])
                    pv_emit(rec)

    def phB_ac(a, c, p):
        for tch in range(8 if "k" in B_PRE else 0):
            c.wait(a, "Pe", ("Bkmm", p, tch))
            ins = a.activation(KT[:, tch * 512:(tch + 1) * 512], psB[4 + (tch % 2)][:, :],
                               AF.Identity, scale=S_QKV)
            c.inc(ins, "Ac", 1, ("Bkev", p, tch))
        for tcg in range(4 if "q" in B_PRE else 0):
            c.wait(a, "Pe", ("Bqmm", p, tcg))
            ins = a.activation(Qp[:, tcg * 512:(tcg + 1) * 512], psB[4 + (tcg % 2)][:, :],
                               AF.Identity, bias=qb2[:, p:p + 1], scale=SC_Q)
            c.inc(ins, "Ac", 1, ("Bqev", p, tcg))
        def ac_tail(rec):
            qc = rec["qc"]
            for gr in rec["groups"]:
                c.wait(a, "Pe", ("Btr", p, gr["gt"]))
                c.inc(a.copy(ATr[gr["bank"]][:, :], psB_T[gr["bank"]][:, :]),
                      "Ac", 1, ("Batev", p, gr["gt"]))
            c.wait(a, "Pe", ("Bpv", p, qc))
            hp2, qt2 = rec["hp"], rec["qt"]
            if qt2 == 0:
                if hp2 == 1:
                    c.wait(a, "St", ("Bl_st", p, 0))
                elif p >= 1:
                    c.wait(a, "St", ("Bl_st", p - 1, 1))
            ins = a.copy(lbuf[0:1, qt2 * P:(qt2 + 1) * P],
                         psB[rec["obank"]][64:65, 0:P])
            c.inc(ins, "Ac", 1, ("Bl", p, qc))

        for hp in range(2 if "a" in B_PRE else 0):
            for j in range(8):
                c.wait(a, "Pe", ("Bamm", p, hp, j))
                if hp == 1 and j == 0:
                    c.wait_if(a, "Ve", ("BUdone", p, 0))
                dst = (A1s if j < 4 else A2s)[:, (j % 4) * 512:(j % 4 + 1) * 512]
                c.inc(a.copy(dst, psB[4 + (j % 2)][:, :]), "Ac", 1, ("Baev", p, hp, j))
            recs = [r for r in SCHEDL if r["hp"] == hp]
            for rec in recs:
                qt, qc = rec["qt"], rec["qc"]
                for ch in rec["chunks"]:
                    if ch["masked"]:
                        continue
                    c.wait(a, "Pe", ("Bsmm", p, ch["sc"]))
                    ins = a.copy(Ssb[:, ch["base"]:ch["base"] + 512], psB[ch["bank"]][:, :])
                    c.inc(ins, "Ac", 1, ("BsevA", p, ch["sc"]))
                c.wait(a, "Ve", ("Bnegmax", p, qc))
                if qc >= 2:
                    c.wait_if(a, "Pe", ("Btr", p, SCHEDL[qc - 2]["groups"][-1]["gt"]))
                w = kch_of(qt) * 512
                a.activation(Asb[qt % 2][:, 0:w], Ssb[:, 0:w], AF.Exp,
                             bias=mxt[:, qt:qt + 1], scale=1.0)
                ins = a.activation(Asb[qt % 2][:, 2048:2048 + w], Ssb[:, 2048:2048 + w],
                                   AF.Exp, bias=mxt[:, qt:qt + 1], scale=1.0)
                c.inc(ins, "Ac", 1, ("Bexp", p, qc))
                if qc > (0 if hp == 0 else ntb):
                    ac_tail(SCHEDL[qc - 1])
            if recs:
                ac_tail(recs[-1])

    def phB_ve(v, c, p):
        if p == 0:
            c.inc(v.memset(Vtm[:, :], 1.0), "Ve", 1, ("Bvmemset",))
        for grp in range(8 if "v" in B_PRE else 0):
            c.wait(v, "Pe", ("Bvmm", p, grp))
            ins = None
            for hp in range(2):
                src = psB[4 + (grp % 2)][:, :].rearrange("p (i f) -> p i f", i=4)[:, :, hp * 64:hp * 64 + 64]
                dstc = Vtm[:, hp * 2080 + grp * 4 * 65: hp * 2080 + (grp * 4 + 4) * 65]
                dst = dstc.rearrange("p (i f) -> p i f", i=4)[:, :, 0:64]
                ins = v.tensor_scalar(dst, src, S_QKV, None, ALU.mult)
            c.inc(ins, "Ve", 1, ("Bvev", p, grp))
        for hp in range(2 if "u" in B_PRE else 0):
            c.wait(v, "Ac", ("Baev", p, hp, 7))
            for tcg in range(4):
                v.tensor_tensor(U1[hp][:, tcg * 512:(tcg + 1) * 512],
                                A1s[:, tcg * 512:(tcg + 1) * 512],
                                sq_b[:, tcg * 512:(tcg + 1) * 512], ALU.mult)
            for tcg in range(4):
                v.tensor_tensor(Utmp[:, tcg * 512:(tcg + 1) * 512],
                                A2s[:, tcg * 512:(tcg + 1) * 512],
                                cq_b[:, tcg * 512:(tcg + 1) * 512], ALU.mult)
            for tcg in range(4):
                v.tensor_tensor(U1[hp][:, tcg * 512:(tcg + 1) * 512],
                                U1[hp][:, tcg * 512:(tcg + 1) * 512],
                                Utmp[:, tcg * 512:(tcg + 1) * 512], ALU.add)
            for tcg in range(4):
                v.tensor_tensor(U2[hp][:, tcg * 512:(tcg + 1) * 512],
                                A2s[:, tcg * 512:(tcg + 1) * 512],
                                sq_b[:, tcg * 512:(tcg + 1) * 512], ALU.mult)
            for tcg in range(4):
                v.tensor_tensor(Utmp[:, tcg * 512:(tcg + 1) * 512],
                                A1s[:, tcg * 512:(tcg + 1) * 512],
                                cq_b[:, tcg * 512:(tcg + 1) * 512], ALU.mult)
            ins = None
            for tcg in range(4):
                ins = v.tensor_tensor(U2[hp][:, tcg * 512:(tcg + 1) * 512],
                                      U2[hp][:, tcg * 512:(tcg + 1) * 512],
                                      Utmp[:, tcg * 512:(tcg + 1) * 512], ALU.subtract)
            c.inc(ins, "Ve", 1, ("BUdone", p, hp))

            def ve_tail(rec):
                qc2, qt2, hp2 = rec["qc"], rec["qt"], rec["hp"]
                c.wait(v, "Ac", ("Bl", p, qc2))
                if p >= 1 and qc2 == 0:
                    c.wait(v, "St", ("Bot_st", p - 1))
                ins2 = v.tensor_copy(otp[64 * hp2:64 * hp2 + 64, qt2 * P:(qt2 + 1) * P],
                                     psB[rec["obank"]][0:64, 0:P])
                c.inc(ins2, "Ve", 1, ("Botev", p, qc2))

            recs = [r for r in SCHEDL if r["hp"] == hp]
            for rec in recs:
                qt, qc = rec["qt"], rec["qc"]
                for ci, ch in enumerate(rec["chunks"]):
                    c.wait(v, "Pe", ("Bsmm", p, ch["sc"]))
                    if ch["masked"]:
                        if qc >= 1:
                            c.wait_if(v, "Ac", ("Bexp", p, qc - 1))
                        mvar = qt % 4
                        msrc = dm_sb if ch["side"] == "own" else om_sb
                        ins = v.tensor_tensor(Ssb[:, ch["base"]:ch["base"] + 512],
                                              psB[ch["bank"]][:, :],
                                              msrc[:, mvar * 512:(mvar + 1) * 512], ALU.add)
                        c.inc(ins, "Ve", 1, ("BsevV", p, ch["sc"]))
                        ins = v.reduce_max(mxt[:, 16 + ci:17 + ci], psB[ch["bank"]][:, :], axis=AX.X)
                    else:
                        c.wait(v, "Ac", ("BsevA", p, ch["sc"]))
                        ins = v.reduce_max(mxt[:, 16 + ci:17 + ci],
                                           Ssb[:, ch["base"]:ch["base"] + 512], axis=AX.X)
                    c.inc(ins, "Ve", 1, ("Bmax", p, ch["sc"]))
                v.drain()
                ins = v.reduce_max(mxt[:, qt:qt + 1], mxt[:, 16:16 + rec["nch"]],
                                   axis=AX.X, negate=True)
                c.inc(ins, "Ve", 1, ("Bnegmax", p, qc))
                if qc > (0 if hp == 0 else ntb):
                    ve_tail(SCHEDL[qc - 1])
            if recs:
                ve_tail(recs[-1])

    def phB_sync(g, c, p):
        if B_MAXQT == 0:
            return
        c.wait(g, "Ac", ("Bl", p, ntb - 1))
        c.inc(g.dma_start(l_sp[p, 0], lbuf[0:1, :]), "St", 16, ("Bl_st", p, 0))
        c.wait(g, "Ve", ("Botev", p, 2 * ntb - 1))
        c.inc(g.dma_start(ot_sp[p], otp[:, :]), "St", 16, ("Bot_st", p))
        c.wait(g, "Ac", ("Bl", p, 2 * ntb - 1))
        c.inc(g.dma_start(l_sp[p, 1], lbuf[0:1, :]), "St", 16, ("Bl_st", p, 1))

    def phC_sync(g, c):
        for p in range(NPAIR):
            c.inc(g.dma_start(ot_ld[p][:, :], ot_sp[p]), "Sd", 16, ("Cot", p))
        for p in range(NPAIR):
            for hp in range(2):
                if (p, hp) != (0, 0):
                    ph = (p, hp - 1) if hp else (p - 1, 1)
                    c.wait(g, "Ve", ("Crecip",) + ph)
                c.inc(g.dma_start(lld[0:1, :], l_sp[p, hp]), "Sd", 16, ("Cl", p, hp))
        do_norm = C_LEVEL >= 2 and ('spill' in C_SUB or C_SUB == 'all')

        def spill(t_):
            if WAC is not None:
                if not c.dry:
                    g.wait_ge(sems["Ac"], WAC)
                elif c.record:
                    g.wait_ge(SemName("Ac"), WAC)
            elif C_SUB != 'all' and 'wAc1700' in C_SUB:
                c.wait(g, "Ac", ("Baev", 3, 1, 7))
            elif C_SUB != 'all' and 'wAc800' in C_SUB:
                c.wait(g, "Ac", ("Baev", 1, 1, 7))
            elif C_SUB == 'all' or 'nonormwait' not in C_SUB:
                c.wait(g, "Ac", ("Cnorm", t_))
            if C_SUB == 'all' or 'nox1sp' not in C_SUB:
                c.inc(g.dma_start(x1_sp[t_], x1scr[t_ % 2][:, :]), "St", 16, ("Cx1sp", t_))
            if stage == "attn" and (C_SUB == 'all' or 'noydbg' not in C_SUB):
                if C_SUB != 'all' and 'srch1' in C_SUB:
                    srcap = h1[t_ % 4][:, :]
                elif C_SUB != 'all' and 'srcxre' in C_SUB:
                    srcap = xre[t_ % 2][:, :]
                else:
                    srcap = x1scr[t_ % 2][:, :]
                c.inc(g.dma_start(y_out[t_], srcap), "St", 16, ("Cydbg", t_))

        for t_ in range(NT if C_LEVEL >= 2 else 0):
            if t_ >= 2:
                c.wait(g, "Ve", ("Cadd", t_ - 2))
            c.inc(g.dma_start(xre[t_ % 2][:, :], x_in[t_]), "Sd", 16, ("Cxre", t_))
            if do_norm and t_ >= 4:
                spill(t_ - 4)
        if do_norm:
            for t_ in range(NT - 4, NT):
                spill(t_)
        if stage == "attn":
            if not c.dry:
                g.wait_ge(sems["St"], c.counts["St"])
            elif c.record:
                g.wait_ge(SemName("St"), c.counts["St"])

    def phC_gp(g, c):
        c.inc(g.dma_start(wo_b[:, :].rearrange("p (dc f) -> p dc f", dc=8),
                          wo_in.rearrange("dc p f -> p dc f")), "Gd", 16, ("Cwo",))
        c.inc(g.dma_start(wr_b[:, :].rearrange("p (dc f) -> p dc f", dc=8),
                          wr_in.rearrange("dc p f -> p dc f")), "Gd", 16, ("Cwr",))
        c.inc(g.dma_start(bpall[:, :], bpe_in), "Gd", 16, ("Cbpe",))
        c.inc(g.dma_start(bpsh[:, :], bps_in), "Gd", 16, ("Cbps",))

    def emit_C_tr_logits(t, c, tl):
        for t_ in tl:
            c.wait(t, "Ac", ("Cx1b", t_))
            if t_ >= 4:
                c.wait(t, "Ac", ("Cx1tev", t_ - 4))
            ins = None
            for dc in range(8):
                ins = t.transpose(psC_T[t_ % 4][:, dc * P:(dc + 1) * P],
                                  x1b[t_ % 4][:, dc * P:(dc + 1) * P], id_b[:, :])
            c.inc(ins, "Pe", 1, ("Cx1t", t_))
        for t_ in (tl if C_LEVEL >= 4 else []):
            c.wait(t, "Ac", ("Cx1tev", t_))
            c.wait(t, "Gd", ("Cwr",))
            c.wait_if(t, "Ac", ("Clsb", t_ - 1))
            ins = None
            for dc in range(8):
                ins = t.matmul(psC[2][:, 0:8],
                               X1T[:, dc * 2048 + t_ * P: dc * 2048 + (t_ + 1) * P],
                               wr_b[:, dc * 8:(dc + 1) * 8], start=(dc == 0), stop=(dc == 7))
            c.inc(ins, "Pe", 1, ("Clog", t_))

    def phC_pe(t, c):
        for p in range(NPAIR):
            c.wait(t, "Sd", ("Cot", p))
        for p in range(NPAIR):
            for hp in range(2):
                c.wait(t, "Ve", ("Crecip", p, hp))
                for ch in range(4):
                    j = (p * 8 + hp * 4 + ch)
                    c.wait_if(t, "Ac", ("Crbev", j - 2))
                    ins = t.matmul(psC[j % 2][0:64, 0:512], ones_f[0:1, 0:64],
                                   lrc[0:1, ch * 512:(ch + 1) * 512],
                                   start=True, stop=True)
                    c.inc(ins, "Pe", 1, ("Crb", j))
        c.wait(t, "Ve", ("Cnot", NPAIR * 8 - 1))
        c.wait(t, "Gd", ("Cwo",))
        for w in range(4):
            tl = [4 * w + i for i in range(4)]
            for t_ in tl:
                for fo in range(2):
                    bank = (2 * t_ + fo) % 2
                    c.wait_if(t, "Ac", ("Copev", t_ - 1, fo))
                    ins = t.matmul(psC[bank][:, :], ones_f[0:1, 0:P],
                                   ob32[0:1, fo * 512:(fo + 1) * 512], start=True, stop=False)
                    for dc in range(8):
                        ins = t.matmul(psC[bank][:, :], ot_ld[dc][:, t_ * P:(t_ + 1) * P],
                                       wo_b[:, dc * D + fo * 512: dc * D + fo * 512 + 512],
                                       start=False, stop=(dc == 7))
                    c.inc(ins, "Pe", 1, ("Copmm", t_, fo))
            if w >= 1 and C_LEVEL >= 3:
                emit_C_tr_logits(t, c, [4 * (w - 1) + i for i in range(4)])
        if C_LEVEL >= 3:
            emit_C_tr_logits(t, c, [12, 13, 14, 15])
        for t_ in range(NT if C_LEVEL >= 5 else 0):
            c.wait(t, "Ve", ("Cgate", t_))
            c.wait(t, "Ac", ("Clsb", NT - 1))
            c.wait_if(t, "Ac", ("CgTev", t_ - 1))
            ins = t.transpose(psC[2][0:8, (t_ % 2) * P:(t_ % 2) * P + P],
                              gate[:, t_ * 9:t_ * 9 + 8], id_f[:, :])
            c.inc(ins, "Pe", 1, ("CgT", t_))

    def phC_ac(a, c):
        def col(t_, r):
            return stC[:, t_ * 8 + r: t_ * 8 + r + 1]
        for j in range(NPAIR * 8):
            c.wait(a, "Pe", ("Crb", j))
            c.wait_if(a, "Ve", ("Cnot", j - 2))
            hp = (j // 4) % 2
            ins = a.copy(rbc[j % 2][64 * hp:64 * hp + 64, 0:512], psC[j % 2][0:64, 0:512])
            c.inc(ins, "Ac", 1, ("Crbev", j))
        for w in range(4):
            tl = [4 * w + i for i in range(4)]
            for t_ in tl:
                for fo in range(2):
                    bank = (2 * t_ + fo) % 2
                    c.wait(a, "Pe", ("Copmm", t_, fo))
                    ins = a.activation(h1[t_ % 4][:, fo * 512:(fo + 1) * 512],
                                       psC[bank][:, :], AF.Identity, scale=SC_WO)
                    c.inc(ins, "Ac", 1, ("Copev", t_, fo))
            if 'sq' in C_SUB or C_SUB == 'all':
                for t_ in (tl if C_LEVEL >= 2 else []):
                    c.wait(a, "Ve", ("Cadd", t_))
                    ins = a.activation(sqscr[:, :], h1[t_ % 4][:, :], AF.Square, accum_out=col(t_, 5))
                    c.inc(ins, "Ac", 1, ("Csq", t_))
            if 'mm2' in C_SUB or C_SUB == 'all':
                for t_ in (tl if C_LEVEL >= 2 else []):
                    c.wait(a, "Ve", ("Cm", t_))
                    c.inc(a.activation(col(t_, 6), col(t_, 1), AF.Square), "Ac", 1, ("Cmm2", t_))
            if 'rstd' in C_SUB or C_SUB == 'all':
                for t_ in (tl if C_LEVEL >= 2 else []):
                    c.wait(a, "Ve", ("Cvar", t_))
                    a.activation(col(t_, 3), col(t_, 2), AF.Ln, bias=epst[:, 0:1], scale=1.0)
                    c.inc(a.activation(col(t_, 3), col(t_, 3), AF.Exp, scale=-0.5),
                          "Ac", 1, ("Crstd", t_))
            if 'norm' in C_SUB or C_SUB == 'all':
                for t_ in (tl if C_LEVEL >= 2 else []):
                    if C_SUB == 'all' or 'nowait' not in C_SUB:
                        c.wait(a, "Ve", ("Cbb", t_))
                    if t_ >= 2:
                        c.wait_if(a, "St", ("Cx1sp", t_ - 2))
                        c.wait_if(a, "St", ("Cydbg", t_ - 2))
                    if C_SUB != 'all' and 'sqdst' in C_SUB:
                        ins = a.copy(sqscr[:, :], h1[t_ % 4][:, :])
                    elif C_SUB != 'all' and 'plain' in C_SUB:
                        ins = a.copy(x1scr[t_ % 2][:, :], h1[t_ % 4][:, :])
                    else:
                        ins = a.activation(x1scr[t_ % 2][:, :], h1[t_ % 4][:, :], AF.Identity,
                                           bias=col(t_, 4), scale=col(t_, 3))
                    c.inc(ins, "Ac", 1, ("Cnorm", t_))
                    if C_LEVEL >= 3:
                        if t_ >= 4:
                            c.wait(a, "Pe", ("Cx1t", t_ - 4))
                        c.inc(a.copy(x1b[t_ % 4][:, :], x1scr[t_ % 2][:, :]), "Ac", 1, ("Cx1b", t_))
            tlp = [t_ - 4 for t_ in tl if t_ - 4 >= 0]
            for t_ in (tlp if C_LEVEL >= 3 else []):
                c.wait(a, "Pe", ("Cx1t", t_))
                ins = a.copy(
                    X1T[:, :].rearrange("p (dc f) -> p dc f", dc=8)[:, :, t_ * P:(t_ + 1) * P],
                    psC_T[t_ % 4][:, :].rearrange("p (dc f) -> p dc f", dc=8))
                c.inc(ins, "Ac", 1, ("Cx1tev", t_))
            for t_ in (tlp if C_LEVEL >= 4 else []):
                c.wait(a, "Pe", ("Clog", t_))
                c.inc(a.copy(l_all[:, t_ * 8:(t_ + 1) * 8], psC[2][:, 0:8]),
                      "Ac", 1, ("Clsb", t_))
        for t_ in ([12, 13, 14, 15] if C_LEVEL >= 3 else []):
            c.wait(a, "Pe", ("Cx1t", t_))
            ins = a.copy(
                X1T[:, :].rearrange("p (dc f) -> p dc f", dc=8)[:, :, t_ * P:(t_ + 1) * P],
                psC_T[t_ % 4][:, :].rearrange("p (dc f) -> p dc f", dc=8))
            c.inc(ins, "Ac", 1, ("Cx1tev", t_))
        for t_ in ([12, 13, 14, 15] if C_LEVEL >= 4 else []):
            c.wait(a, "Pe", ("Clog", t_))
            c.inc(a.copy(l_all[:, t_ * 8:(t_ + 1) * 8], psC[2][:, 0:8]),
                  "Ac", 1, ("Clsb", t_))
        for t_ in range(NT if C_LEVEL >= 5 else 0):
            c.wait(a, "Ve", ("Cnegm1", t_))
            ins = a.activation(gch[:, t_ * 40 + 8: t_ * 40 + 16],
                               l_all[:, t_ * 8:(t_ + 1) * 8], AF.Exp,
                               bias=gch[:, t_ * 40: t_ * 40 + 1], scale=1.0)
            c.inc(ins, "Ac", 1, ("Cge", t_))
        for t_ in range(NT if C_LEVEL >= 5 else 0):
            c.wait(a, "Pe", ("CgT", t_))
            ins = a.copy(gT[0:8, t_ * P:(t_ + 1) * P],
                         psC[2][0:8, (t_ % 2) * P:(t_ % 2) * P + P])
            c.inc(ins, "Ac", 1, ("CgTev", t_))

    def phC_ve(v, c):
        def col(t_, r):
            return stC[:, t_ * 8 + r: t_ * 8 + r + 1]
        for p in range(NPAIR):
            for hp in range(2):
                c.wait(v, "Sd", ("Cl", p, hp))
                c.inc(v.reciprocal(lrc[0:1, :], lld[0:1, :]), "Ve", 1, ("Crecip", p, hp))
                for ch in range(4):
                    j = (p * 8 + hp * 4 + ch)
                    c.wait(v, "Ac", ("Crbev", j))
                    dst = ot_ld[p][64 * hp:64 * hp + 64, ch * 512:(ch + 1) * 512]
                    ins = v.tensor_tensor(dst, dst, rbc[j % 2][64 * hp:64 * hp + 64, 0:512], ALU.mult)
                    c.inc(ins, "Ve", 1, ("Cnot", j))
        v.memset(gate[:, :], 1.0)
        for w in range(4):
            tl = [4 * w + i for i in range(4)]
            if 'add' in C_SUB or C_SUB == 'all':
                for t_ in (tl if C_LEVEL >= 2 else []):
                    c.wait(v, "Ac", ("Copev", t_, 1))
                    c.wait(v, "Sd", ("Cxre", t_))
                    ins = v.tensor_tensor(h1[t_ % 4][:, :], h1[t_ % 4][:, :], xre[t_ % 2][:, :], ALU.add)
                    c.inc(ins, "Ve", 1, ("Cadd", t_))
            if 's1' in C_SUB or C_SUB == 'all':
                for t_ in (tl if C_LEVEL >= 2 else []):
                    c.inc(v.reduce_sum(col(t_, 0), h1[t_ % 4][:, :], axis=AX.X), "Ve", 1, ("Cs1", t_))
            if 'm' in C_SUB or C_SUB == 'all':
                for t_ in (tl if C_LEVEL >= 2 else []):
                    c.inc(v.tensor_scalar(col(t_, 1), col(t_, 0), 1.0 / D, None, ALU.mult),
                          "Ve", 1, ("Cm", t_))
            if 'var' in C_SUB or C_SUB == 'all':
                for t_ in (tl if C_LEVEL >= 2 else []):
                    c.wait(v, "Ac", ("Csq", t_))
                    c.wait(v, "Ac", ("Cmm2", t_))
                    ins = v.scalar_tensor_tensor(col(t_, 2), col(t_, 5), 1.0 / D, col(t_, 6),
                                                 ALU.mult, ALU.subtract)
                    c.inc(ins, "Ve", 1, ("Cvar", t_))
            if 'bb' in C_SUB or C_SUB == 'all':
                for t_ in (tl if C_LEVEL >= 2 else []):
                    c.wait(v, "Ac", ("Crstd", t_))
                    ins = v.scalar_tensor_tensor(col(t_, 4), col(t_, 1), -1.0, col(t_, 3),
                                                 ALU.mult, ALU.mult)
                    c.inc(ins, "Ve", 1, ("Cbb", t_))
        def g8(t_, r):
            return gch[:, t_ * 40 + r: t_ * 40 + r + 8]
        def g1(t_, r):
            return gch[:, t_ * 40 + r: t_ * 40 + r + 1]
        for t_ in range(NT if C_LEVEL >= 5 else 0):
            c.wait(v, "Ac", ("Clsb", t_))
            v.reduce_max(g1(t_, 1), l_all[:, t_ * 8:(t_ + 1) * 8], axis=AX.X)
        for t_ in range(NT if C_LEVEL >= 5 else 0):
            c.inc(v.tensor_scalar(g1(t_, 0), g1(t_, 1), -1.0, None, ALU.mult),
                  "Ve", 1, ("Cnegm1", t_))
        for t_ in range(NT if C_LEVEL >= 5 else 0):
            v.tensor_scalar(g8(t_, 16), l_all[:, t_ * 8:(t_ + 1) * 8], g1(t_, 1), None, ALU.is_equal)
        for t_ in range(NT if C_LEVEL >= 5 else 0):
            v.tensor_scalar(g8(t_, 24), g8(t_, 16), NEG, None, ALU.mult)
        for t_ in range(NT if C_LEVEL >= 5 else 0):
            v.tensor_tensor(g8(t_, 24), g8(t_, 24), l_all[:, t_ * 8:(t_ + 1) * 8], ALU.add)
        for t_ in range(NT if C_LEVEL >= 5 else 0):
            v.reduce_max(g1(t_, 2), g8(t_, 24), axis=AX.X)
        for t_ in range(NT if C_LEVEL >= 5 else 0):
            v.tensor_scalar(g8(t_, 32), l_all[:, t_ * 8:(t_ + 1) * 8], g1(t_, 2), None, ALU.is_ge)
        for t_ in range(NT if C_LEVEL >= 5 else 0):
            c.wait(v, "Ac", ("Cge", t_))
            v.tensor_tensor(g8(t_, 32), g8(t_, 32), g8(t_, 8), ALU.mult)
        for t_ in range(NT if C_LEVEL >= 5 else 0):
            v.reduce_sum(g1(t_, 3), g8(t_, 32), axis=AX.X)
        for t_ in range(NT if C_LEVEL >= 5 else 0):
            v.reciprocal(g1(t_, 4), g1(t_, 3))
        for t_ in range(NT if C_LEVEL >= 5 else 0):
            ins = v.tensor_scalar(gate[:, t_ * 9: t_ * 9 + 8], g8(t_, 32), g1(t_, 4), None, ALU.mult)
            c.inc(ins, "Ve", 1, ("Cgate", t_))

    NE = 9

    def gcol(ei):
        return 8 if ei == 0 else ei - 1

    def phD_gp(g, c):
        def wfc_load(k, hb):
            ei2 = k % NE
            kk = k * 32 + hb
            if kk >= 4:
                pk, phb = (kk - 4) // 32, (kk - 4) % 32
                c.wait(g, "Ac", ("Dfcev", pk, phb, 1))
            s2 = wfc_in[ei2, :, :, hb * P:(hb + 1) * P].rearrange("dc p f -> p dc f")
            ins = g.dma_start(wfc_t[kk % 4][:, :].rearrange("p (dc f) -> p dc f", dc=8), s2)
            c.inc(ins, "Gd", 16, ("Dwfc", k, hb))

        def wpj_load(k, fo, hb):
            ei2 = k % NE
            kk = (k * 2 + fo) * 32 + hb
            if kk >= 4:
                pkk = kk - 4
                pk, pfo, phb = pkk // 64, (pkk // 32) % 2, pkk % 32
                if phb < 31:
                    c.wait(g, "Pe", ("Dpjhb", pk, pfo, phb))
                else:
                    c.wait(g, "Pe", ("Dpj", pk, pfo, 7))
            ins = g.dma_start(wpj_t[kk % 4][:, :],
                              wpj_in[ei2, hb, :, fo * 512:(fo + 1) * 512])
            c.inc(ins, "Gd", 16, ("Dwpj", k, fo, hb))

        for hb in range(4):
            wfc_load(0, hb)
        for k in range(2 * NE):
            for hb in range(4, 32):
                wfc_load(k, hb)
            for hb in range(32):
                wpj_load(k, 0, hb)
            if k + 1 < 2 * NE:
                for hb in range(4):
                    wfc_load(k + 1, hb)
            for hb in range(32):
                wpj_load(k, 1, hb)

    def phD_sync(g, c):
        for tg in range(2):
            for ei in range(NE):
                k = tg * NE + ei
                if k >= 2:
                    c.wait(g, "Pe", ("Dbfct", k - 2))
                c.inc(g.dma_start(bfc_st[k % 2][:, :], bfc_in[ei]), "Sd", 16, ("Dbfc", k))
            for tt in range(8):
                t_ = tg * 8 + tt
                if t_ >= 2:
                    c.wait(g, "Ve", ("Eadd", t_ - 2))
                c.inc(g.dma_start(x1re[t_ % 2][:, :], x1_sp[t_]), "Sd", 16, ("Ex1", t_))
            for tt in range(8):
                t_ = tg * 8 + tt
                c.wait(g, "Ac", ("Enorm", t_))
                c.inc(g.dma_start(y_out[t_], yst[t_ % 2][:, :]), "St", 16, ("Ey", t_))
        if not c.dry:
            g.wait_ge(sems["St"], c.counts["St"])
        elif c.record:
            g.wait_ge(SemName("St"), c.counts["St"])

    def phD_pe(t, c):
        for tg in range(2):
            for ei in range(NE):
                k = tg * NE + ei
                c.wait(t, "Sd", ("Dbfc", k))
                c.wait_if(t, "Ac", ("Dbfcev", k - 1))
                c.wait_if(t, "Ve", ("Dpjev", k - 1, 1, 4))
                c.inc(t.transpose(psD[4][:, 0:32], bfc_st[k % 2][0:32, 0:P], id_f[0:32, 0:32]),
                      "Pe", 1, ("Dbfct", k))
                for hb in range(32):
                    c.wait(t, "Gd", ("Dwfc", k, hb))
                    for tc in range(2):
                        bank = (hb % 2) * 2 + tc
                        c.wait_if(t, "Ac", ("Dfcev", k, hb - 2, tc))
                        if hb < 2:
                            c.wait_if(t, "Ve", ("Dpjev", k - 1, 1, bank))
                        ins = None
                        for dc in range(8):
                            ins = t.matmul(
                                psD[bank][:, :], wfc_t[(k * 32 + hb) % 4][:, dc * P:(dc + 1) * P],
                                X1T[:, dc * 2048 + tg * 1024 + tc * 512:
                                    dc * 2048 + tg * 1024 + (tc + 1) * 512],
                                start=(dc == 0), stop=(dc == 7))
                        c.inc(ins, "Pe", 1, ("Dfcmm", k, hb, tc))
                for fo in range(2):
                    biased = (ei == 0)
                    for tt in range(8):
                        bank = tt
                        if fo == 0:
                            if tt < 4:
                                c.wait(t, "Ac", ("Dfcev", k, 30 + tt // 2, tt % 2))
                            elif tt == 4:
                                c.wait(t, "Ac", ("Dbfcev", k))
                            else:
                                c.wait_if(t, "Ve", ("Dpjev", k - 1, 1, tt))
                        else:
                            c.wait(t, "Ve", ("Dpjev", k, 0, tt))
                        if biased:
                            if k == 0 and tt == 0 and fo == 0:
                                c.wait(t, "Ac", ("CgTev", NT - 1))
                                c.wait(t, "Gd", ("Cbpe",))
                                c.wait(t, "Gd", ("Cbps",))
                            t.matmul(psD[bank][:, :],
                                     gT[0:8, (tg * 8 + tt) * P:(tg * 8 + tt + 1) * P],
                                     bpall[0:8, fo * 512:(fo + 1) * 512], start=True, stop=False)
                            t.matmul(psD[bank][:, :], ones_b[0:1, 0:P],
                                     bpsh[0:1, fo * 512:(fo + 1) * 512], start=False, stop=False)
                    for hb in range(32):
                        c.wait(t, "Gd", ("Dwpj", k, fo, hb))
                        if fo == 0:
                            c.wait(t, "Ve", ("DHw", k, hb))
                        ins = None
                        for tt in range(8):
                            ins = t.matmul(psD[tt][:, :],
                                           Hb[:, hb * 1024 + tt * P: hb * 1024 + (tt + 1) * P],
                                           wpj_t[((k * 2 + fo) * 32 + hb) % 4][:, :],
                                           start=((not biased) and hb == 0), stop=(hb == 31))
                            if hb == 31:
                                c.inc(ins, "Pe", 1, ("Dpj", k, fo, tt))
                        if hb < 31:
                            c.inc(ins, "Pe", 1, ("Dpjhb", k, fo, hb))

    def phD_ac(a, c):
        for tg in range(2):
            for ei in range(NE):
                k = tg * NE + ei
                c.wait(a, "Pe", ("Dbfct", k))
                c.inc(a.copy(bfc_t[k % 2][:, 0:32], psD[4][:, 0:32]), "Ac", 1, ("Dbfcev", k))
                for hb in range(32):
                    if hb >= 1:
                        c.wait(a, "Ve", ("DHw", k, hb - 1))
                    elif k >= 1:
                        c.wait(a, "Ve", ("DHw", k - 1, 31))
                    ins = None
                    for tc in range(2):
                        bank = (hb % 2) * 2 + tc
                        c.wait(a, "Pe", ("Dfcmm", k, hb, tc))
                        ins = a.activation(xf[hb % 2][:, tc * 512:(tc + 1) * 512], psD[bank][:, :],
                                           AF.Identity, bias=bfc_t[k % 2][:, hb:hb + 1])
                        c.inc(ins, "Ac", 1, ("Dfcev", k, hb, tc))
                    a.activation(wscr[hb % 2][:, :], xf[hb % 2][:, :], AF.Square)
                    ins = a.activation(wscr[hb % 2][:, :], wscr[hb % 2][:, :], AF.Identity,
                                       bias=1.0, scale=GELU_C2)
                    c.inc(ins, "Ac", 1, ("Dw", k, hb))
                    c.wait(a, "Ve", ("Dt", k, hb))
                    ins = a.activation(sgscr[hb % 2][:, :], tscr[hb % 2][:, :], AF.Sigmoid,
                                       scale=GELU_2C)
                    c.inc(ins, "Ac", 1, ("Dsg", k, hb))
            def colE(tt, r):
                return stE[:, tt * 8 + r: tt * 8 + r + 1]
            for tt in range(8):
                t_ = tg * 8 + tt
                c.wait(a, "Ve", ("Eadd", t_))
                ins = a.activation(wscr[0][:, :], routed[:, tt * D:(tt + 1) * D], AF.Square,
                                   accum_out=colE(tt, 5))
                c.inc(ins, "Ac", 1, ("Esq", t_))
            for tt in range(8):
                t_ = tg * 8 + tt
                c.wait(a, "Ve", ("Em", t_))
                c.inc(a.activation(colE(tt, 6), colE(tt, 1), AF.Square), "Ac", 1, ("Emm2", t_))
            for tt in range(8):
                t_ = tg * 8 + tt
                c.wait(a, "Ve", ("Evar", t_))
                a.activation(colE(tt, 3), colE(tt, 2), AF.Ln, bias=epst[:, 0:1], scale=1.0)
                c.inc(a.activation(colE(tt, 3), colE(tt, 3), AF.Exp, scale=-0.5),
                      "Ac", 1, ("Erstd", t_))
            for tt in range(8):
                t_ = tg * 8 + tt
                c.wait(a, "Ve", ("Ebb", t_))
                if t_ >= 2:
                    c.wait(a, "St", ("Ey", t_ - 2))
                ins = a.activation(yst[t_ % 2][:, :], routed[:, tt * D:(tt + 1) * D], AF.Identity,
                                   bias=colE(tt, 4), scale=colE(tt, 3))
                c.inc(ins, "Ac", 1, ("Enorm", t_))

    def phD_ve(v, c):
        for tg in range(2):
            for ei in range(NE):
                k = tg * NE + ei
                for hb in range(32):
                    c.wait(v, "Ac", ("Dw", k, hb))
                    ins = v.tensor_tensor(tscr[hb % 2][:, :], xf[hb % 2][:, :],
                                          wscr[hb % 2][:, :], ALU.mult)
                    c.inc(ins, "Ve", 1, ("Dt", k, hb))
                    c.wait(v, "Ac", ("Dsg", k, hb))
                    if k >= 1:
                        if hb < 31:
                            c.wait(v, "Pe", ("Dpjhb", k - 1, 1, hb))
                        else:
                            c.wait(v, "Pe", ("Dpj", k - 1, 1, 7))
                    ins = v.tensor_tensor(Hb[:, hb * 1024:(hb + 1) * 1024], xf[hb % 2][:, :],
                                          sgscr[hb % 2][:, :], ALU.mult)
                    c.inc(ins, "Ve", 1, ("DHw", k, hb))
                for fo in range(2):
                    for tt in range(8):
                        c.wait(v, "Pe", ("Dpj", k, fo, tt))
                        dst = routed[:, tt * D + fo * 512: tt * D + (fo + 1) * 512]
                        if ei == 0:
                            if tg == 1:
                                c.wait(v, "Ac", ("Enorm", tt))
                            ins = v.tensor_copy(dst, psD[tt][:, :])
                        else:
                            gc = gate[:, (tg * 8 + tt) * 9 + gcol(ei):
                                      (tg * 8 + tt) * 9 + gcol(ei) + 1]
                            ins = v.scalar_tensor_tensor(dst, psD[tt][:, :], gc, dst,
                                                         ALU.mult, ALU.add)
                        c.inc(ins, "Ve", 1, ("Dpjev", k, fo, tt))
            def colE(tt, r):
                return stE[:, tt * 8 + r: tt * 8 + r + 1]
            for tt in range(8):
                t_ = tg * 8 + tt
                c.wait(v, "Sd", ("Ex1", t_))
                ins = v.tensor_tensor(routed[:, tt * D:(tt + 1) * D],
                                      routed[:, tt * D:(tt + 1) * D], x1re[t_ % 2][:, :], ALU.add)
                c.inc(ins, "Ve", 1, ("Eadd", t_))
            for tt in range(8):
                t_ = tg * 8 + tt
                c.inc(v.reduce_sum(colE(tt, 0), routed[:, tt * D:(tt + 1) * D], axis=AX.X),
                      "Ve", 1, ("Es1", t_))
            for tt in range(8):
                t_ = tg * 8 + tt
                c.inc(v.tensor_scalar(colE(tt, 1), colE(tt, 0), 1.0 / D, None, ALU.mult),
                      "Ve", 1, ("Em", t_))
            for tt in range(8):
                t_ = tg * 8 + tt
                c.wait(v, "Ac", ("Esq", t_))
                c.wait(v, "Ac", ("Emm2", t_))
                ins = v.scalar_tensor_tensor(colE(tt, 2), colE(tt, 5), 1.0 / D, colE(tt, 6),
                                             ALU.mult, ALU.subtract)
                c.inc(ins, "Ve", 1, ("Evar", t_))
            for tt in range(8):
                t_ = tg * 8 + tt
                c.wait(v, "Ac", ("Erstd", t_))
                ins = v.scalar_tensor_tensor(colE(tt, 4), colE(tt, 1), -1.0, colE(tt, 3),
                                             ALU.mult, ALU.mult)
                c.inc(ins, "Ve", 1, ("Ebb", t_))

    passes = [("dry", True, False)]
    if SIM is not None:
        passes.append(("sim", True, True))
    else:
        passes.append(("real", False, False))
    for pname, dry, record in passes:
        c = Ctx(sems, dry, labels, record=record)
        if dry:
            FE = FakeEng()

            def blk(*fns):
                if record:
                    for eng_name, fn in fns:
                        fn(SIM.engine(eng_name), c)
                    SIM.barrier()
                else:
                    for eng_name, fn in fns:
                        fn(FE, c)
        else:
            def blk(*fns):
                with nc.Block() as block:
                    for eng_name, fn in fns:
                        getattr(block, eng_name)(lambda engine, f=fn: f(engine, c))

        blk(("sync", phA_sync), ("gpsimd", phA_gp), ("vector", phA_ve),
            ("tensor", phA_pe), ("scalar", phA_ac))
        if stage.startswith("b"):
            npair_run = int(stage[1:])
        else:
            npair_run = NPAIR
        if stage == "a":
            def dumpA_ac(a, c2):
                for i in range(NT):
                    if i >= 2:
                        c2.wait(a, "St", ("dbgAs", i - 2))
                    ins = a.copy(x_st[i % 2][:, :], XT[:, i * 1024:(i + 1) * 1024])
                    c2.inc(ins, "Ac", 1, ("dbgA", i))
            def dumpA_sync(g, c2):
                for i in range(NT):
                    c2.wait(g, "Ac", ("dbgA", i))
                    c2.inc(g.dma_start(y_out[i], x_st[i % 2][:, :]), "St", 16, ("dbgAs", i))
                if not c2.dry:
                    g.wait_ge(sems["St"], c2.counts["St"])
            blk(("scalar", dumpA_ac), ("sync", dumpA_sync))
            continue_pairs = False
        else:
            continue_pairs = True
        for p in range(npair_run if continue_pairs else 0):
            fns = []
            if p == 0:
                fns.append(("gpsimd", lambda eng, c2: phB_gp(eng, c2, 0)))
            fns += [("tensor", lambda eng, c2, p=p: phB_pe(eng, c2, p)),
                    ("scalar", lambda eng, c2, p=p: phB_ac(eng, c2, p)),
                    ("vector", lambda eng, c2, p=p: phB_ve(eng, c2, p)),
                    ("sync", lambda eng, c2, p=p: phB_sync(eng, c2, p))]
            if p + 1 < NPAIR:
                fns.append(("gpsimd", lambda eng, c2, pn=p + 1: phB_gp(eng, c2, pn)))
            blk(*fns)
        if stage.startswith("b"):
            def dumpB_gp(g, c2):
                if B_MAXQT == 0:
                    outs = [KT[:, 0:1024], KT[:, 1024:2048], KT[:, 2048:3072], KT[:, 3072:4096],
                            Qp[:, 0:1024], Qp[:, 1024:2048],
                            U1[0][:, 0:1024], U1[0][:, 1024:2048],
                            U2[0][:, 0:1024], U2[0][:, 1024:2048],
                            A1s[:, 0:1024], A1s[:, 1024:2048],
                            Vtm[:, 0:1024], Vtm[:, 1024:2048]]
                    for i, src_ap in enumerate(outs):
                        ins = g.dma_start(y_out[i], src_ap)
                        c2.inc(ins, "St", 16, ("dbgB", i, 0))
                else:
                    for p in range(npair_run):
                        c2.wait(g, "St", ("Bot_st", p))
                        for half in range(2):
                            ins = g.dma_start(y_out[2 * p + half],
                                              ot_sp[p][:, half * 1024:(half + 1) * 1024])
                            c2.inc(ins, "St", 16, ("dbgB", p, half))
                if not c2.dry:
                    g.wait_ge(sems["St"], c2.counts["St"])
            blk(("gpsimd", dumpB_gp))
        if stage in ("attn", "full"):
            blk(("sync", phC_sync), ("gpsimd", phC_gp), ("tensor", phC_pe),
                ("scalar", phC_ac), ("vector", phC_ve))
        if stage == "full":
            blk(("gpsimd", phD_gp), ("sync", phD_sync), ("tensor", phD_pe),
                ("scalar", phD_ac), ("vector", phD_ve))

    st_all.close()
    return nc



def _tables():
    j = np.arange(128, dtype=np.float64)
    w = 10000.0 ** (-(2.0 * j) / 256.0)
    pp = np.arange(SEQ, dtype=np.float64)
    ang = pp[None, :] * w[:, None]
    return np.sin(ang).astype(np.float32), np.cos(ang).astype(np.float32)


def _masks(h):
    r = np.arange(P)[:, None]
    cidx = np.arange(512)[None, :]
    dm = np.zeros((4, P, 512), np.float32)
    om = np.zeros((4, P, 512), np.float32)
    for v in range(4):
        dm[v] = np.where(cidx <= 128 * v + r, 0.0, NEG)
        if h == 0:
            om[v] = np.where(cidx < 128 * v, 0.0, NEG)
        else:
            om[v] = np.where(cidx < 128 * (v + 1), 0.0, NEG)
    return dm, om


_NC_CACHE = {}


def kernel(L0_attn_input, L0_attn_qkv, L0_attn_q_bias, L0_attn_positional,
           L0_attn_out_kernel, L0_attn_out_bias, L0_ffn_Wfc_s, L0_ffn_bfc_s,
           L0_ffn_Wproj_s, L0_ffn_bproj_s, L0_ffn_Wr, L0_ffn_Wfc, L0_ffn_bfc,
           L0_ffn_Wproj, L0_ffn_bproj):
    x = np.asarray(L0_attn_input, np.float32)
    qkv = np.ascontiguousarray(np.asarray(L0_attn_qkv, np.float32).reshape(8, P, 3, 16, 64))
    qb = np.ascontiguousarray(np.asarray(L0_attn_q_bias, np.float32))
    pos = np.ascontiguousarray(np.asarray(L0_attn_positional, np.float32).reshape(P, 2048))
    wo = np.ascontiguousarray(np.asarray(L0_attn_out_kernel, np.float32).reshape(8, P, D))
    ob = np.ascontiguousarray(np.asarray(L0_attn_out_bias, np.float32).reshape(1, D))
    wr = np.ascontiguousarray(np.asarray(L0_ffn_Wr, np.float32).reshape(8, P, 8))
    wfc = np.ascontiguousarray(np.concatenate(
        [np.asarray(L0_ffn_Wfc_s, np.float32).reshape(1, 1024, 4096),
         np.asarray(L0_ffn_Wfc, np.float32)], 0).reshape(9, 8, P, 4096))
    wpj = np.ascontiguousarray(np.concatenate(
        [np.asarray(L0_ffn_Wproj_s, np.float32).reshape(1, 4096, 1024),
         np.asarray(L0_ffn_Wproj, np.float32)], 0).reshape(9, 32, P, D))
    bfc = np.ascontiguousarray(np.concatenate(
        [np.asarray(L0_ffn_bfc_s, np.float32).reshape(1, 4096),
         np.asarray(L0_ffn_bfc, np.float32)], 0).reshape(9, 32, P))
    bpe = np.ascontiguousarray(np.asarray(L0_ffn_bproj, np.float32))
    bps = np.ascontiguousarray(np.asarray(L0_ffn_bproj_s, np.float32).reshape(1, D))
    sqt, cqt = _tables()
    ident = np.eye(P, dtype=np.float32)

    if STAGE not in _NC_CACHE:
        _NC_CACHE[STAGE] = build_nc(STAGE)
    nc = _NC_CACHE[STAGE]

    in_maps = []
    metas = []
    for core in range(8):
        b, h = core // 2, core % 2
        perm_tiles = [2 * t + h for t in range(NT)] + [2 * t + (1 - h) for t in range(NT)]
        tok_perm = np.concatenate([np.arange(128 * g, 128 * g + 128) for g in perm_tiles])
        xb_ = np.ascontiguousarray(x[b][tok_perm].reshape(32, P, D))
        dmm, omm = _masks(h)
        in_maps.append(dict(
            x_in=xb_, qkv_in=qkv, qb_in=qb, pos_in=pos, wo_in=wo, ob_in=ob,
            wr_in=wr, wfc_in=wfc, wpj_in=wpj, bfc_in=bfc, bpe_in=bpe, bps_in=bps,
            sq_in=np.ascontiguousarray(sqt[:, tok_perm]),
            cq_in=np.ascontiguousarray(cqt[:, tok_perm]),
            dm_in=dmm, om_in=omm, id_in=ident,
        ))
        metas.append((b, h))

    res = run_bass_kernel_spmd(nc, in_maps, core_ids=list(range(8)))

    out = np.empty((4, SEQ, D), np.float32)
    for core in range(8):
        b, h = metas[core]
        y = np.asarray(res.results[core]["y_out"], np.float32)
        for t in range(NT):
            g = 2 * t + h
            out[b, 128 * g:128 * g + 128, :] = y[t]
    return out


if __name__ == "__main__":
    nc = build_nc(STAGE)
    print("built instructions:", nc.n_instructions())


# version 07 (final)
# speedup vs baseline: 1.0643x; 1.0184x over previous
import math
import numpy as np
from contextlib import ExitStack

import concourse.bass as bass
import concourse.mybir as mybir
from concourse.bass_utils import run_bass_kernel_spmd

F32 = mybir.dt.float32
BF16 = mybir.dt.bfloat16
AF = mybir.ActivationFunctionType
ALU = mybir.AluOpType
AX = mybir.AxisListType

P = 128
D = 1024
SEQ = 4096
NT = 16
NPAIR = 8
NEG = -1.0e9
EPS = 1e-5

S_QKV = (3.0 * 1024 * 64 * 16) ** (-0.25)
SC_Q = S_QKV / 8.0
C_POS = math.sqrt(2.0) / 16.0
SC_WO = (1024.0 * 1024.0) ** (-0.25)
GELU_C2 = 0.044715
GELU_2C = 2.0 * math.sqrt(2.0 / math.pi)

STAGE = "full"
WAC = None
SIM = None
B_MAXQT = None
B_PRE = "kvqau"
C_LEVEL = 5
C_SUB = 'all'


def kch_of(t):
    return t // 4 + 1


def make_sched(ntb=NT):
    sched = []
    sc = gt = qc = 0
    for hp in range(2):
        for qt in range(ntb):
            n1 = kch_of(qt)
            chunks = []
            for side_i, side in enumerate(("own", "oth")):
                for j in range(n1):
                    base = j * 512 + side_i * 2048
                    masked = (j == n1 - 1)
                    chunks.append(dict(sc=sc, side=side, j=j, base=base,
                                       masked=masked, bank=sc % 3))
                    sc += 1
            groups = []
            for gi in range(2 * n1):
                groups.append(dict(gt=gt, gi=gi, base=chunks[gi]["base"],
                                   bank=gt % 2))
                gt += 1
            sched.append(dict(hp=hp, qt=qt, qc=qc, chunks=chunks, groups=groups,
                              obank=2 + qc % 2, nch=2 * n1))
            qc += 1
    return sched


SCHED = make_sched()


class FakeInstr:
    def then_inc(self, *a, **k):
        return self


class FakeEng:
    def __getattr__(self, name):
        def f(*a, **k):
            return FakeInstr()
        return f


class SemName:
    def __init__(self, name):
        self.name = name


class Ctx:
    def __init__(self, sems, dry, labels, record=False):
        self.sems = sems
        self.dry = dry
        self.labels = labels
        self.record = record
        self.counts = {k: 0 for k in sems}

    def inc(self, instr, s, n=1, label=None):
        self.counts[s] += n
        if label is not None:
            key = (s,) + tuple(label)
            if self.dry:
                self.labels[key] = self.counts[s]
            else:
                assert self.labels.get(key) == self.counts[s], (
                    key, self.labels.get(key), self.counts[s])
        if not self.dry:
            instr.then_inc(self.sems[s], n)
        elif self.record:
            instr.then_inc(SemName(s), n)

    def alias(self, s, label):
        key = (s,) + tuple(label)
        if self.dry:
            self.labels[key] = self.counts[s]
        else:
            assert self.labels.get(key) == self.counts[s]

    def wait(self, eng, s, label):
        key = (s,) + tuple(label)
        if not self.dry or self.record:
            eng.wait_ge(SemName(s), self.labels[key]) if self.record else                 eng.wait_ge(self.sems[s], self.labels[key])

    def wait_if(self, eng, s, label):
        key = (s,) + tuple(label)
        if key in self.labels and (not self.dry or self.record):
            eng.wait_ge(SemName(s), self.labels[key]) if self.record else                 eng.wait_ge(self.sems[s], self.labels[key])


BMAP = (0, 1, 4)


def build_nc(stage=STAGE):
    nc = bass.Bass("TRN2", target_bir_lowering=False)

    x_in = nc.dram_tensor("x_in", [32, P, D], F32, kind="ExternalInput").ap()
    qkv_in = nc.dram_tensor("qkv_in", [8, P, 3, 16, 64], F32, kind="ExternalInput").ap()
    qb_in = nc.dram_tensor("qb_in", [16, 64], F32, kind="ExternalInput").ap()
    pos_in = nc.dram_tensor("pos_in", [P, 2048], F32, kind="ExternalInput").ap()
    wo_in = nc.dram_tensor("wo_in", [8, P, D], F32, kind="ExternalInput").ap()
    ob_in = nc.dram_tensor("ob_in", [1, D], F32, kind="ExternalInput").ap()
    wr_in = nc.dram_tensor("wr_in", [8, P, 8], F32, kind="ExternalInput").ap()
    wfc_in = nc.dram_tensor("wfc_in", [9, 8, P, 4096], F32, kind="ExternalInput").ap()
    wpj_in = nc.dram_tensor("wpj_in", [9, 32, P, D], F32, kind="ExternalInput").ap()
    bfc_in = nc.dram_tensor("bfc_in", [9, 32, P], F32, kind="ExternalInput").ap()
    bpe_in = nc.dram_tensor("bpe_in", [8, D], F32, kind="ExternalInput").ap()
    bps_in = nc.dram_tensor("bps_in", [1, D], F32, kind="ExternalInput").ap()
    sq_in = nc.dram_tensor("sq_in", [P, SEQ], F32, kind="ExternalInput").ap()
    cq_in = nc.dram_tensor("cq_in", [P, SEQ], F32, kind="ExternalInput").ap()
    dm_in = nc.dram_tensor("dm_in", [4, P, 512], F32, kind="ExternalInput").ap()
    om_in = nc.dram_tensor("om_in", [4, P, 512], F32, kind="ExternalInput").ap()
    id_in = nc.dram_tensor("id_in", [P, P], F32, kind="ExternalInput").ap()

    y_out = nc.dram_tensor("y_out", [NT, P, D], F32, kind="ExternalOutput").ap()
    x1_sp = nc.dram_tensor("x1_sp", [NT, P, D], F32, kind="Internal").ap()
    ot_sp = nc.dram_tensor("ot_sp", [NPAIR, P, 2048], BF16, kind="Internal").ap()
    l_sp = nc.dram_tensor("l_sp", [NPAIR, 2, 1, 2048], F32, kind="Internal").ap()

    st_all = ExitStack()
    e = st_all.enter_context
    sems = {k: e(nc.semaphore(k)) for k in ("Sd", "Gd", "Pe", "Ac", "Ve", "St")}

    id_f = e(nc.sbuf_tensor("id_f", [P, P], F32))
    id_b = e(nc.sbuf_tensor("id_b", [P, P], BF16))
    ones_f = e(nc.sbuf_tensor("ones_f", [1, P], F32))
    ones_b = e(nc.sbuf_tensor("ones_b", [1, P], BF16))
    qb2 = e(nc.sbuf_tensor("qb2", [P, 8], F32))
    qbT = e(nc.sbuf_tensor("qbT", [64, 16], F32))
    ob32 = e(nc.sbuf_tensor("ob32", [1, D], F32))
    epst = e(nc.sbuf_tensor("epst", [P, 1], F32))
    PeT_e = e(nc.sbuf_tensor("PeT_e", [P, 8 * P], BF16))
    PeT_o = e(nc.sbuf_tensor("PeT_o", [P, 8 * P], BF16))

    st_ab = ExitStack()
    ea = st_ab.enter_context
    XT = ea(nc.sbuf_tensor("XT", [P, 8 * SEQ], BF16))
    sq_b = ea(nc.sbuf_tensor("sq_b", [P, SEQ], BF16))
    cq_b = ea(nc.sbuf_tensor("cq_b", [P, SEQ], BF16))
    dm_sb = ea(nc.sbuf_tensor("dm_sb", [P, 4 * 512], BF16))
    om_sb = ea(nc.sbuf_tensor("om_sb", [P, 4 * 512], BF16))

    st_a = ExitStack()
    x_st = [st_a.enter_context(nc.sbuf_tensor(f"x_st{i}", [P, D], F32)) for i in range(2)]
    xb_st = [st_a.enter_context(nc.sbuf_tensor(f"xb_st{i}", [P, D], BF16)) for i in range(2)]
    pos_st = st_a.enter_context(nc.sbuf_tensor("pos_st", [P, 2048], F32))
    qb_st = st_a.enter_context(nc.sbuf_tensor("qb_st", [16, 64], F32))
    ob_st = st_a.enter_context(nc.sbuf_tensor("ob_st", [1, D], F32))
    psA_T = [st_a.enter_context(nc.psum_tensor(f"psA_T{i}", [P, 1024], BF16)) for i in range(2)]
    psA_G = [st_a.enter_context(nc.psum_tensor(f"psA_G{i}", [P, 512], F32)) for i in range(2)]
    st_a.close()

    st_b = ExitStack()
    eb = st_b.enter_context
    KT = eb(nc.sbuf_tensor("KT", [P, SEQ], BF16))
    Vtm = eb(nc.sbuf_tensor("Vtm", [P, 2 * 32 * 65], BF16))
    Qp = eb(nc.sbuf_tensor("Qp", [P, 2048], BF16))
    A1s = eb(nc.sbuf_tensor("A1s", [P, 2048], BF16))
    A2s = eb(nc.sbuf_tensor("A2s", [P, 2048], BF16))
    U1 = [eb(nc.sbuf_tensor(f"U1_{i}", [P, 2048], BF16)) for i in range(2)]
    U2 = [eb(nc.sbuf_tensor(f"U2_{i}", [P, 2048], BF16)) for i in range(2)]
    Utmp = eb(nc.sbuf_tensor("Utmp", [P, 2048], BF16))
    Ssb = eb(nc.sbuf_tensor("Ssb", [P, SEQ], F32))
    Asb = [eb(nc.sbuf_tensor(f"Asb{i}", [P, SEQ], BF16)) for i in range(2)]
    ATr = [eb(nc.sbuf_tensor(f"ATr{i}", [P, 512], BF16)) for i in range(2)]
    mxt = eb(nc.sbuf_tensor("mxt", [P, 24], F32))
    lbuf = eb(nc.sbuf_tensor("lbuf", [1, 2048], F32))
    otp = eb(nc.sbuf_tensor("otp", [P, 2048], BF16))
    wq_t = [eb(nc.sbuf_tensor(f"wq_t{i}", [P, 1024], BF16)) for i in range(2)]
    wk_t = [eb(nc.sbuf_tensor(f"wk_t{i}", [P, 1024], BF16)) for i in range(2)]
    wv_t = [eb(nc.sbuf_tensor(f"wv_t{i}", [P, 1024], BF16)) for i in range(2)]
    psB = [eb(nc.psum_tensor(f"psB{i}", [P, 512], F32)) for i in range(6)]
    psB_T = [eb(nc.psum_tensor(f"psB_T{i}", [P, 512], BF16)) for i in range(2)]
    st_b.close()
    st_ab.close()

    st_cd = ExitStack()
    ec = st_cd.enter_context
    X1T = ec(nc.sbuf_tensor("X1T", [P, 8 * 2048], BF16))
    gate = ec(nc.sbuf_tensor("gate", [P, NT * 9], F32))
    gT = ec(nc.sbuf_tensor("gT", [8, 2048], BF16))

    st_c = ExitStack()
    cc = st_c.enter_context
    ot_ld = [cc(nc.sbuf_tensor(f"ot_ld{i}", [P, 2048], BF16)) for i in range(NPAIR)]
    wo_b = cc(nc.sbuf_tensor("wo_b", [P, 8 * D], BF16))
    wr_b = cc(nc.sbuf_tensor("wr_b", [P, 64], BF16))
    h1 = [cc(nc.sbuf_tensor(f"h1_{i}", [P, D], F32)) for i in range(4)]
    x1scr = [cc(nc.sbuf_tensor(f"x1scr{i}", [P, D], F32)) for i in range(2)]
    x1b = [cc(nc.sbuf_tensor(f"x1b{i}", [P, D], BF16)) for i in range(4)]
    sqscr = cc(nc.sbuf_tensor("sqscr", [P, D], F32))
    xre = [cc(nc.sbuf_tensor(f"xre{i}", [P, D], F32)) for i in range(2)]
    lld = cc(nc.sbuf_tensor("lld", [1, 2048], F32))
    lrc = cc(nc.sbuf_tensor("lrc", [1, 2048], F32))
    rbc = [cc(nc.sbuf_tensor(f"rbc{i}", [P, 512], BF16)) for i in range(2)]
    l_all = cc(nc.sbuf_tensor("l_all", [P, NT * 8], F32))
    stC = cc(nc.sbuf_tensor("stC", [P, NT * 8], F32))
    gch = cc(nc.sbuf_tensor("gch", [P, NT * 40], F32))
    psC = [cc(nc.psum_tensor(f"psC{i}", [P, 512], F32)) for i in range(3)]
    psC_T = [cc(nc.psum_tensor(f"psC_T{i}", [P, 1024], BF16)) for i in range(4)]
    st_c.close()

    st_d = ExitStack()
    dd = st_d.enter_context
    Hb = dd(nc.sbuf_tensor("Hb", [P, 32 * 1024], BF16))
    routed = dd(nc.sbuf_tensor("routed", [P, 8 * D], F32))
    xf = [dd(nc.sbuf_tensor(f"xf{i}", [P, D], F32)) for i in range(2)]
    wscr = [dd(nc.sbuf_tensor(f"wscr{i}", [P, D], F32)) for i in range(2)]
    tscr = [dd(nc.sbuf_tensor(f"tscr{i}", [P, D], F32)) for i in range(2)]
    sgscr = [dd(nc.sbuf_tensor(f"sgscr{i}", [P, D], F32)) for i in range(2)]
    wfc_t = [dd(nc.sbuf_tensor(f"wfc_t{i}", [P, 1024], BF16)) for i in range(4)]
    wpj_t = [dd(nc.sbuf_tensor(f"wpj_t{i}", [P, 512], BF16)) for i in range(4)]
    bfc_t = [dd(nc.sbuf_tensor(f"bfc_t{i}", [P, 32], F32)) for i in range(2)]
    bfc_st = [dd(nc.sbuf_tensor(f"bfc_st{i}", [32, P], F32)) for i in range(2)]
    bpall = dd(nc.sbuf_tensor("bpall", [8, D], BF16))
    bpsh = dd(nc.sbuf_tensor("bpsh", [1, D], BF16))
    x1re = [dd(nc.sbuf_tensor(f"x1re{i}", [P, D], F32)) for i in range(2)]
    yst = [dd(nc.sbuf_tensor(f"yst{i}", [P, D], F32)) for i in range(2)]
    stE = dd(nc.sbuf_tensor("stE", [P, 8 * 8], F32))
    psD = [dd(nc.psum_tensor(f"psD{i}", [P, 512], F32)) for i in range(8)]
    st_d.close()
    st_cd.close()

    labels = {}
    ntb = NT if B_MAXQT is None else B_MAXQT
    SCHEDL = make_sched(ntb)

    def phA_sync(g, c):
        c.inc(g.dma_start(id_f[:, :], id_in), "Sd", 16, ("idf",))

        c.inc(g.dma_start(pos_st[:, :], pos_in), "Sd", 16, ("pos",))
        c.inc(g.dma_start(qb_st[:, :], qb_in), "Sd", 16, ("qb",))
        c.inc(g.dma_start(ob_st[:, :], ob_in), "Sd", 16, ("ob",))
        for i in range(32):
            if i >= 2:
                c.wait(g, "Ve", ("Acast", i - 2))
            c.inc(g.dma_start(x_st[i % 2][:, :], x_in[i]), "Sd", 16, ("Axd", i))

    def phA_gp(g, c):
        c.inc(g.dma_start(dm_sb[:, :].rearrange("p (v f) -> p v f", v=4),
                          dm_in.rearrange("v p f -> p v f")), "Gd", 16, ("dm",))
        c.inc(g.dma_start(om_sb[:, :].rearrange("p (v f) -> p v f", v=4),
                          om_in.rearrange("v p f -> p v f")), "Gd", 16, ("om",))
        c.inc(g.dma_start(id_b[:, :], id_in), "Gd", 16, ("idb",))
        c.inc(g.dma_start(sq_b[:, :], sq_in), "Gd", 16, ("sqb",))
        c.inc(g.dma_start(cq_b[:, :], cq_in), "Gd", 16, ("cqb",))

    def phA_ve(v, c):
        v.memset(ones_f[:, :], 1.0)
        v.memset(epst[:, :], EPS)
        c.inc(v.memset(ones_b[:, :], 1.0), "Ve", 1, ("Aones",))
        for i in range(32):
            c.wait(v, "Sd", ("Axd", i))
            if i >= 2:
                c.wait(v, "Pe", ("Atr", i - 2))
            c.inc(v.tensor_copy(xb_st[i % 2][:, :], x_st[i % 2][:, :]), "Ve", 1, ("Acast", i))

    def phA_pe(t, c):
        c.wait(t, "Gd", ("idb",))
        for i in range(32):
            c.wait(t, "Ve", ("Acast", i))
            if i >= 2:
                c.wait(t, "Ac", ("Aev", i - 2))
            ins = None
            for dc in range(8):
                ins = t.transpose(psA_T[i % 2][:, dc * P:(dc + 1) * P],
                                  xb_st[i % 2][:, dc * P:(dc + 1) * P], id_b[:, :])
            c.inc(ins, "Pe", 1, ("Atr", i))
        c.wait(t, "Sd", ("pos",))
        c.wait(t, "Sd", ("idf",))
        for par in range(2):
            for n in range(16):
                j = par * 16 + n
                c.wait_if(t, "Ac", ("Apev", j - 2))
                ins = t.transpose(psA_G[j % 2][0:64, 0:P],
                                  pos_st[:, par * 1024 + n * 64: par * 1024 + (n + 1) * 64],
                                  id_f[:, :])
                c.inc(ins, "Pe", 1, ("Aptr", j))
        c.wait(t, "Sd", ("qb",))
        c.wait(t, "Ac", ("Apev", 30))
        c.inc(t.transpose(psA_G[0][0:64, 0:16], qb_st[0:16, 0:64], id_f[0:16, 0:16]),
              "Pe", 1, ("Aqbt",))

    def phA_ac(a, c):
        for i in range(32):
            c.wait(a, "Pe", ("Atr", i))
            ins = a.copy(
                XT[:, :].rearrange("p (dc f) -> p dc f", dc=8)[:, :, i * P:(i + 1) * P],
                psA_T[i % 2][:, :].rearrange("p (dc f) -> p dc f", dc=8))
            c.inc(ins, "Ac", 1, ("Aev", i))
        for par in range(2):
            for n in range(16):
                j = par * 16 + n
                c.wait(a, "Pe", ("Aptr", j))
                dst = PeT_e if par == 0 else PeT_o
                pp, hp = n // 2, n % 2
                ins = a.activation(dst[64 * hp:64 * hp + 64, pp * P:(pp + 1) * P],
                                   psA_G[j % 2][0:64, 0:P], AF.Identity, scale=C_POS)
                c.inc(ins, "Ac", 1, ("Apev", j))
        c.wait(a, "Pe", ("Aqbt",))
        a.activation(qbT[0:64, 0:16], psA_G[0][0:64, 0:16], AF.Identity, scale=0.125)
        for n in range(16):
            pp, hp = n // 2, n % 2
            a.copy(qb2[64 * hp:64 * hp + 64, pp:pp + 1], qbT[0:64, n:n + 1])
        c.wait(a, "Sd", ("ob",))
        c.inc(a.mul(ob32[0:1, :], ob_st[0:1, :], 32.0), "Ac", 1, ("Aconsts",))

    def phB_gp(g, c, p):
        buf = p % 2
        for Aidx, wt in ((0, wq_t), (1, wk_t), (2, wv_t)):
            if p >= 2:
                c.wait(g, "Pe", ("Bwdone", p - 2))
            src = qkv_in[:, :, Aidx, 2 * p:2 * p + 2, :].rearrange("dc p n h -> p dc (n h)")
            ins = g.dma_start(wt[buf][:, :].rearrange("p (dc f) -> p dc f", dc=8), src)
            c.inc(ins, "Gd", 16, ("Bw", p, Aidx))

    def phB_pe(t, c, p):
        buf = p % 2
        for Aidx in range(3):
            c.wait(t, "Gd", ("Bw", p, Aidx))
        if p == 0:
            c.wait(t, "Gd", ("sqb",))
            c.wait(t, "Gd", ("cqb",))
            c.wait(t, "Ve", ("Bvmemset",))
            c.wait(t, "Ac", ("Aconsts",))
            c.wait(t, "Ve", ("Aones",))
        for tch in range(8 if "k" in B_PRE else 0):
            c.wait_if(t, "Ac", ("Bkev", p, tch - 2))
            ins = None
            for dc in range(8):
                ins = t.matmul(psB[4 + (tch % 2)][:, :], wk_t[buf][:, dc * P:(dc + 1) * P],
                               XT[:, dc * SEQ + tch * 512: dc * SEQ + (tch + 1) * 512],
                               start=(dc == 0), stop=(dc == 7))
            c.inc(ins, "Pe", 1, ("Bkmm", p, tch))
        for grp in range(8 if "v" in B_PRE else 0):
            if grp < 2:
                c.wait_if(t, "Ac", ("Bkev", p, 6 + grp))
            c.wait_if(t, "Ve", ("Bvev", p, grp - 2))
            ins = None
            for ktl in range(4):
                kt = grp * 4 + ktl
                for dc in range(8):
                    ins = t.matmul(psB[4 + (grp % 2)][:, ktl * P:(ktl + 1) * P],
                                   XT[:, dc * SEQ + kt * P: dc * SEQ + (kt + 1) * P],
                                   wv_t[buf][:, dc * P:(dc + 1) * P],
                                   start=(dc == 0), stop=(dc == 7))
            c.inc(ins, "Pe", 1, ("Bvmm", p, grp))
        for tcg in range(4 if "q" in B_PRE else 0):
            if tcg < 2:
                c.wait_if(t, "Ve", ("Bvev", p, 6 + tcg))
            c.wait_if(t, "Ac", ("Bqev", p, tcg - 2))
            ins = None
            for dc in range(8):
                ins = t.matmul(psB[4 + (tcg % 2)][:, :], wq_t[buf][:, dc * P:(dc + 1) * P],
                               XT[:, dc * SEQ + tcg * 512: dc * SEQ + (tcg + 1) * 512],
                               start=(dc == 0), stop=(dc == 7))
            c.inc(ins, "Pe", 1, ("Bqmm", p, tcg))
        c.alias("Pe", ("Bwdone", p))

        def scores_emit(rec):
            hp, qt = rec["hp"], rec["qt"]
            for ch in rec["chunks"]:
                sc, bank = ch["sc"], BMAP[ch["bank"]]
                c.wait_if(t, "Ac", ("BsevA", p, sc - 3))
                c.wait_if(t, "Ve", ("BsevV", p, sc - 3))
                c.wait_if(t, "Ve", ("Bmax", p, sc - 3))
                t.matmul(psB[bank][:, :],
                         Qp[64 * hp:64 * hp + 64, qt * P:(qt + 1) * P],
                         KT[64 * hp:64 * hp + 64, ch["base"]:ch["base"] + 512],
                         start=True, stop=False)
                t.matmul(psB[bank][:, :], U1[hp][:, qt * P:(qt + 1) * P],
                         cq_b[:, ch["base"]:ch["base"] + 512], start=False, stop=False)
                ins = t.matmul(psB[bank][:, :], U2[hp][:, qt * P:(qt + 1) * P],
                               sq_b[:, ch["base"]:ch["base"] + 512], start=False, stop=True)
                c.inc(ins, "Pe", 1, ("Bsmm", p, sc))

        def pv_group(rec, k, first, last):
            gr = rec["groups"][k]
            hp, obank = rec["hp"], rec["obank"]
            c.wait(t, "Ac", ("Batev", p, gr["gt"]))
            if first:
                c.wait_if(t, "Ve", ("Botev", p, rec["qc"] - 2))
            ins = None
            for i4 in range(4):
                ktile = gr["base"] // P + i4
                ins = t.matmul(psB[obank][0:65, 0:P],
                               Vtm[:, hp * 2080 + ktile * 65: hp * 2080 + ktile * 65 + 65],
                               ATr[gr["bank"]][:, i4 * P:(i4 + 1) * P],
                               start=(first and i4 == 0), stop=(last and i4 == 3))
            if last:
                c.inc(ins, "Pe", 1, ("Bpv", p, rec["qc"]))

        def pv_emit(rec):
            qt, qc = rec["qt"], rec["qc"]
            c.wait(t, "Ac", ("Bexp", p, qc))
            groups = rec["groups"]
            for k, gr in enumerate(groups):
                if gr["gt"] >= 2:
                    c.wait_if(t, "Ac", ("Batev", p, gr["gt"] - 2))
                ins = None
                for i4 in range(4):
                    ins = t.transpose(psB_T[gr["bank"]][:, i4 * P:(i4 + 1) * P],
                                      Asb[qt % 2][:, gr["base"] + i4 * P: gr["base"] + (i4 + 1) * P],
                                      id_b[:, :])
                c.inc(ins, "Pe", 1, ("Btr", p, gr["gt"]))
                if k >= 1:
                    pv_group(rec, k - 1, first=(k - 1 == 0), last=False)
            pv_group(rec, len(groups) - 1, first=(len(groups) == 1), last=True)

        for hp in range(2 if "a" in B_PRE else 0):
            c.wait_if(t, "Ac", ("Bqev", p, 3))
            if hp == 1:
                c.wait_if(t, "Ve", ("BUdone", p, 0))
                hp0_scs = [ch["sc"] for r2 in SCHEDL if r2["hp"] == 0 for ch in r2["chunks"]]
                b4 = [s2 for s2 in hp0_scs if s2 % 3 == 2]
                if b4:
                    c.wait_if(t, "Ac", ("BsevA", p, b4[-1]))
                    c.wait_if(t, "Ve", ("BsevV", p, b4[-1]))
                    c.wait_if(t, "Ve", ("Bmax", p, b4[-1]))
            for par, PeTx in ((0, PeT_e), (1, PeT_o)):
                for tcg in range(4):
                    j = par * 4 + tcg
                    if hp == 1 and j < 2:
                        c.wait_if(t, "Ac", ("Baev", p, 0, 6 + j))
                    c.wait_if(t, "Ac", ("Baev", p, hp, j - 2))
                    ins = t.matmul(psB[4 + (j % 2)][:, :],
                                   PeTx[64 * hp:64 * hp + 64, p * P:(p + 1) * P],
                                   Qp[64 * hp:64 * hp + 64, tcg * 512:(tcg + 1) * 512],
                                   start=True, stop=True)
                    c.inc(ins, "Pe", 1, ("Bamm", p, hp, j))
            c.wait_if(t, "Ac", ("Bkev", p, 7))
            c.wait_if(t, "Ve", ("BUdone", p, hp))
            if hp == 0:
                c.wait_if(t, "Ve", ("Bvev", p, 7))

            recs = [r for r in SCHEDL if r["hp"] == hp]
            if recs:
                scores_emit(recs[0])
                for i, rec in enumerate(recs):
                    if i + 1 < len(recs):
                        scores_emit(recs[i + 1])
                    pv_emit(rec)

    def phB_ac(a, c, p):
        for tch in range(8 if "k" in B_PRE else 0):
            c.wait(a, "Pe", ("Bkmm", p, tch))
            ins = a.activation(KT[:, tch * 512:(tch + 1) * 512], psB[4 + (tch % 2)][:, :],
                               AF.Identity, scale=S_QKV)
            c.inc(ins, "Ac", 1, ("Bkev", p, tch))
        for tcg in range(4 if "q" in B_PRE else 0):
            c.wait(a, "Pe", ("Bqmm", p, tcg))
            ins = a.activation(Qp[:, tcg * 512:(tcg + 1) * 512], psB[4 + (tcg % 2)][:, :],
                               AF.Identity, bias=qb2[:, p:p + 1], scale=SC_Q)
            c.inc(ins, "Ac", 1, ("Bqev", p, tcg))
        def ac_tail(rec):
            qc = rec["qc"]
            for gr in rec["groups"]:
                c.wait(a, "Pe", ("Btr", p, gr["gt"]))
                c.inc(a.copy(ATr[gr["bank"]][:, :], psB_T[gr["bank"]][:, :]),
                      "Ac", 1, ("Batev", p, gr["gt"]))
            c.wait(a, "Pe", ("Bpv", p, qc))
            hp2, qt2 = rec["hp"], rec["qt"]
            if qt2 == 0:
                if hp2 == 1:
                    c.wait(a, "St", ("Bl_st", p, 0))
                elif p >= 1:
                    c.wait(a, "St", ("Bl_st", p - 1, 1))
            ins = a.copy(lbuf[0:1, qt2 * P:(qt2 + 1) * P],
                         psB[rec["obank"]][64:65, 0:P])
            c.inc(ins, "Ac", 1, ("Bl", p, qc))

        for hp in range(2 if "a" in B_PRE else 0):
            for j in range(8):
                c.wait(a, "Pe", ("Bamm", p, hp, j))
                if hp == 1 and j == 0:
                    c.wait_if(a, "Ve", ("BUdone", p, 0))
                dst = (A1s if j < 4 else A2s)[:, (j % 4) * 512:(j % 4 + 1) * 512]
                c.inc(a.copy(dst, psB[4 + (j % 2)][:, :]), "Ac", 1, ("Baev", p, hp, j))
            recs = [r for r in SCHEDL if r["hp"] == hp]
            for rec in recs:
                qt, qc = rec["qt"], rec["qc"]
                for ch in rec["chunks"]:
                    if ch["masked"]:
                        continue
                    c.wait(a, "Pe", ("Bsmm", p, ch["sc"]))
                    ins = a.copy(Ssb[:, ch["base"]:ch["base"] + 512], psB[BMAP[ch["bank"]]][:, :])
                    c.inc(ins, "Ac", 1, ("BsevA", p, ch["sc"]))
                c.wait(a, "Ve", ("Bnegmax", p, qc))
                if qc >= 2:
                    c.wait_if(a, "Pe", ("Btr", p, SCHEDL[qc - 2]["groups"][-1]["gt"]))
                w = kch_of(qt) * 512
                a.activation(Asb[qt % 2][:, 0:w], Ssb[:, 0:w], AF.Exp,
                             bias=mxt[:, qt:qt + 1], scale=1.0)
                ins = a.activation(Asb[qt % 2][:, 2048:2048 + w], Ssb[:, 2048:2048 + w],
                                   AF.Exp, bias=mxt[:, qt:qt + 1], scale=1.0)
                c.inc(ins, "Ac", 1, ("Bexp", p, qc))
                if qc > (0 if hp == 0 else ntb):
                    ac_tail(SCHEDL[qc - 1])
            if recs:
                ac_tail(recs[-1])

    def phB_ve(v, c, p):
        if p == 0:
            c.inc(v.memset(Vtm[:, :], 1.0), "Ve", 1, ("Bvmemset",))
        for grp in range(8 if "v" in B_PRE else 0):
            c.wait(v, "Pe", ("Bvmm", p, grp))
            ins = None
            for hp in range(2):
                src = psB[4 + (grp % 2)][:, :].rearrange("p (i f) -> p i f", i=4)[:, :, hp * 64:hp * 64 + 64]
                dstc = Vtm[:, hp * 2080 + grp * 4 * 65: hp * 2080 + (grp * 4 + 4) * 65]
                dst = dstc.rearrange("p (i f) -> p i f", i=4)[:, :, 0:64]
                ins = v.tensor_scalar(dst, src, S_QKV, None, ALU.mult)
            c.inc(ins, "Ve", 1, ("Bvev", p, grp))
        for hp in range(2 if "u" in B_PRE else 0):
            c.wait(v, "Ac", ("Baev", p, hp, 7))
            for tcg in range(4):
                v.tensor_tensor(U1[hp][:, tcg * 512:(tcg + 1) * 512],
                                A1s[:, tcg * 512:(tcg + 1) * 512],
                                sq_b[:, tcg * 512:(tcg + 1) * 512], ALU.mult)
            for tcg in range(4):
                v.tensor_tensor(Utmp[:, tcg * 512:(tcg + 1) * 512],
                                A2s[:, tcg * 512:(tcg + 1) * 512],
                                cq_b[:, tcg * 512:(tcg + 1) * 512], ALU.mult)
            for tcg in range(4):
                v.tensor_tensor(U1[hp][:, tcg * 512:(tcg + 1) * 512],
                                U1[hp][:, tcg * 512:(tcg + 1) * 512],
                                Utmp[:, tcg * 512:(tcg + 1) * 512], ALU.add)
            for tcg in range(4):
                v.tensor_tensor(U2[hp][:, tcg * 512:(tcg + 1) * 512],
                                A2s[:, tcg * 512:(tcg + 1) * 512],
                                sq_b[:, tcg * 512:(tcg + 1) * 512], ALU.mult)
            for tcg in range(4):
                v.tensor_tensor(Utmp[:, tcg * 512:(tcg + 1) * 512],
                                A1s[:, tcg * 512:(tcg + 1) * 512],
                                cq_b[:, tcg * 512:(tcg + 1) * 512], ALU.mult)
            ins = None
            for tcg in range(4):
                ins = v.tensor_tensor(U2[hp][:, tcg * 512:(tcg + 1) * 512],
                                      U2[hp][:, tcg * 512:(tcg + 1) * 512],
                                      Utmp[:, tcg * 512:(tcg + 1) * 512], ALU.subtract)
            c.inc(ins, "Ve", 1, ("BUdone", p, hp))

            def ve_tail(rec):
                qc2, qt2, hp2 = rec["qc"], rec["qt"], rec["hp"]
                c.wait(v, "Ac", ("Bl", p, qc2))
                if p >= 1 and qc2 == 0:
                    c.wait(v, "St", ("Bot_st", p - 1))
                ins2 = v.tensor_copy(otp[64 * hp2:64 * hp2 + 64, qt2 * P:(qt2 + 1) * P],
                                     psB[rec["obank"]][0:64, 0:P])
                c.inc(ins2, "Ve", 1, ("Botev", p, qc2))

            recs = [r for r in SCHEDL if r["hp"] == hp]
            for rec in recs:
                qt, qc = rec["qt"], rec["qc"]
                for ci, ch in enumerate(rec["chunks"]):
                    c.wait(v, "Pe", ("Bsmm", p, ch["sc"]))
                    if ch["masked"]:
                        if qc >= 1:
                            c.wait_if(v, "Ac", ("Bexp", p, qc - 1))
                        mvar = qt % 4
                        msrc = dm_sb if ch["side"] == "own" else om_sb
                        ins = v.tensor_tensor(Ssb[:, ch["base"]:ch["base"] + 512],
                                              psB[BMAP[ch["bank"]]][:, :],
                                              msrc[:, mvar * 512:(mvar + 1) * 512], ALU.add)
                        c.inc(ins, "Ve", 1, ("BsevV", p, ch["sc"]))
                        ins = v.reduce_max(mxt[:, 16 + ci:17 + ci], psB[BMAP[ch["bank"]]][:, :], axis=AX.X)
                    else:
                        c.wait(v, "Ac", ("BsevA", p, ch["sc"]))
                        ins = v.reduce_max(mxt[:, 16 + ci:17 + ci],
                                           Ssb[:, ch["base"]:ch["base"] + 512], axis=AX.X)
                    c.inc(ins, "Ve", 1, ("Bmax", p, ch["sc"]))
                v.drain()
                ins = v.reduce_max(mxt[:, qt:qt + 1], mxt[:, 16:16 + rec["nch"]],
                                   axis=AX.X, negate=True)
                c.inc(ins, "Ve", 1, ("Bnegmax", p, qc))
                if qc > (0 if hp == 0 else ntb):
                    ve_tail(SCHEDL[qc - 1])
            if recs:
                ve_tail(recs[-1])

    def phB_sync(g, c, p):
        if B_MAXQT == 0:
            return
        c.wait(g, "Ac", ("Bl", p, ntb - 1))
        c.inc(g.dma_start(l_sp[p, 0], lbuf[0:1, :]), "St", 16, ("Bl_st", p, 0))
        c.wait(g, "Ve", ("Botev", p, 2 * ntb - 1))
        c.inc(g.dma_start(ot_sp[p], otp[:, :]), "St", 16, ("Bot_st", p))
        c.wait(g, "Ac", ("Bl", p, 2 * ntb - 1))
        c.inc(g.dma_start(l_sp[p, 1], lbuf[0:1, :]), "St", 16, ("Bl_st", p, 1))

    def phC_sync(g, c):
        for p in range(NPAIR):
            c.inc(g.dma_start(ot_ld[p][:, :], ot_sp[p]), "Sd", 16, ("Cot", p))
        for p in range(NPAIR):
            for hp in range(2):
                if (p, hp) != (0, 0):
                    ph = (p, hp - 1) if hp else (p - 1, 1)
                    c.wait(g, "Ve", ("Crecip",) + ph)
                c.inc(g.dma_start(lld[0:1, :], l_sp[p, hp]), "Sd", 16, ("Cl", p, hp))
        do_norm = C_LEVEL >= 2 and ('spill' in C_SUB or C_SUB == 'all')

        def spill(t_):
            if WAC is not None:
                if not c.dry:
                    g.wait_ge(sems["Ac"], WAC)
                elif c.record:
                    g.wait_ge(SemName("Ac"), WAC)
            elif C_SUB != 'all' and 'wAc1700' in C_SUB:
                c.wait(g, "Ac", ("Baev", 3, 1, 7))
            elif C_SUB != 'all' and 'wAc800' in C_SUB:
                c.wait(g, "Ac", ("Baev", 1, 1, 7))
            elif C_SUB == 'all' or 'nonormwait' not in C_SUB:
                c.wait(g, "Ac", ("Cnorm", t_))
            if C_SUB == 'all' or 'nox1sp' not in C_SUB:
                c.inc(g.dma_start(x1_sp[t_], x1scr[t_ % 2][:, :]), "St", 16, ("Cx1sp", t_))
            if stage == "attn" and (C_SUB == 'all' or 'noydbg' not in C_SUB):
                if C_SUB != 'all' and 'srch1' in C_SUB:
                    srcap = h1[t_ % 4][:, :]
                elif C_SUB != 'all' and 'srcxre' in C_SUB:
                    srcap = xre[t_ % 2][:, :]
                else:
                    srcap = x1scr[t_ % 2][:, :]
                c.inc(g.dma_start(y_out[t_], srcap), "St", 16, ("Cydbg", t_))

        for t_ in range(NT if C_LEVEL >= 2 else 0):
            if t_ >= 2:
                c.wait(g, "Ve", ("Cadd", t_ - 2))
            c.inc(g.dma_start(xre[t_ % 2][:, :], x_in[t_]), "Sd", 16, ("Cxre", t_))
            if do_norm and t_ >= 4:
                spill(t_ - 4)
        if do_norm:
            for t_ in range(NT - 4, NT):
                spill(t_)
        if stage == "attn":
            if not c.dry:
                g.wait_ge(sems["St"], c.counts["St"])
            elif c.record:
                g.wait_ge(SemName("St"), c.counts["St"])

    def phC_gp(g, c):
        c.inc(g.dma_start(wo_b[:, :].rearrange("p (dc f) -> p dc f", dc=8),
                          wo_in.rearrange("dc p f -> p dc f")), "Gd", 16, ("Cwo",))
        c.inc(g.dma_start(wr_b[:, :].rearrange("p (dc f) -> p dc f", dc=8),
                          wr_in.rearrange("dc p f -> p dc f")), "Gd", 16, ("Cwr",))
        c.inc(g.dma_start(bpall[:, :], bpe_in), "Gd", 16, ("Cbpe",))
        c.inc(g.dma_start(bpsh[:, :], bps_in), "Gd", 16, ("Cbps",))

    def emit_C_tr_logits(t, c, tl):
        for t_ in tl:
            c.wait(t, "Ac", ("Cx1b", t_))
            if t_ >= 4:
                c.wait(t, "Ac", ("Cx1tev", t_ - 4))
            ins = None
            for dc in range(8):
                ins = t.transpose(psC_T[t_ % 4][:, dc * P:(dc + 1) * P],
                                  x1b[t_ % 4][:, dc * P:(dc + 1) * P], id_b[:, :])
            c.inc(ins, "Pe", 1, ("Cx1t", t_))
        for t_ in (tl if C_LEVEL >= 4 else []):
            c.wait(t, "Ac", ("Cx1tev", t_))
            c.wait(t, "Gd", ("Cwr",))
            c.wait_if(t, "Ac", ("Clsb", t_ - 1))
            ins = None
            for dc in range(8):
                ins = t.matmul(psC[2][:, 0:8],
                               X1T[:, dc * 2048 + t_ * P: dc * 2048 + (t_ + 1) * P],
                               wr_b[:, dc * 8:(dc + 1) * 8], start=(dc == 0), stop=(dc == 7))
            c.inc(ins, "Pe", 1, ("Clog", t_))

    def phC_pe(t, c):
        for p in range(NPAIR):
            c.wait(t, "Sd", ("Cot", p))
        for p in range(NPAIR):
            for hp in range(2):
                c.wait(t, "Ve", ("Crecip", p, hp))
                for ch in range(4):
                    j = (p * 8 + hp * 4 + ch)
                    c.wait_if(t, "Ac", ("Crbev", j - 2))
                    ins = t.matmul(psC[j % 2][0:64, 0:512], ones_f[0:1, 0:64],
                                   lrc[0:1, ch * 512:(ch + 1) * 512],
                                   start=True, stop=True)
                    c.inc(ins, "Pe", 1, ("Crb", j))
        c.wait(t, "Ve", ("Cnot", NPAIR * 8 - 1))
        c.wait(t, "Gd", ("Cwo",))
        for w in range(4):
            tl = [4 * w + i for i in range(4)]
            for t_ in tl:
                for fo in range(2):
                    bank = (2 * t_ + fo) % 2
                    c.wait_if(t, "Ac", ("Copev", t_ - 1, fo))
                    ins = t.matmul(psC[bank][:, :], ones_f[0:1, 0:P],
                                   ob32[0:1, fo * 512:(fo + 1) * 512], start=True, stop=False)
                    for dc in range(8):
                        ins = t.matmul(psC[bank][:, :], ot_ld[dc][:, t_ * P:(t_ + 1) * P],
                                       wo_b[:, dc * D + fo * 512: dc * D + fo * 512 + 512],
                                       start=False, stop=(dc == 7))
                    c.inc(ins, "Pe", 1, ("Copmm", t_, fo))
            if w >= 1 and C_LEVEL >= 3:
                emit_C_tr_logits(t, c, [4 * (w - 1) + i for i in range(4)])
        if C_LEVEL >= 3:
            emit_C_tr_logits(t, c, [12, 13, 14, 15])
        for t_ in range(NT if C_LEVEL >= 5 else 0):
            c.wait(t, "Ve", ("Cgate", t_))
            c.wait(t, "Ac", ("Clsb", NT - 1))
            c.wait_if(t, "Ac", ("CgTev", t_ - 1))
            ins = t.transpose(psC[2][0:8, (t_ % 2) * P:(t_ % 2) * P + P],
                              gate[:, t_ * 9:t_ * 9 + 8], id_f[:, :])
            c.inc(ins, "Pe", 1, ("CgT", t_))

    def phC_ac(a, c):
        def col(t_, r):
            return stC[:, t_ * 8 + r: t_ * 8 + r + 1]
        for j in range(NPAIR * 8):
            c.wait(a, "Pe", ("Crb", j))
            c.wait_if(a, "Ve", ("Cnot", j - 2))
            hp = (j // 4) % 2
            ins = a.copy(rbc[j % 2][64 * hp:64 * hp + 64, 0:512], psC[j % 2][0:64, 0:512])
            c.inc(ins, "Ac", 1, ("Crbev", j))
        for w in range(4):
            tl = [4 * w + i for i in range(4)]
            for t_ in tl:
                for fo in range(2):
                    bank = (2 * t_ + fo) % 2
                    c.wait(a, "Pe", ("Copmm", t_, fo))
                    ins = a.activation(h1[t_ % 4][:, fo * 512:(fo + 1) * 512],
                                       psC[bank][:, :], AF.Identity, scale=SC_WO)
                    c.inc(ins, "Ac", 1, ("Copev", t_, fo))
            if 'sq' in C_SUB or C_SUB == 'all':
                for t_ in (tl if C_LEVEL >= 2 else []):
                    c.wait(a, "Ve", ("Cadd", t_))
                    ins = a.activation(sqscr[:, :], h1[t_ % 4][:, :], AF.Square, accum_out=col(t_, 5))
                    c.inc(ins, "Ac", 1, ("Csq", t_))
            if 'mm2' in C_SUB or C_SUB == 'all':
                for t_ in (tl if C_LEVEL >= 2 else []):
                    c.wait(a, "Ve", ("Cm", t_))
                    c.inc(a.activation(col(t_, 6), col(t_, 1), AF.Square), "Ac", 1, ("Cmm2", t_))
            if 'rstd' in C_SUB or C_SUB == 'all':
                for t_ in (tl if C_LEVEL >= 2 else []):
                    c.wait(a, "Ve", ("Cvar", t_))
                    a.activation(col(t_, 3), col(t_, 2), AF.Ln, bias=epst[:, 0:1], scale=1.0)
                    c.inc(a.activation(col(t_, 3), col(t_, 3), AF.Exp, scale=-0.5),
                          "Ac", 1, ("Crstd", t_))
            if 'norm' in C_SUB or C_SUB == 'all':
                for t_ in (tl if C_LEVEL >= 2 else []):
                    if C_SUB == 'all' or 'nowait' not in C_SUB:
                        c.wait(a, "Ve", ("Cbb", t_))
                    if t_ >= 2:
                        c.wait_if(a, "St", ("Cx1sp", t_ - 2))
                        c.wait_if(a, "St", ("Cydbg", t_ - 2))
                    if C_SUB != 'all' and 'sqdst' in C_SUB:
                        ins = a.copy(sqscr[:, :], h1[t_ % 4][:, :])
                    elif C_SUB != 'all' and 'plain' in C_SUB:
                        ins = a.copy(x1scr[t_ % 2][:, :], h1[t_ % 4][:, :])
                    else:
                        ins = a.activation(x1scr[t_ % 2][:, :], h1[t_ % 4][:, :], AF.Identity,
                                           bias=col(t_, 4), scale=col(t_, 3))
                    c.inc(ins, "Ac", 1, ("Cnorm", t_))
                    if C_LEVEL >= 3:
                        if t_ >= 4:
                            c.wait(a, "Pe", ("Cx1t", t_ - 4))
                        c.inc(a.copy(x1b[t_ % 4][:, :], x1scr[t_ % 2][:, :]), "Ac", 1, ("Cx1b", t_))
            tlp = [t_ - 4 for t_ in tl if t_ - 4 >= 0]
            for t_ in (tlp if C_LEVEL >= 3 else []):
                c.wait(a, "Pe", ("Cx1t", t_))
                ins = a.copy(
                    X1T[:, :].rearrange("p (dc f) -> p dc f", dc=8)[:, :, t_ * P:(t_ + 1) * P],
                    psC_T[t_ % 4][:, :].rearrange("p (dc f) -> p dc f", dc=8))
                c.inc(ins, "Ac", 1, ("Cx1tev", t_))
            for t_ in (tlp if C_LEVEL >= 4 else []):
                c.wait(a, "Pe", ("Clog", t_))
                c.inc(a.copy(l_all[:, t_ * 8:(t_ + 1) * 8], psC[2][:, 0:8]),
                      "Ac", 1, ("Clsb", t_))
        for t_ in ([12, 13, 14, 15] if C_LEVEL >= 3 else []):
            c.wait(a, "Pe", ("Cx1t", t_))
            ins = a.copy(
                X1T[:, :].rearrange("p (dc f) -> p dc f", dc=8)[:, :, t_ * P:(t_ + 1) * P],
                psC_T[t_ % 4][:, :].rearrange("p (dc f) -> p dc f", dc=8))
            c.inc(ins, "Ac", 1, ("Cx1tev", t_))
        for t_ in ([12, 13, 14, 15] if C_LEVEL >= 4 else []):
            c.wait(a, "Pe", ("Clog", t_))
            c.inc(a.copy(l_all[:, t_ * 8:(t_ + 1) * 8], psC[2][:, 0:8]),
                  "Ac", 1, ("Clsb", t_))
        for t_ in range(NT if C_LEVEL >= 5 else 0):
            c.wait(a, "Ve", ("Cnegm1", t_))
            ins = a.activation(gch[:, t_ * 40 + 8: t_ * 40 + 16],
                               l_all[:, t_ * 8:(t_ + 1) * 8], AF.Exp,
                               bias=gch[:, t_ * 40: t_ * 40 + 1], scale=1.0)
            c.inc(ins, "Ac", 1, ("Cge", t_))
        for t_ in range(NT if C_LEVEL >= 5 else 0):
            c.wait(a, "Pe", ("CgT", t_))
            ins = a.copy(gT[0:8, t_ * P:(t_ + 1) * P],
                         psC[2][0:8, (t_ % 2) * P:(t_ % 2) * P + P])
            c.inc(ins, "Ac", 1, ("CgTev", t_))

    def phC_ve(v, c):
        def col(t_, r):
            return stC[:, t_ * 8 + r: t_ * 8 + r + 1]
        for p in range(NPAIR):
            for hp in range(2):
                c.wait(v, "Sd", ("Cl", p, hp))
                c.inc(v.reciprocal(lrc[0:1, :], lld[0:1, :]), "Ve", 1, ("Crecip", p, hp))
                for ch in range(4):
                    j = (p * 8 + hp * 4 + ch)
                    c.wait(v, "Ac", ("Crbev", j))
                    dst = ot_ld[p][64 * hp:64 * hp + 64, ch * 512:(ch + 1) * 512]
                    ins = v.tensor_tensor(dst, dst, rbc[j % 2][64 * hp:64 * hp + 64, 0:512], ALU.mult)
                    c.inc(ins, "Ve", 1, ("Cnot", j))
        v.memset(gate[:, :], 1.0)
        for w in range(4):
            tl = [4 * w + i for i in range(4)]
            if 'add' in C_SUB or C_SUB == 'all':
                for t_ in (tl if C_LEVEL >= 2 else []):
                    c.wait(v, "Ac", ("Copev", t_, 1))
                    c.wait(v, "Sd", ("Cxre", t_))
                    ins = v.tensor_tensor(h1[t_ % 4][:, :], h1[t_ % 4][:, :], xre[t_ % 2][:, :], ALU.add)
                    c.inc(ins, "Ve", 1, ("Cadd", t_))
            if 's1' in C_SUB or C_SUB == 'all':
                for t_ in (tl if C_LEVEL >= 2 else []):
                    c.inc(v.reduce_sum(col(t_, 0), h1[t_ % 4][:, :], axis=AX.X), "Ve", 1, ("Cs1", t_))
            if 'm' in C_SUB or C_SUB == 'all':
                for t_ in (tl if C_LEVEL >= 2 else []):
                    c.inc(v.tensor_scalar(col(t_, 1), col(t_, 0), 1.0 / D, None, ALU.mult),
                          "Ve", 1, ("Cm", t_))
            if 'var' in C_SUB or C_SUB == 'all':
                for t_ in (tl if C_LEVEL >= 2 else []):
                    c.wait(v, "Ac", ("Csq", t_))
                    c.wait(v, "Ac", ("Cmm2", t_))
                    ins = v.scalar_tensor_tensor(col(t_, 2), col(t_, 5), 1.0 / D, col(t_, 6),
                                                 ALU.mult, ALU.subtract)
                    c.inc(ins, "Ve", 1, ("Cvar", t_))
            if 'bb' in C_SUB or C_SUB == 'all':
                for t_ in (tl if C_LEVEL >= 2 else []):
                    c.wait(v, "Ac", ("Crstd", t_))
                    ins = v.scalar_tensor_tensor(col(t_, 4), col(t_, 1), -1.0, col(t_, 3),
                                                 ALU.mult, ALU.mult)
                    c.inc(ins, "Ve", 1, ("Cbb", t_))
        def g8(t_, r):
            return gch[:, t_ * 40 + r: t_ * 40 + r + 8]
        def g1(t_, r):
            return gch[:, t_ * 40 + r: t_ * 40 + r + 1]
        for t_ in range(NT if C_LEVEL >= 5 else 0):
            c.wait(v, "Ac", ("Clsb", t_))
            v.reduce_max(g1(t_, 1), l_all[:, t_ * 8:(t_ + 1) * 8], axis=AX.X)
        for t_ in range(NT if C_LEVEL >= 5 else 0):
            c.inc(v.tensor_scalar(g1(t_, 0), g1(t_, 1), -1.0, None, ALU.mult),
                  "Ve", 1, ("Cnegm1", t_))
        for t_ in range(NT if C_LEVEL >= 5 else 0):
            v.tensor_scalar(g8(t_, 16), l_all[:, t_ * 8:(t_ + 1) * 8], g1(t_, 1), None, ALU.is_equal)
        for t_ in range(NT if C_LEVEL >= 5 else 0):
            v.tensor_scalar(g8(t_, 24), g8(t_, 16), NEG, None, ALU.mult)
        for t_ in range(NT if C_LEVEL >= 5 else 0):
            v.tensor_tensor(g8(t_, 24), g8(t_, 24), l_all[:, t_ * 8:(t_ + 1) * 8], ALU.add)
        for t_ in range(NT if C_LEVEL >= 5 else 0):
            v.reduce_max(g1(t_, 2), g8(t_, 24), axis=AX.X)
        for t_ in range(NT if C_LEVEL >= 5 else 0):
            v.tensor_scalar(g8(t_, 32), l_all[:, t_ * 8:(t_ + 1) * 8], g1(t_, 2), None, ALU.is_ge)
        for t_ in range(NT if C_LEVEL >= 5 else 0):
            c.wait(v, "Ac", ("Cge", t_))
            v.tensor_tensor(g8(t_, 32), g8(t_, 32), g8(t_, 8), ALU.mult)
        for t_ in range(NT if C_LEVEL >= 5 else 0):
            v.reduce_sum(g1(t_, 3), g8(t_, 32), axis=AX.X)
        for t_ in range(NT if C_LEVEL >= 5 else 0):
            v.reciprocal(g1(t_, 4), g1(t_, 3))
        for t_ in range(NT if C_LEVEL >= 5 else 0):
            ins = v.tensor_scalar(gate[:, t_ * 9: t_ * 9 + 8], g8(t_, 32), g1(t_, 4), None, ALU.mult)
            c.inc(ins, "Ve", 1, ("Cgate", t_))

    NE = 9

    def gcol(ei):
        return 8 if ei == 0 else ei - 1

    def phD_gp(g, c):
        def wfc_load(k, hb):
            ei2 = k % NE
            kk = k * 32 + hb
            if kk >= 4:
                pk, phb = (kk - 4) // 32, (kk - 4) % 32
                c.wait(g, "Ac", ("Dfcev", pk, phb, 1))
            s2 = wfc_in[ei2, :, :, hb * P:(hb + 1) * P].rearrange("dc p f -> p dc f")
            ins = g.dma_start(wfc_t[kk % 4][:, :].rearrange("p (dc f) -> p dc f", dc=8), s2)
            c.inc(ins, "Gd", 16, ("Dwfc", k, hb))

        def wpj_load(k, fo, hb):
            ei2 = k % NE
            kk = (k * 2 + fo) * 32 + hb
            if kk >= 4:
                pkk = kk - 4
                pk, pfo, phb = pkk // 64, (pkk // 32) % 2, pkk % 32
                if phb < 31:
                    c.wait(g, "Pe", ("Dpjhb", pk, pfo, phb))
                else:
                    c.wait(g, "Pe", ("Dpj", pk, pfo, 7))
            ins = g.dma_start(wpj_t[kk % 4][:, :],
                              wpj_in[ei2, hb, :, fo * 512:(fo + 1) * 512])
            c.inc(ins, "Gd", 16, ("Dwpj", k, fo, hb))

        for hb in range(4):
            wfc_load(0, hb)
        for k in range(2 * NE):
            for hb in range(4, 32):
                wfc_load(k, hb)
            for hb in range(32):
                wpj_load(k, 0, hb)
            if k + 1 < 2 * NE:
                for hb in range(4):
                    wfc_load(k + 1, hb)
            for hb in range(32):
                wpj_load(k, 1, hb)

    def phD_sync(g, c):
        for tg in range(2):
            for ei in range(NE):
                k = tg * NE + ei
                if k >= 2:
                    c.wait(g, "Pe", ("Dbfct", k - 2))
                c.inc(g.dma_start(bfc_st[k % 2][:, :], bfc_in[ei]), "Sd", 16, ("Dbfc", k))
            for tt in range(8):
                t_ = tg * 8 + tt
                if t_ >= 2:
                    c.wait(g, "Ve", ("Eadd", t_ - 2))
                c.inc(g.dma_start(x1re[t_ % 2][:, :], x1_sp[t_]), "Sd", 16, ("Ex1", t_))
            for tt in range(8):
                t_ = tg * 8 + tt
                c.wait(g, "Ac", ("Enorm", t_))
                c.inc(g.dma_start(y_out[t_], yst[t_ % 2][:, :]), "St", 16, ("Ey", t_))
        if not c.dry:
            g.wait_ge(sems["St"], c.counts["St"])
        elif c.record:
            g.wait_ge(SemName("St"), c.counts["St"])

    def phD_pe(t, c):
        for tg in range(2):
            for ei in range(NE):
                k = tg * NE + ei
                c.wait(t, "Sd", ("Dbfc", k))
                c.wait_if(t, "Ac", ("Dbfcev", k - 1))
                c.wait_if(t, "Ve", ("Dpjev", k - 1, 1, 4))
                c.inc(t.transpose(psD[4][:, 0:32], bfc_st[k % 2][0:32, 0:P], id_f[0:32, 0:32]),
                      "Pe", 1, ("Dbfct", k))
                for hb in range(32):
                    c.wait(t, "Gd", ("Dwfc", k, hb))
                    for tc in range(2):
                        bank = (hb % 2) * 2 + tc
                        c.wait_if(t, "Ac", ("Dfcev", k, hb - 2, tc))
                        if hb < 2:
                            c.wait_if(t, "Ve", ("Dpjev", k - 1, 1, bank))
                        ins = None
                        for dc in range(8):
                            ins = t.matmul(
                                psD[bank][:, :], wfc_t[(k * 32 + hb) % 4][:, dc * P:(dc + 1) * P],
                                X1T[:, dc * 2048 + tg * 1024 + tc * 512:
                                    dc * 2048 + tg * 1024 + (tc + 1) * 512],
                                start=(dc == 0), stop=(dc == 7))
                        c.inc(ins, "Pe", 1, ("Dfcmm", k, hb, tc))
                for fo in range(2):
                    biased = (ei == 0)
                    for tt in range(8):
                        bank = tt
                        if fo == 0:
                            if tt < 4:
                                c.wait(t, "Ac", ("Dfcev", k, 30 + tt // 2, tt % 2))
                            elif tt == 4:
                                c.wait(t, "Ac", ("Dbfcev", k))
                            else:
                                c.wait_if(t, "Ve", ("Dpjev", k - 1, 1, tt))
                        else:
                            c.wait(t, "Ve", ("Dpjev", k, 0, tt))
                        if biased:
                            if k == 0 and tt == 0 and fo == 0:
                                c.wait(t, "Ac", ("CgTev", NT - 1))
                                c.wait(t, "Gd", ("Cbpe",))
                                c.wait(t, "Gd", ("Cbps",))
                            t.matmul(psD[bank][:, :],
                                     gT[0:8, (tg * 8 + tt) * P:(tg * 8 + tt + 1) * P],
                                     bpall[0:8, fo * 512:(fo + 1) * 512], start=True, stop=False)
                            t.matmul(psD[bank][:, :], ones_b[0:1, 0:P],
                                     bpsh[0:1, fo * 512:(fo + 1) * 512], start=False, stop=False)
                    for hb in range(32):
                        c.wait(t, "Gd", ("Dwpj", k, fo, hb))
                        if fo == 0:
                            c.wait(t, "Ve", ("DHw", k, hb))
                        ins = None
                        for tt in range(8):
                            ins = t.matmul(psD[tt][:, :],
                                           Hb[:, hb * 1024 + tt * P: hb * 1024 + (tt + 1) * P],
                                           wpj_t[((k * 2 + fo) * 32 + hb) % 4][:, :],
                                           start=((not biased) and hb == 0), stop=(hb == 31))
                            if hb == 31:
                                c.inc(ins, "Pe", 1, ("Dpj", k, fo, tt))
                        if hb < 31:
                            c.inc(ins, "Pe", 1, ("Dpjhb", k, fo, hb))

    def phD_ac(a, c):
        for tg in range(2):
            for ei in range(NE):
                k = tg * NE + ei
                c.wait(a, "Pe", ("Dbfct", k))
                c.inc(a.copy(bfc_t[k % 2][:, 0:32], psD[4][:, 0:32]), "Ac", 1, ("Dbfcev", k))
                for hb in range(32):
                    if hb >= 1:
                        c.wait(a, "Ve", ("DHw", k, hb - 1))
                    elif k >= 1:
                        c.wait(a, "Ve", ("DHw", k - 1, 31))
                    ins = None
                    for tc in range(2):
                        bank = (hb % 2) * 2 + tc
                        c.wait(a, "Pe", ("Dfcmm", k, hb, tc))
                        ins = a.activation(xf[hb % 2][:, tc * 512:(tc + 1) * 512], psD[bank][:, :],
                                           AF.Identity, bias=bfc_t[k % 2][:, hb:hb + 1])
                        c.inc(ins, "Ac", 1, ("Dfcev", k, hb, tc))
                    a.activation(wscr[hb % 2][:, :], xf[hb % 2][:, :], AF.Square)
                    ins = a.activation(wscr[hb % 2][:, :], wscr[hb % 2][:, :], AF.Identity,
                                       bias=1.0, scale=GELU_C2)
                    c.inc(ins, "Ac", 1, ("Dw", k, hb))
                    c.wait(a, "Ve", ("Dt", k, hb))
                    ins = a.activation(sgscr[hb % 2][:, :], tscr[hb % 2][:, :], AF.Sigmoid,
                                       scale=GELU_2C)
                    c.inc(ins, "Ac", 1, ("Dsg", k, hb))
            def colE(tt, r):
                return stE[:, tt * 8 + r: tt * 8 + r + 1]
            for tt in range(8):
                t_ = tg * 8 + tt
                c.wait(a, "Ve", ("Eadd", t_))
                ins = a.activation(wscr[0][:, :], routed[:, tt * D:(tt + 1) * D], AF.Square,
                                   accum_out=colE(tt, 5))
                c.inc(ins, "Ac", 1, ("Esq", t_))
            for tt in range(8):
                t_ = tg * 8 + tt
                c.wait(a, "Ve", ("Em", t_))
                c.inc(a.activation(colE(tt, 6), colE(tt, 1), AF.Square), "Ac", 1, ("Emm2", t_))
            for tt in range(8):
                t_ = tg * 8 + tt
                c.wait(a, "Ve", ("Evar", t_))
                a.activation(colE(tt, 3), colE(tt, 2), AF.Ln, bias=epst[:, 0:1], scale=1.0)
                c.inc(a.activation(colE(tt, 3), colE(tt, 3), AF.Exp, scale=-0.5),
                      "Ac", 1, ("Erstd", t_))
            for tt in range(8):
                t_ = tg * 8 + tt
                c.wait(a, "Ve", ("Ebb", t_))
                if t_ >= 2:
                    c.wait(a, "St", ("Ey", t_ - 2))
                ins = a.activation(yst[t_ % 2][:, :], routed[:, tt * D:(tt + 1) * D], AF.Identity,
                                   bias=colE(tt, 4), scale=colE(tt, 3))
                c.inc(ins, "Ac", 1, ("Enorm", t_))

    def phD_ve(v, c):
        for tg in range(2):
            for ei in range(NE):
                k = tg * NE + ei
                for hb in range(32):
                    c.wait(v, "Ac", ("Dw", k, hb))
                    ins = v.tensor_tensor(tscr[hb % 2][:, :], xf[hb % 2][:, :],
                                          wscr[hb % 2][:, :], ALU.mult)
                    c.inc(ins, "Ve", 1, ("Dt", k, hb))
                    c.wait(v, "Ac", ("Dsg", k, hb))
                    if k >= 1:
                        if hb < 31:
                            c.wait(v, "Pe", ("Dpjhb", k - 1, 1, hb))
                        else:
                            c.wait(v, "Pe", ("Dpj", k - 1, 1, 7))
                    ins = v.tensor_tensor(Hb[:, hb * 1024:(hb + 1) * 1024], xf[hb % 2][:, :],
                                          sgscr[hb % 2][:, :], ALU.mult)
                    c.inc(ins, "Ve", 1, ("DHw", k, hb))
                for fo in range(2):
                    for tt in range(8):
                        c.wait(v, "Pe", ("Dpj", k, fo, tt))
                        dst = routed[:, tt * D + fo * 512: tt * D + (fo + 1) * 512]
                        if ei == 0:
                            if tg == 1:
                                c.wait(v, "Ac", ("Enorm", tt))
                            ins = v.tensor_copy(dst, psD[tt][:, :])
                        else:
                            gc = gate[:, (tg * 8 + tt) * 9 + gcol(ei):
                                      (tg * 8 + tt) * 9 + gcol(ei) + 1]
                            ins = v.scalar_tensor_tensor(dst, psD[tt][:, :], gc, dst,
                                                         ALU.mult, ALU.add)
                        c.inc(ins, "Ve", 1, ("Dpjev", k, fo, tt))
            def colE(tt, r):
                return stE[:, tt * 8 + r: tt * 8 + r + 1]
            for tt in range(8):
                t_ = tg * 8 + tt
                c.wait(v, "Sd", ("Ex1", t_))
                ins = v.tensor_tensor(routed[:, tt * D:(tt + 1) * D],
                                      routed[:, tt * D:(tt + 1) * D], x1re[t_ % 2][:, :], ALU.add)
                c.inc(ins, "Ve", 1, ("Eadd", t_))
            for tt in range(8):
                t_ = tg * 8 + tt
                c.inc(v.reduce_sum(colE(tt, 0), routed[:, tt * D:(tt + 1) * D], axis=AX.X),
                      "Ve", 1, ("Es1", t_))
            for tt in range(8):
                t_ = tg * 8 + tt
                c.inc(v.tensor_scalar(colE(tt, 1), colE(tt, 0), 1.0 / D, None, ALU.mult),
                      "Ve", 1, ("Em", t_))
            for tt in range(8):
                t_ = tg * 8 + tt
                c.wait(v, "Ac", ("Esq", t_))
                c.wait(v, "Ac", ("Emm2", t_))
                ins = v.scalar_tensor_tensor(colE(tt, 2), colE(tt, 5), 1.0 / D, colE(tt, 6),
                                             ALU.mult, ALU.subtract)
                c.inc(ins, "Ve", 1, ("Evar", t_))
            for tt in range(8):
                t_ = tg * 8 + tt
                c.wait(v, "Ac", ("Erstd", t_))
                ins = v.scalar_tensor_tensor(colE(tt, 4), colE(tt, 1), -1.0, colE(tt, 3),
                                             ALU.mult, ALU.mult)
                c.inc(ins, "Ve", 1, ("Ebb", t_))

    passes = [("dry", True, False)]
    if SIM is not None:
        passes.append(("sim", True, True))
    else:
        passes.append(("real", False, False))
    for pname, dry, record in passes:
        c = Ctx(sems, dry, labels, record=record)
        if dry:
            FE = FakeEng()

            def blk(*fns):
                if record:
                    for eng_name, fn in fns:
                        fn(SIM.engine(eng_name), c)
                    SIM.barrier()
                else:
                    for eng_name, fn in fns:
                        fn(FE, c)
        else:
            def blk(*fns):
                with nc.Block() as block:
                    for eng_name, fn in fns:
                        getattr(block, eng_name)(lambda engine, f=fn: f(engine, c))

        blk(("sync", phA_sync), ("gpsimd", phA_gp), ("vector", phA_ve),
            ("tensor", phA_pe), ("scalar", phA_ac))
        if stage.startswith("b"):
            npair_run = int(stage[1:])
        else:
            npair_run = NPAIR
        if stage == "a":
            def dumpA_ac(a, c2):
                for i in range(NT):
                    if i >= 2:
                        c2.wait(a, "St", ("dbgAs", i - 2))
                    ins = a.copy(x_st[i % 2][:, :], XT[:, i * 1024:(i + 1) * 1024])
                    c2.inc(ins, "Ac", 1, ("dbgA", i))
            def dumpA_sync(g, c2):
                for i in range(NT):
                    c2.wait(g, "Ac", ("dbgA", i))
                    c2.inc(g.dma_start(y_out[i], x_st[i % 2][:, :]), "St", 16, ("dbgAs", i))
                if not c2.dry:
                    g.wait_ge(sems["St"], c2.counts["St"])
            blk(("scalar", dumpA_ac), ("sync", dumpA_sync))
            continue_pairs = False
        else:
            continue_pairs = True
        for p in range(npair_run if continue_pairs else 0):
            fns = []
            if p == 0:
                fns.append(("gpsimd", lambda eng, c2: phB_gp(eng, c2, 0)))
            fns += [("tensor", lambda eng, c2, p=p: phB_pe(eng, c2, p)),
                    ("scalar", lambda eng, c2, p=p: phB_ac(eng, c2, p)),
                    ("vector", lambda eng, c2, p=p: phB_ve(eng, c2, p)),
                    ("sync", lambda eng, c2, p=p: phB_sync(eng, c2, p))]
            if p + 1 < NPAIR:
                fns.append(("gpsimd", lambda eng, c2, pn=p + 1: phB_gp(eng, c2, pn)))
            blk(*fns)
        if stage.startswith("b"):
            def dumpB_gp(g, c2):
                if B_MAXQT == 0:
                    outs = [KT[:, 0:1024], KT[:, 1024:2048], KT[:, 2048:3072], KT[:, 3072:4096],
                            Qp[:, 0:1024], Qp[:, 1024:2048],
                            U1[0][:, 0:1024], U1[0][:, 1024:2048],
                            U2[0][:, 0:1024], U2[0][:, 1024:2048],
                            A1s[:, 0:1024], A1s[:, 1024:2048],
                            Vtm[:, 0:1024], Vtm[:, 1024:2048]]
                    for i, src_ap in enumerate(outs):
                        ins = g.dma_start(y_out[i], src_ap)
                        c2.inc(ins, "St", 16, ("dbgB", i, 0))
                else:
                    for p in range(npair_run):
                        c2.wait(g, "St", ("Bot_st", p))
                        for half in range(2):
                            ins = g.dma_start(y_out[2 * p + half],
                                              ot_sp[p][:, half * 1024:(half + 1) * 1024])
                            c2.inc(ins, "St", 16, ("dbgB", p, half))
                if not c2.dry:
                    g.wait_ge(sems["St"], c2.counts["St"])
            blk(("gpsimd", dumpB_gp))
        if stage in ("attn", "full"):
            blk(("sync", phC_sync), ("gpsimd", phC_gp), ("tensor", phC_pe),
                ("scalar", phC_ac), ("vector", phC_ve))
        if stage == "full":
            blk(("gpsimd", phD_gp), ("sync", phD_sync), ("tensor", phD_pe),
                ("scalar", phD_ac), ("vector", phD_ve))

    st_all.close()
    return nc



def _tables():
    j = np.arange(128, dtype=np.float64)
    w = 10000.0 ** (-(2.0 * j) / 256.0)
    pp = np.arange(SEQ, dtype=np.float64)
    ang = pp[None, :] * w[:, None]
    return np.sin(ang).astype(np.float32), np.cos(ang).astype(np.float32)


def _masks(h):
    r = np.arange(P)[:, None]
    cidx = np.arange(512)[None, :]
    dm = np.zeros((4, P, 512), np.float32)
    om = np.zeros((4, P, 512), np.float32)
    for v in range(4):
        dm[v] = np.where(cidx <= 128 * v + r, 0.0, NEG)
        if h == 0:
            om[v] = np.where(cidx < 128 * v, 0.0, NEG)
        else:
            om[v] = np.where(cidx < 128 * (v + 1), 0.0, NEG)
    return dm, om


_NC_CACHE = {}


def kernel(L0_attn_input, L0_attn_qkv, L0_attn_q_bias, L0_attn_positional,
           L0_attn_out_kernel, L0_attn_out_bias, L0_ffn_Wfc_s, L0_ffn_bfc_s,
           L0_ffn_Wproj_s, L0_ffn_bproj_s, L0_ffn_Wr, L0_ffn_Wfc, L0_ffn_bfc,
           L0_ffn_Wproj, L0_ffn_bproj):
    x = np.asarray(L0_attn_input, np.float32)
    qkv = np.ascontiguousarray(np.asarray(L0_attn_qkv, np.float32).reshape(8, P, 3, 16, 64))
    qb = np.ascontiguousarray(np.asarray(L0_attn_q_bias, np.float32))
    pos = np.ascontiguousarray(np.asarray(L0_attn_positional, np.float32).reshape(P, 2048))
    wo = np.ascontiguousarray(np.asarray(L0_attn_out_kernel, np.float32).reshape(8, P, D))
    ob = np.ascontiguousarray(np.asarray(L0_attn_out_bias, np.float32).reshape(1, D))
    wr = np.ascontiguousarray(np.asarray(L0_ffn_Wr, np.float32).reshape(8, P, 8))
    wfc = np.ascontiguousarray(np.concatenate(
        [np.asarray(L0_ffn_Wfc_s, np.float32).reshape(1, 1024, 4096),
         np.asarray(L0_ffn_Wfc, np.float32)], 0).reshape(9, 8, P, 4096))
    wpj = np.ascontiguousarray(np.concatenate(
        [np.asarray(L0_ffn_Wproj_s, np.float32).reshape(1, 4096, 1024),
         np.asarray(L0_ffn_Wproj, np.float32)], 0).reshape(9, 32, P, D))
    bfc = np.ascontiguousarray(np.concatenate(
        [np.asarray(L0_ffn_bfc_s, np.float32).reshape(1, 4096),
         np.asarray(L0_ffn_bfc, np.float32)], 0).reshape(9, 32, P))
    bpe = np.ascontiguousarray(np.asarray(L0_ffn_bproj, np.float32))
    bps = np.ascontiguousarray(np.asarray(L0_ffn_bproj_s, np.float32).reshape(1, D))
    sqt, cqt = _tables()
    ident = np.eye(P, dtype=np.float32)

    if STAGE not in _NC_CACHE:
        _NC_CACHE[STAGE] = build_nc(STAGE)
    nc = _NC_CACHE[STAGE]

    in_maps = []
    metas = []
    for core in range(8):
        b, h = core // 2, core % 2
        perm_tiles = [2 * t + h for t in range(NT)] + [2 * t + (1 - h) for t in range(NT)]
        tok_perm = np.concatenate([np.arange(128 * g, 128 * g + 128) for g in perm_tiles])
        xb_ = np.ascontiguousarray(x[b][tok_perm].reshape(32, P, D))
        dmm, omm = _masks(h)
        in_maps.append(dict(
            x_in=xb_, qkv_in=qkv, qb_in=qb, pos_in=pos, wo_in=wo, ob_in=ob,
            wr_in=wr, wfc_in=wfc, wpj_in=wpj, bfc_in=bfc, bpe_in=bpe, bps_in=bps,
            sq_in=np.ascontiguousarray(sqt[:, tok_perm]),
            cq_in=np.ascontiguousarray(cqt[:, tok_perm]),
            dm_in=dmm, om_in=omm, id_in=ident,
        ))
        metas.append((b, h))

    res = run_bass_kernel_spmd(nc, in_maps, core_ids=list(range(8)))

    out = np.empty((4, SEQ, D), np.float32)
    for core in range(8):
        b, h = metas[core]
        y = np.asarray(res.results[core]["y_out"], np.float32)
        for t in range(NT):
            g = 2 * t + h
            out[b, 128 * g:128 * g + 128, :] = y[t]
    return out


if __name__ == "__main__":
    nc = build_nc(STAGE)
    print("built instructions:", nc.n_instructions())
